# version 14
# baseline (speedup 1.0000x reference)
import numpy as np
import ml_dtypes
from contextlib import ExitStack
import concourse.bass as bass
import concourse.mybir as mybir

F32 = mybir.dt.float32
BF16 = mybir.dt.bfloat16
AF = mybir.ActivationFunctionType
ALU = mybir.AluOpType
AX = mybir.AxisListType
BF = ml_dtypes.bfloat16

NT = 2048
HALO = 640
NTH = NT + HALO
DM = 1024

ENGS = ("TE", "ACT", "DVE", "POOL", "SYNC")
DROP_TE_SELF = False
DMAQ = {"SYNC": "QSP", "ACT": "QACT", "POOL": "QPOOL"}


class Sched:
    def __init__(self):
        self.items = {e: [] for e in ENGS}
        self.c = {"TE": 0, "ACT": 0, "DVE": 0, "POOL": 0,
                  "QSP": 0, "QACT": 0, "QPOOL": 0}
        self.lw = {}
        self.rd = {}

    def _tok_of(self, eng, q, n):
        q = q or eng
        self.c[q] += n
        return q, self.c[q]

    def op(self, eng, fn, reads=(), writes=(), waits=(), q=None, n=1):
        wl = {}

        def addw(tok):
            if tok is None:
                return
            qq, vv = tok
            if vv > 0 and wl.get(qq, 0) < vv:
                wl[qq] = vv

        for t in waits:
            addw(t)
        for k in reads:
            addw(self.lw.get(k))
        for k in writes:
            addw(self.lw.get(k))
            for qq, vv in self.rd.get(k, {}).items():
                addw((qq, vv))
        q = q or eng
        if q.startswith("Q"):
            addw((q, self.c[q]))
        self.c[q] += n
        tok = (q, self.c[q])
        self.items[eng].append((fn, sorted(wl.items()), q, n))
        for k in reads:
            d = self.rd.setdefault(k, {})
            if d.get(tok[0], 0) < tok[1]:
                d[tok[0]] = tok[1]
        for k in writes:
            self.lw[k] = tok
            self.rd[k] = {}
        return tok

    def dma(self, eng, out, in_, reads=(), writes=(), waits=()):
        return self.op(eng, lambda e, o=out, i=in_: e.dma_start(out=o, in_=i),
                       reads, writes, waits, q=DMAQ[eng], n=16)

    def mmg(self, bank, fns, reads=(), waits=()):
        tok = self.op("TE", fns[0], reads=reads, writes=[bank], waits=waits)
        for fn in fns[1:]:
            tok = self.op("TE", fn)
        self.lw[bank] = tok
        for k in reads:
            d = self.rd.setdefault(k, {})
            if d.get(tok[0], 0) < tok[1]:
                d[tok[0]] = tok[1]
        return tok

    def now(self, *qs):
        return [(q, self.c[q]) for q in qs]

    def barrier_tokens(self):
        return self.now("TE", "ACT", "DVE", "POOL", "QSP", "QACT", "QPOOL")

    def barrier(self):
        toks = [t for t in self.barrier_tokens() if t[1] > 0]
        for e in ENGS:
            self.items[e].append((None, toks, None, 0))

    def replay(self, blk, sems):
        def mk(eng_items, ename):
            def body(eng):
                wm = {}
                for fn, waits, q, n in eng_items:
                    for qq, vv in waits:
                        if ename == "TE" and qq == "TE" and DROP_TE_SELF:
                            continue
                        if vv > wm.get(qq, 0):
                            eng.wait_ge(sems[qq], vv)
                            wm[qq] = vv
                    if fn is not None:
                        fn(eng).then_inc(sems[q], n)
            return body
        blk.tensor(mk(self.items["TE"], "TE"))
        blk.scalar(mk(self.items["ACT"], "ACT"))
        blk.vector(mk(self.items["DVE"], "DVE"))
        blk.gpsimd(mk(self.items["POOL"], "POOL"))
        blk.sync(mk(self.items["SYNC"], "SYNC"))


def make_sems(nc, ctx):
    names = ("TE", "ACT", "DVE", "POOL", "QSP", "QACT", "QPOOL")
    return {q: ctx.enter_context(nc.semaphore("s_" + q.lower())) for q in names}


def _bf(x):
    return np.ascontiguousarray(np.asarray(x), dtype=BF)


def _f32(x):
    return np.ascontiguousarray(np.asarray(x), dtype=np.float32)


def _kc(x, p=128):
    K, N = x.shape
    return np.ascontiguousarray(x.reshape(K // p, p, N).transpose(1, 0, 2).reshape(p, -1))


def _ctile_cols(v, p=128):
    return np.ascontiguousarray(np.asarray(v, dtype=np.float32).reshape(-1, p).T)


def rotary_tables_half(t0, n):
    D = 64
    inv_freq = 1.0 / 10000.0 ** (np.arange(0, D, 2, dtype=np.float32) / D)
    pos = np.arange(t0, t0 + n, dtype=np.float32)
    fr = inv_freq[np.arange(D) % 32][:, None] * pos[None, :]
    return (np.concatenate([np.cos(fr)] * 2, 0).astype(np.float32),
            np.concatenate([np.sin(fr)] * 2, 0).astype(np.float32))


def rot_lhsT():
    R = np.zeros((128, 128), dtype=np.float32)
    for base in (0, 64):
        for d in range(32):
            R[base + d, base + d + 32] = -1.0
            R[base + d + 32, base + d] = 1.0
    return np.ascontiguousarray(R.T)


def l0_masks(half):
    out = np.zeros((128, 2 * 8 * 512), dtype=np.float32)
    NEG = -30000.0
    r = np.arange(128)[:, None]
    j = np.arange(512)[None, :]
    for m in range(8):
        d = 512 - 128 * m + j - r
        std = np.where((d >= 0) & (d <= 511), 0.0, NEG)
        out[:, m * 512:(m + 1) * 512] = std
        dead = np.full((128, 512), NEG, dtype=np.float32)
        first = dead if (half == 0 and m < 4) else std
        out[:, (8 + m) * 512:(9 + m) * 512] = first
    return _bf(out)


def hostA(inp, core):
    b, h = core // 2, core % 2
    x = _f32(inp["L0_attn_x"])
    own = x[b, h * 2048:(h + 1) * 2048]
    halo = (np.zeros((HALO, DM), np.float32) if h == 0
            else x[b, 2048 - HALO:2048])
    xh = np.concatenate([halo, own], 0)
    cos, sin = rotary_tables_half(h * 2048 - HALO, NTH)
    if h == 0:
        c2_, s2_ = rotary_tables_half(0, NTH)
        cos[:, :HALO] = c2_[:, :HALO]
        sin[:, :HALO] = s2_[:, :HALO]
    cdiag = {}
    for p in "qkv":
        w = _f32(inp[f"L0_attn_c{p}w"])
        d = np.zeros((128, 8 * 4 * 128), np.float32)
        for cch in range(8):
            for j in range(4):
                blk = np.zeros((128, 128), np.float32)
                np.fill_diagonal(blk, w[cch * 128:(cch + 1) * 128, j])
                d[:, (cch * 4 + j) * 128:(cch * 4 + j + 1) * 128] = blk
        cdiag[p] = d
    grp2 = np.zeros((128, 2), np.float32)
    grp2[:64, 0] = 1.0
    grp2[64:, 1] = 1.0
    return {
        "xh": _f32(xh),
        "wqT": _kc(_f32(inp["L0_attn_Wq"]).T),
        "wkT": _kc(_f32(inp["L0_attn_Wk"]).T),
        "wvT": _kc(_f32(inp["L0_attn_Wv"]).T),
        "woT": _kc(_f32(inp["L0_attn_Wo"]).T),
        "bq": _ctile_cols(inp["L0_attn_bq"]),
        "bk": _ctile_cols(inp["L0_attn_bk"]),
        "bv": _ctile_cols(inp["L0_attn_bv"]),
        "cbq": _ctile_cols(inp["L0_attn_cqb"]),
        "cbk": _ctile_cols(inp["L0_attn_ckb"]),
        "cbv": _ctile_cols(inp["L0_attn_cvb"]),
        "cdq": cdiag["q"], "cdk": cdiag["k"], "cdv": cdiag["v"],
        "cosT": cos, "sinT": sin, "rotT": rot_lhsT(),
        "grp2": grp2, "bc2": np.ascontiguousarray(grp2.T),
        "ones64": np.ones((128, 64), np.float32),
        "ident": np.eye(128, dtype=np.float32),
        "masks": l0_masks(h),
        "bo_bc": np.broadcast_to(_f32(inp["L0_attn_bo"]), (128, 1024)).copy(),
        "wr": _kc(_f32(inp["L0_ffn_Wr"])),
        "br_bc": np.broadcast_to(_f32(inp["L0_ffn_br"]), (128, 8)).copy(),
        "consts": np.tile(np.array(
            [0.125, 1e-12, 1e-5, 1.0 / 1024, 0.0, 1.0, -1.0, 0.5], np.float32),
            (128, 1)),
    }


def emit_ln(S, src_ap, dst_ap, mst, sq_scratch, c3, c2, rk_src, wk_dst,
            rk_extra=(), tag=0):
    mk, sk = ("mstk", tag), ("sqk", tag)
    rk_src = list(rk_src) if isinstance(rk_src, list) else [rk_src]
    S.op("DVE", lambda e: e.reduce_sum(mst[:, 0:1], src_ap, axis=AX.X),
         reads=rk_src, writes=[mk])
    S.op("ACT", lambda e: e.activation(sq_scratch, src_ap, AF.Square),
         reads=rk_src, writes=[sk])
    S.op("DVE", lambda e: e.reduce_sum(mst[:, 1:2], sq_scratch, axis=AX.X),
         reads=[sk], writes=[mk])
    S.op("DVE", lambda e: e.tensor_scalar(mst[:, 2:3], mst[:, 0:1], c3, None, ALU.mult),
         reads=[mk], writes=[mk])
    S.op("DVE", lambda e: e.tensor_scalar(mst[:, 3:4], mst[:, 1:2], c3, None, ALU.mult),
         reads=[mk], writes=[mk])
    S.op("DVE", lambda e: e.tensor_mul(mst[:, 4:5], mst[:, 2:3], mst[:, 2:3]),
         reads=[mk], writes=[mk])
    S.op("DVE", lambda e: e.tensor_sub(mst[:, 5:6], mst[:, 3:4], mst[:, 4:5]),
         reads=[mk], writes=[mk])
    S.op("ACT", lambda e: e.activation(mst[:, 6:7], mst[:, 5:6], AF.Sqrt, bias=c2),
         reads=[mk], writes=[mk])
    S.op("DVE", lambda e: e.reciprocal(mst[:, 6:7], mst[:, 6:7]),
         reads=[mk], writes=[mk])
    S.op("DVE", lambda e: e.tensor_mul(mst[:, 7:8], mst[:, 2:3], mst[:, 6:7]),
         reads=[mk], writes=[mk])
    S.op("DVE", lambda e: e.tensor_scalar(mst[:, 7:8], mst[:, 7:8], -1.0, None, ALU.mult),
         reads=[mk], writes=[mk])
    return S.op("ACT", lambda e: e.activation(dst_ap, src_ap, AF.Identity,
                                              bias=mst[:, 7:8], scale=mst[:, 6:7]),
                reads=[mk] + rk_src + list(rk_extra), writes=[wk_dst])


CH = [512] * 5 + [128]
CH0 = [0, 512, 1024, 1536, 2048, 2560]


def build_A(fuse=None):
    nc = bass.Bass("TRN2", target_bir_lowering=False) if fuse is None else fuse["nc"]
    pref = "" if fuse is None else "a_"
    g = lambda n, s, dt=F32: nc.dram_tensor(n, s, dt, kind="ExternalInput").ap()

    xh = g("xh", [NTH, DM])
    wT = {p: g(f"w{p}T", [128, 8 * 1024]) for p in "qkv"}
    woT = g("woT", [128, 8 * 1024])
    bias_in = {p: g(f"b{p}", [128, 8]) for p in "qkv"}
    cbias_in = {p: g(f"cb{p}", [128, 8]) for p in "qkv"}
    cdiag_in = {p: g(f"cd{p}", [128, 8 * 4 * 128]) for p in "qkv"}
    cosT_in = g("cosT", [128, NTH])
    sinT_in = g("sinT", [128, NTH])
    rotT_in = g("rotT", [128, 128])
    grp2_in = g("grp2", [128, 2])
    bc2_in = g("bc2", [2, 128])
    ones64_in = g("ones64", [128, 64])
    ident_in = g("ident", [128, 128])
    masks_in = g("masks", [128, 8192], BF16)
    bo_in = g("bo_bc", [128, 1024])
    wr_in = g("wr", [128, 64])
    br_in = g("br_bc", [128, 8])
    consts_in = g("consts", [128, 8])

    if fuse is None:
        x1_out = nc.dram_tensor("x1", [NT, DM], F32, kind="ExternalOutput").ap()
        probs_out = nc.dram_tensor("probs", [NT, 8], F32, kind="ExternalOutput").ap()
    else:
        x1_out = fuse["x1_hbm"]
        probs_out = None

    qhbm = nc.dram_tensor("qhbm", [128, 8 * NTH], BF16, kind="Internal").ap()
    khbm = nc.dram_tensor("khbm", [128, 8 * NTH], BF16, kind="Internal").ap()
    vhbm = nc.dram_tensor("vhbm", [128, 21 * 1024], BF16, kind="Internal").ap()

    with ExitStack() as ctx:
        sb = lambda n, s, dt: ctx.enter_context(nc.sbuf_tensor(pref + n, s, dt))
        psm = lambda n, s, dt=F32: ctx.enter_context(nc.psum_tensor(n, s, dt))

        bias_sb = {p: sb(f"b{p}sb", [128, 8], F32) for p in "qkv"}
        cbias_sb = {p: sb(f"cb{p}sb", [128, 8], F32) for p in "qkv"}
        bo_sb = sb("bosb", [128, 1024], F32)
        br_sb = sb("brsb", [128, 8], F32)
        cst = sb("cst", [128, 8], F32)
        id_bf = sb("idbf", [128, 128], BF16)
        rot_bf = sb("rotbf", [128, 128], BF16)
        grp_bf = sb("grpbf", [128, 2], BF16)
        bc2_bf = sb("bc2bf", [2, 128], BF16)
        on64_bf = sb("on64bf", [128, 64], BF16)
        wr_bf = sb("wrbf", [128, 64], BF16)
        mst = sb("mst", [128, 16], F32)
        if fuse is not None:
            fz = {
                "w_probs": sb("fzwp", [128, 128], F32),
                "probsT_own_sb": sb("fzpt", [8, 2048], F32),
                "splf": sb("fzsplf", [128, 16], F32),
                "splb": sb("fzsplb", [128, 24], BF16),
                "ptf": sb("fzptf", [8, 128], F32),
            }

        if fuse is None:
            pb = [psm(f"pb{i}", [128, 512]) for i in range(6)]
            pt_bf = [psm(f"ptb{i}", [128, 512], BF16) for i in range(2)]
            S = Sched()
            sems = make_sems(nc, ctx)
        else:
            pb, pt_bf = fuse["pb"], fuse["pt_bf"]
            S, sems = fuse["S"], fuse["sems"]
        PB = [f"pb{i}" for i in range(6)]
        PT = ["pt0", "pt1"]

        c1 = cst[:, 1:2]
        c2 = cst[:, 2:3]
        c3 = cst[:, 3:4]

        with ExitStack() as p1:
            sb1 = lambda n, s, dt: p1.enter_context(nc.sbuf_tensor(pref + n, s, dt))
            lnxT = sb1("lnxT", [128, 8 * NTH], BF16)
            stage = sb1("stage", [128, 2048], F32)
            w_bf = sb1("w_bf", [128, 8 * 1024], BF16)
            cd_bf = sb1("cd_bf", [128, 4 * 128 * 8], BF16)
            cos_bf = sb1("cosbf", [128, NTH], BF16)
            sin_bf = sb1("sinbf", [128, NTH], BF16)
            xt = sb1("xt", [128, 2048], F32)
            sqs = sb1("sqs", [128, 2048], F32)
            lnx = sb1("lnx", [128, 2048], BF16)
            plin = {b: sb1(f"plin{b}", [128, 8 * 512], BF16) for b in range(2)}
            pc = sb1("pc", [128, 8 * 512], BF16)
            qch = sb1("qch", [128, 8 * 512], BF16)
            vch = sb1("vch", [128, 4 * 1024], BF16)
            tmp1 = sb1("tmp1", [128, 2 * 512], F32)
            tmp2 = sb1("tmp2", [128, 2 * 512], F32)
            tmp3 = sb1("tmp3", [128, 2 * 512], BF16)
            sqb = sb1("sqb", [128, 2 * 512], BF16)
            rs_f = sb1("rs_f", [2, 2 * 512], F32)
            rs_b = sb1("rs_b", [2, 2 * 512], BF16)
            rb_bf = sb1("rb_bf", [128, 2 * 512], BF16)

            for p in "qkv":
                S.dma("SYNC", bias_sb[p][:], bias_in[p], writes=[f"b{p}"])
                S.dma("SYNC", cbias_sb[p][:], cbias_in[p], writes=[f"cb{p}"])
            S.dma("SYNC", bo_sb[:], bo_in, writes=["bo"])
            S.dma("SYNC", br_sb[:], br_in, writes=["br"])
            S.dma("SYNC", cst[:], consts_in, writes=["cst"])

            def conv_in(src_ap, dst_ap, width, key):
                for off in range(0, width, 2048):
                    wdt = min(2048, width - off)
                    sl = stage[:, 0:wdt]
                    S.dma("SYNC", sl, src_ap[:, off:off + wdt], writes=["stage"])
                    S.op("DVE", lambda e, d=dst_ap[:, off:off + wdt], s=sl:
                         e.tensor_copy(d, s), reads=["stage"], writes=[key])

            conv_in(cosT_in, cos_bf[:], NTH, "cos")
            conv_in(sinT_in, sin_bf[:], NTH, "sin")
            conv_in(rotT_in, rot_bf[:], 128, "rot")
            conv_in(ident_in, id_bf[:], 128, "id")
            conv_in(ones64_in, on64_bf[:], 64, "on64")
            conv_in(wr_in, wr_bf[:], 64, "wrk")
            S.dma("SYNC", stage[0:128, 0:2], grp2_in, writes=["stage"])
            S.op("DVE", lambda e: e.tensor_copy(grp_bf[:], stage[0:128, 0:2]),
                 reads=["stage"], writes=["grp"])
            S.dma("SYNC", stage[0:2, 0:128], bc2_in, writes=["stage"])
            S.op("DVE", lambda e: e.tensor_copy(bc2_bf[:], stage[0:2, 0:128]),
                 reads=["stage"], writes=["bc2"])

            for tt in range(21):
                par = tt % 2
                zz = par * 1024
                S.dma("SYNC", xt[:, zz:zz + 1024], xh[tt * 128:(tt + 1) * 128, :],
                      writes=[("xt", par)])
                emit_ln(S, xt[:, zz:zz + 1024], lnx[:, zz:zz + 1024],
                        mst[:, par * 8:(par + 1) * 8], sqs[:, zz:zz + 1024],
                        c3, c2, ("xt", par), ("lnx", par),
                        rk_extra=["cst"], tag=par)
                for half in range(2):
                    pti = half
                    fns = []
                    for q4 in range(4):
                        dch = half * 4 + q4
                        fns.append(lambda e, o=pt_bf[pti][:, q4 * 128:(q4 + 1) * 128],
                                   i=lnx[:, zz + dch * 128: zz + (dch + 1) * 128]:
                                   e.transpose(o, i, id_bf[:]))
                    tok = S.op("TE", fns[0], reads=[("lnx", par), "id"], writes=[PT[pti]])
                    for f in fns[1:]:
                        tok = S.op("TE", f)
                    S.lw[PT[pti]] = tok

                    def ev(e, pt=pt_bf[pti], half=half, tt=tt):
                        src = pt[:, 0:512].rearrange("p (a b) -> p a b", a=4)
                        d = lnxT[:].rearrange("p (a b) -> p a b", a=8, b=NTH)[
                            :, half * 4: half * 4 + 4, tt * 128: tt * 128 + 128]
                        return e.tensor_copy(d, src)
                    S.op("DVE", ev, reads=[PT[pti]], writes=[("lnxT", tt)])
            lnxT_keys = [("lnxT", tt) for tt in range(21)]

            pbi = [0]

            def next_pb():
                i = pbi[0]
                pbi[0] = (i + 1) % 4
                return i

            for p in "qkv":
                conv_in(wT[p], w_bf[:], 8192, "w_bf")
                conv_in(cdiag_in[p], cd_bf[:], 4096, "cd_bf")
                for ci, (n, t0) in enumerate(zip(CH, CH0)):
                    ntile = n // 128
                    buf = ci % 2
                    for cch in range(8):
                        bi = next_pb()
                        fns = [lambda e, o=pb[bi][:, 0:n],
                               l=w_bf[:, kch * 1024 + cch * 128: kch * 1024 + (cch + 1) * 128],
                               r=lnxT[:, kch * NTH + t0: kch * NTH + t0 + n],
                               st=(kch == 0), sp=(kch == 7):
                               e.matmul(o, l, r, start=st, stop=sp) for kch in range(8)]
                        S.mmg(PB[bi], fns, reads=lnxT_keys + ["w_bf"])
                        S.op("ACT", lambda e, o=plin[buf][:, cch * 512: cch * 512 + n],
                             i=pb[bi][:, 0:n], b=bias_sb[p][:, cch:cch + 1]:
                             e.activation(o, i, AF.Identity, bias=b),
                             reads=[PB[bi], f"b{p}"], writes=[("plin", buf, cch)])
                    prev_buf = 1 - buf
                    prevlen = CH[ci - 1] if ci > 0 else 0
                    for cch in range(8):
                        bi = next_pb()
                        specs = []
                        for j in [3, 2, 1, 0]:
                            sh = 3 - j
                            l_ap = cd_bf[:, (cch * 4 + j) * 128:(cch * 4 + j + 1) * 128]
                            if n - sh > 0:
                                specs.append((pb[bi][:, sh:n], l_ap,
                                              plin[buf][:, cch * 512: cch * 512 + n - sh]))
                            if sh > 0 and ci > 0:
                                specs.append((pb[bi][:, 0:sh], l_ap,
                                              plin[prev_buf][:, cch * 512 + prevlen - sh:
                                                             cch * 512 + prevlen]))
                        fns = [lambda e, o=o_, l=l_, r=r_, st=(i == 0),
                               sp=(i == len(specs) - 1):
                               e.matmul(o, l, r, start=st, stop=sp)
                               for i, (o_, l_, r_) in enumerate(specs)]
                        rd = [("plin", buf, cch), "cd_bf"]
                        if ci > 0:
                            rd.append(("plin", prev_buf, cch))
                        S.mmg(PB[bi], fns, reads=rd)
                        S.op("ACT", lambda e, o=pc[:, cch * 512: cch * 512 + n],
                             i=pb[bi][:, 0:n], b=cbias_sb[p][:, cch:cch + 1]:
                             e.activation(o, i, AF.Identity, bias=b),
                             reads=[PB[bi], f"cb{p}"], writes=[("pc", cch)])
                    if p in "qk":
                        for cch in range(8):
                            par = cch % 2
                            z = par * 512
                            pcsl = pc[:, cch * 512: cch * 512 + n]
                            pck = ("pc", cch)
                            bi = next_pb()
                            S.mmg(PB[bi], [lambda e, o=pb[bi][:, 0:n], r=pcsl:
                                           e.matmul(o, rot_bf[:], r, start=True, stop=True)],
                                  reads=[pck, "rot"])
                            S.op("DVE", lambda e, o=tmp1[:, z:z + n], a=pcsl,
                                 b=cos_bf[:, t0:t0 + n]: e.tensor_mul(o, a, b),
                                 reads=[pck, "cos"], writes=[("tmp1", par)])
                            S.op("ACT", lambda e, o=tmp3[:, z:z + n], i=pb[bi][:, 0:n]:
                                 e.activation(o, i, AF.Copy), reads=[PB[bi]],
                                 writes=[("tmp3", par)])
                            S.op("DVE", lambda e, o=tmp2[:, z:z + n], a=tmp3[:, z:z + n],
                                 b=sin_bf[:, t0:t0 + n]: e.tensor_mul(o, a, b),
                                 reads=[("tmp3", par), "sin"], writes=[("tmp2", par)])
                            S.op("DVE", lambda e, o=pcsl, a=tmp1[:, z:z + n],
                                 b=tmp2[:, z:z + n]:
                                 e.tensor_add(o, a, b),
                                 reads=[("tmp1", par), ("tmp2", par)], writes=[pck])
                            S.op("DVE", lambda e, o=sqb[:, z:z + n], a=pcsl, b=pcsl:
                                 e.tensor_mul(o, a, b), reads=[pck], writes=[("sqb", par)])
                            bi2 = next_pb()
                            S.mmg(PB[bi2], [lambda e, o=pb[bi2][0:2, 0:n], r=sqb[:, z:z + n]:
                                            e.matmul(o, grp_bf[:], r, start=True, stop=True)],
                                  reads=[("sqb", par), "grp"])
                            S.op("ACT", lambda e, o=rs_f[:, z:z + n], i=pb[bi2][0:2, 0:n]:
                                 e.activation(o, i, AF.Sqrt, bias=c1[0:2, :]),
                                 reads=[PB[bi2], "cst"], writes=[("rs_f", par)])
                            S.op("DVE", lambda e, o=rs_f[:, z:z + n]: e.reciprocal(o, o),
                                 reads=[("rs_f", par)], writes=[("rs_f", par)])
                            S.op("DVE", lambda e, o=rs_b[:, z:z + n], i=rs_f[:, z:z + n]:
                                 e.tensor_copy(o, i),
                                 reads=[("rs_f", par)], writes=[("rs_b", par)])
                            bi3 = next_pb()
                            S.mmg(PB[bi3], [lambda e, o=pb[bi3][:, 0:n], r=rs_b[0:2, z:z + n]:
                                            e.matmul(o, bc2_bf[:], r, start=True, stop=True)],
                                  reads=[("rs_b", par), "bc2"])
                            S.op("ACT", lambda e, o=rb_bf[:, z:z + n], i=pb[bi3][:, 0:n]:
                                 e.activation(o, i, AF.Copy), reads=[PB[bi3]],
                                 writes=[("rb_bf", par)])
                            S.op("DVE", lambda e, o=qch[:, cch * 512: cch * 512 + n],
                                 a=pcsl, b=rb_bf[:, z:z + n]: e.tensor_mul(o, a, b),
                                 reads=[pck, ("rb_bf", par)], writes=[("qch", cch)])
                        dst_hbm = qhbm if p == "q" else khbm
                        for cch in range(8):
                            S.dma("SYNC", dst_hbm[:, cch * NTH + t0: cch * NTH + t0 + n],
                                  qch[:, cch * 512: cch * 512 + n],
                                  reads=[("qch", cch)],
                                  writes=[(p + "hbm", ci, cch)])
                    else:
                        for cch in range(8):
                            for k in range(ntile):
                                pti = k % 2
                                S.op("TE", lambda e, o=pt_bf[pti][:, 0:128],
                                     i=pc[:, cch * 512 + k * 128: cch * 512 + (k + 1) * 128]:
                                     e.transpose(o, i, id_bf[:]),
                                     reads=[("pc", cch), "id"], writes=[PT[pti]])
                                S.op("DVE", lambda e, o=vch[:, k * 1024 + cch * 128:
                                                           k * 1024 + (cch + 1) * 128],
                                     i=pt_bf[pti][:, 0:128]: e.tensor_copy(o, i),
                                     reads=[PT[pti]], writes=[("vch", k, cch)])
                        S.dma("SYNC", vhbm[:, (t0 // 128) * 1024: (t0 // 128 + ntile) * 1024],
                              vch[:, 0: ntile * 1024],
                              reads=[("vch", k_, c_) for k_ in range(ntile) for c_ in range(8)],
                              writes=[("vhbm", ci)])

            S.barrier()

        with ExitStack() as p2:
            sb2 = lambda n, s, dt: p2.enter_context(nc.sbuf_tensor(pref + n, s, dt))
            mask_sb = sb2("masksb", [128, 8192], BF16)
            wo_bf = sb2("wobf", [128, 8 * 1024], BF16)
            stage2 = sb2("stage2", [128, 2048], F32)
            qT_sb = sb2("qTsb", [128, 8 * 512], BF16)
            kT_sb = sb2("kTsb", [128, 8 * 1024], BF16)
            vt_sb = sb2("vtsb", [128, 8 * 1024], BF16)
            outT_ch = sb2("outTch", [128, 8 * 512], BF16)
            pT = [sb2(f"pT{i}", [128, 512], BF16) for i in range(2)]
            dnr = sb2("dnr", [128, 512], F32)
            y_sb = sb2("y_sb", [128, 1024], F32)
            lt_sb = sb2("lt_sb", [128, 8 * 128], BF16)
            lg_sb = sb2("lg_sb", [128, 8], F32)
            m_sb = sb2("m_sb", [128, 2], F32)
            xt2 = sb2("xt2", [128, 1024], F32)
            lnx2 = sb2("lnx2", [128, 1024], BF16)
            sqs2 = sb2("sqs2", [128, 1024], F32)

            S.dma("SYNC", mask_sb[:], masks_in, writes=["mask"])
            for off in range(0, 8192, 2048):
                sl = stage2[:, 0:2048]
                S.dma("SYNC", sl, woT[:, off:off + 2048], writes=["stage2"])
                S.op("DVE", lambda e, d=wo_bf[:, off:off + 2048], s=sl:
                     e.tensor_copy(d, s), reads=["stage2"], writes=["wo"])

            spill_keys = ([("qhbm", ci, cc) for ci in range(6) for cc in range(8)] +
                          [("khbm", ci, cc) for ci in range(6) for cc in range(8)] +
                          [("vhbm", ci) for ci in range(6)])
            for qc in range(4):
                qoff = HALO + 512 * qc
                koff0 = qoff - 512
                mset = 1 if qc == 0 else 0
                for cch in range(8):
                    S.dma("SYNC", qT_sb[:, cch * 512:(cch + 1) * 512],
                          qhbm[:, cch * NTH + qoff: cch * NTH + qoff + 512],
                          reads=spill_keys, writes=["qT_sb"])
                    S.dma("SYNC", kT_sb[:, cch * 1024:(cch + 1) * 1024],
                          khbm[:, cch * NTH + koff0: cch * NTH + koff0 + 1024],
                          reads=spill_keys, writes=["kT_sb"])
                S.dma("SYNC", vt_sb[:], vhbm[:, (koff0 // 128) * 1024:
                                              (koff0 // 128 + 8) * 1024],
                      reads=spill_keys, writes=["vt_sb"])

                for pair in range(8):
                    last_mm = {}
                    for m in range(8):
                        ktt = m
                        for hh in range(2):
                            h = pair * 2 + hh
                            cch = h // 2
                            rb = 64 * (h % 2)
                            sbk = hh
                            ob = 2 + hh
                            db_ = 4 + hh
                            fns = [
                                lambda e, o=pb[sbk][:, 0:512],
                                l=kT_sb[rb:rb + 64, cch * 1024 + m * 128: cch * 1024 + (m + 1) * 128],
                                r=qT_sb[rb:rb + 64, cch * 512: (cch + 1) * 512]:
                                e.matmul(o, l, r, start=True, stop=False),
                                lambda e, o=pb[sbk][:, 0:512],
                                r=mask_sb[:, (mset * 8 + m) * 512:(mset * 8 + m + 1) * 512]:
                                e.matmul(o, id_bf[:], r, start=False, stop=True),
                            ]
                            S.mmg(PB[sbk], fns, reads=["kT_sb", "qT_sb", "mask", "id"])
                            pti = hh
                            S.op("ACT", lambda e, o=pT[pti][:], i=pb[sbk][:, 0:512]:
                                 e.activation(o, i, AF.Exp, scale=cst[:, 0:1]),
                                 reads=[PB[sbk], "cst"], writes=[("pT", pti)])
                            tok_o = S.op("TE", lambda e, o=pb[ob][rb:rb + 64, 0:512],
                                         l=vt_sb[:, ktt * 1024 + h * 64: ktt * 1024 + h * 64 + 64],
                                         r=pT[pti][:], st=(m == 0), sp=(m == 7), tp=(0, rb):
                                         e.matmul(o, l, r, start=st, stop=sp, tile_position=tp),
                                         reads=[("pT", pti), "vt_sb"] +
                                               ([PB[ob]] if m == 0 else []),
                                         writes=[PB[ob]] if m == 0 else [])
                            tok_d = S.op("TE", lambda e, o=pb[db_][rb:rb + 64, 0:512],
                                         l=on64_bf[:], r=pT[pti][:], st=(m == 0), sp=(m == 7),
                                         tp=(0, rb):
                                         e.matmul(o, l, r, start=st, stop=sp, tile_position=tp),
                                         reads=[("pT", pti), "on64"] +
                                               ([PB[db_]] if m == 0 else []),
                                         writes=[PB[db_]] if m == 0 else [])
                            last_mm[ob] = tok_o
                            last_mm[db_] = tok_d
                    for bk, tk in last_mm.items():
                        S.lw[PB[bk]] = tk
                    for hh in range(2):
                        ob, db_ = 2 + hh, 4 + hh
                        rb = 64 * hh
                        S.op("DVE", lambda e, o=dnr[rb:rb + 64, :], i=pb[db_][rb:rb + 64, 0:512]:
                             e.reciprocal(o, i), reads=[PB[db_]], writes=[("dnr", hh)])
                        S.op("DVE", lambda e, o=outT_ch[rb:rb + 64, pair * 512:(pair + 1) * 512],
                             a=pb[ob][rb:rb + 64, 0:512], b=dnr[rb:rb + 64, :]:
                             e.tensor_mul(o, a, b),
                             reads=[PB[ob], ("dnr", hh)], writes=[("outT", pair, hh)])
                for tt in range(4):
                    trow = qc * 512 + tt * 128
                    S.dma("SYNC", xt2[:], xh[HALO + trow: HALO + trow + 128, :],
                          writes=["xt2"])
                    for dmc in range(2):
                        bi = dmc
                        fns = [lambda e, o=pb[bi][:, 0:512],
                               l=outT_ch[:, kch * 512 + tt * 128: kch * 512 + tt * 128 + 128],
                               r=wo_bf[:, kch * 1024 + dmc * 512: kch * 1024 + dmc * 512 + 512],
                               st=(kch == 0), sp=(kch == 7):
                               e.matmul(o, l, r, start=st, stop=sp) for kch in range(8)]
                        S.mmg(PB[bi], fns,
                              reads=[("outT", pr, hh_) for pr in range(8)
                                     for hh_ in range(2)] + ["wo"])
                        S.op("DVE", lambda e, o=y_sb[:, dmc * 512:(dmc + 1) * 512],
                             a=pb[bi][:, 0:512], b=xt2[:, dmc * 512:(dmc + 1) * 512]:
                             e.tensor_add(o, a, b),
                             reads=[PB[bi], "xt2"], writes=[("y", dmc)])
                        S.op("DVE", lambda e, o=y_sb[:, dmc * 512:(dmc + 1) * 512],
                             a=y_sb[:, dmc * 512:(dmc + 1) * 512],
                             b=bo_sb[:, dmc * 512:(dmc + 1) * 512]: e.tensor_add(o, a, b),
                             reads=[("y", dmc), "bo"], writes=[("y", dmc)])
                    S.dma("SYNC", x1_out[trow: trow + 128, :], y_sb[:],
                          reads=[("y", 0), ("y", 1)])
                    emit_ln(S, y_sb[:], lnx2[:], mst, sqs2[:], c3, c2,
                            [("y", 0), ("y", 1)], "lnx2", rk_extra=["cst"])
                    for half in range(2):
                        pti = half
                        fns = []
                        for q4 in range(4):
                            dch = half * 4 + q4
                            fns.append(lambda e, o=pt_bf[pti][:, q4 * 128:(q4 + 1) * 128],
                                       i=lnx2[:, dch * 128:(dch + 1) * 128]:
                                       e.transpose(o, i, id_bf[:]))
                        tok = S.op("TE", fns[0], reads=["lnx2", "id"], writes=[PT[pti]])
                        for f in fns[1:]:
                            tok = S.op("TE", f)
                        S.lw[PT[pti]] = tok
                        S.op("DVE", lambda e, pt=pt_bf[pti], half=half:
                             e.tensor_copy(lt_sb[:, half * 512: half * 512 + 512], pt[:, 0:512]),
                             reads=[PT[pti]], writes=[("lt", half)])
                    bi = 2
                    fns = [lambda e, o=pb[bi][:, 0:8],
                           l=lt_sb[:, kch * 128:(kch + 1) * 128],
                           r=wr_bf[:, kch * 8:(kch + 1) * 8],
                           st=(kch == 0), sp=(kch == 7):
                           e.matmul(o, l, r, start=st, stop=sp) for kch in range(8)]
                    S.mmg(PB[bi], fns, reads=[("lt", 0), ("lt", 1), "wrk"])
                    S.op("DVE", lambda e: e.tensor_add(lg_sb[:], pb[2][:, 0:8], br_sb[:]),
                         reads=[PB[2], "br"], writes=["lg"])
                    S.op("DVE", lambda e: e.reduce_max(m_sb[:, 0:1], lg_sb[:], axis=AX.X),
                         reads=["lg"], writes=["msb"])
                    S.op("DVE", lambda e: e.tensor_scalar(m_sb[:, 1:2], m_sb[:, 0:1], -1.0,
                                                          None, ALU.mult),
                         reads=["msb"], writes=["msb"])
                    S.op("ACT", lambda e: e.activation(lg_sb[:], lg_sb[:], AF.Exp,
                                                       bias=m_sb[:, 1:2]),
                         reads=["lg", "msb"], writes=["lg"])
                    S.op("DVE", lambda e: e.reduce_sum(m_sb[:, 0:1], lg_sb[:], axis=AX.X),
                         reads=["lg"], writes=["msb"])
                    S.op("DVE", lambda e: e.reciprocal(m_sb[:, 0:1], m_sb[:, 0:1]),
                         reads=["msb"], writes=["msb"])
                    S.op("DVE", lambda e: e.tensor_scalar(lg_sb[:], lg_sb[:], m_sb[:, 0:1],
                                                          None, ALU.mult),
                         reads=["lg", "msb"], writes=["lg"])
                    tile_i = qc * 4 + tt
                    if fuse is None:
                        S.dma("SYNC", probs_out[trow: trow + 128, :], lg_sb[:],
                              reads=["lg"])
                    else:
                        emit_probs_fused(S, fz, lg_sb, pt_bf, PT, id_bf, tile_i)

            if fuse is not None:
                S.dma("SYNC", fuse["pTown_h"], fz["probsT_own_sb"][:],
                      reads=[("pTown", t_) for t_ in range(16)], writes=["pTownH"])
                S.dma("SYNC", fuse["wprobs_h"], fz["w_probs"][:],
                      reads=[("wp", t_) for t_ in range(16)], writes=["wpH"])
            S.op("SYNC", lambda e: e.nop(), waits=S.now("QSP"), q="QSP", n=16)
            S.barrier()

        if fuse is None:
            blk = ctx.enter_context(nc.Block())
            S.replay(blk, sems)

    return nc


def emit_probs_fused(S, fuse, lg_sb, pt_bf, PT, id_bf, tile_i):
    w_probs = fuse["w_probs"]
    pTown = fuse["probsT_own_sb"]
    splf = fuse["splf"]
    splb = fuse["splb"]
    ptf = fuse["ptf"]
    S.op("DVE", lambda e, o=w_probs[:, tile_i * 8:(tile_i + 1) * 8]:
         e.tensor_copy(o, lg_sb[:]), reads=["lg"], writes=[("wp", tile_i)])
    S.op("DVE", lambda e: e.tensor_copy(splb[:, 0:8], lg_sb[:]),
         reads=["lg"], writes=["splb"])
    S.op("DVE", lambda e: e.tensor_copy(splf[:, 0:8], splb[:, 0:8]),
         reads=["splb"], writes=["splf"])
    S.op("DVE", lambda e: e.tensor_sub(splf[:, 8:16], lg_sb[:], splf[:, 0:8]),
         reads=["lg", "splf"], writes=["splf"])
    S.op("DVE", lambda e: e.tensor_copy(splb[:, 8:16], splf[:, 8:16]),
         reads=["splf"], writes=["splb"])
    S.op("DVE", lambda e: e.tensor_copy(splf[:, 0:8], splb[:, 8:16]),
         reads=["splb"], writes=["splf"])
    S.op("DVE", lambda e: e.tensor_sub(splf[:, 8:16], splf[:, 8:16], splf[:, 0:8]),
         reads=["splf"], writes=["splf"])
    S.op("DVE", lambda e: e.tensor_copy(splb[:, 16:24], splf[:, 8:16]),
         reads=["splf"], writes=["splb"])
    for j in range(3):
        pti = j % 2
        S.op("TE", lambda e, o=pt_bf[pti][0:8, 0:128], i=splb[:, j * 8:(j + 1) * 8]:
             e.transpose(o, i, id_bf[:]),
             reads=["splb", "id"], writes=[PT[pti]])
        if j == 0:
            S.op("ACT", lambda e, o=ptf[0:8, 0:128], i=pt_bf[pti][0:8, 0:128]:
                 e.activation(o, i, AF.Identity),
                 reads=[PT[pti]], writes=["ptf"])
        else:
            S.op("DVE", lambda e, o=ptf[0:8, 0:128], i=pt_bf[pti][0:8, 0:128]:
                 e.tensor_add(o, o, i),
                 reads=[PT[pti], "ptf"], writes=["ptf"])
    S.op("DVE", lambda e, o=pTown[0:8, tile_i * 128:(tile_i + 1) * 128]:
         e.tensor_copy(o, ptf[0:8, 0:128]),
         reads=["ptf"], writes=[("pTown", tile_i)])


FCH = [128] * 21 + [42]
FOF = [sum(FCH[:i]) for i in range(22)]


def hostB(inp, core, probs_all):
    if probs_all is None:
        probs_all = np.zeros((16384, 8), np.float32)
    sel64T = np.zeros((64, 8), np.float32)
    sel8 = np.zeros((8, 64), np.float32)
    for r in range(64):
        sel64T[r, r % 8] = 1.0
        sel8[r % 8, r] = 1.0
    pT = probs_all.T.reshape(8, 8, 2048)
    segs = [(s + core) % 8 for s in range(8)]
    pr = np.zeros((64, 2048), np.float32)
    for sp in range(8):
        for e in range(8):
            pr[sp * 8 + e] = pT[e, segs[sp]]
    bu = np.zeros((128, 4 * 22), np.float32)
    b_up = _f32(inp["L0_ffn_b_up"])
    for gg in range(4):
        pad = np.zeros(2816, np.float32)
        pad[:2730] = b_up[gg]
        bu[:, gg * 22:(gg + 1) * 22] = pad.reshape(22, 128).T
    return {
        "x1": np.zeros((NT, DM), np.float32),
        "probs_own": np.zeros((NT, 8), np.float32),
        "probsT_r": pr,
        "sel64T": sel64T, "sel8": sel8,
        "wup": _f32(inp["L0_ffn_W_up"]),
        "bu": bu,
        "wdn": _f32(inp["L0_ffn_W_down"]),
        "bd_row": _f32(inp["L0_ffn_b_down"]).reshape(1, 8 * 1024),
        "ones_row": np.ones((1, 128), np.float32),
        "wkvT": _kc(_f32(inp["L1_attn_Wkv_down"]).T),
        "identB": np.eye(128, dtype=np.float32),
        "constsB": np.tile(np.array(
            [0.5, 1e-12, 1e-5, 1.0 / 1024, 2048.0, 1.0, -1.0, 1e-6], np.float32),
            (128, 1)),
    }


def build_B(fuse=None):
    nc = bass.Bass("TRN2", target_bir_lowering=False) if fuse is None else fuse["nc"]
    pref = "" if fuse is None else "b_"
    g = lambda n, s, dt=F32: nc.dram_tensor(n, s, dt, kind="ExternalInput").ap()
    if fuse is None:
        x1_in = g("x1", [NT, DM])
        probs_own = g("probs_own", [NT, 8])
        probsT_r = g("probsT_r", [64, 2048])
    else:
        x1_in = fuse["x1_hbm"]
        probs_own = None
        probsT_r = fuse["pts_hbm"]
    sel64T_in = g("sel64T", [64, 8])
    sel8_in = g("sel8", [8, 64])
    wup_in = g("wup", [4, 1024, 2730])
    bu_in = g("bu", [128, 88])
    wdn_in = g("wdn", [8, 2730, 1024])
    bd_in = g("bd_row", [1, 8192])
    ones_in = g("ones_row", [1, 128])
    wkvT_in = g("wkvT", [128, 8 * 64])
    id_in = g("identB", [128, 128])
    cst_in = g("constsB", [128, 8])

    if fuse is None:
        x2_out = nc.dram_tensor("x2", [NT, DM], F32, kind="ExternalOutput").ap()
        lat_out = nc.dram_tensor("latent", [64, NT], F32, kind="ExternalOutput").ap()
        lo_dbg = nc.dram_tensor("lo_dbg", [64, 1], F32, kind="ExternalOutput").ap()
    else:
        x2_out = fuse["x2_hbm"]
        lat_out = fuse["lat_own_hbm"]
        lo_dbg = nc.dram_tensor("lo_dbg", [64, 1], F32, kind="Internal").ap()

    lnT_hbm = nc.dram_tensor("lnT_hbm", [128, 8 * NT], BF16, kind="Internal").ap()
    yg_hbm = nc.dram_tensor("yg_hbm", [4 * NT, DM], F32, kind="Internal").ap()

    with ExitStack() as ctx:
        sb = lambda n, s, dt: ctx.enter_context(nc.sbuf_tensor(pref + n, s, dt))
        psm = lambda n, s, dt=F32: ctx.enter_context(nc.psum_tensor(n, s, dt))

        cst = sb("cst", [128, 8], F32)
        id_bf = sb("idbf", [128, 128], BF16)
        ones_bf = sb("onesbf", [1, 128], BF16)
        bd_bf = sb("bdbf", [1, 8192], BF16)
        bu_sb = sb("busb", [128, 88], F32)
        mst = sb("mst", [128, 16], F32)
        w_tok = sb("w_tok", [128, 16 * 8], F32)
        wkv_bf = sb("wkvbf", [128, 8 * 64], BF16)

        if fuse is None:
            pb = [psm(f"pb{i}", [128, 512]) for i in range(6)]
            pt_bf = [psm(f"ptb{i}", [128, 512], BF16) for i in range(2)]
            S = Sched()
            sems = make_sems(nc, ctx)
        else:
            pb, pt_bf = fuse["pb"], fuse["pt_bf"]
            S, sems = fuse["S"], fuse["sems"]
        PB = [f"pb{i}" for i in range(6)]
        PT = ["pt0", "pt1"]

        c2 = cst[:, 2:3]
        c3 = cst[:, 3:4]

        S.dma("SYNC", cst[:], cst_in, writes=["cst"])
        S.dma("SYNC", bu_sb[:], bu_in, writes=["bu"])

        with ExitStack() as pA:
            sbA = lambda n, s, dt: pA.enter_context(nc.sbuf_tensor(pref + n, s, dt))
            stage = sbA("stage", [128, 2730], F32)
            xt = sbA("xt", [128, 2048], F32)
            sqs = sbA("sqs", [128, 2048], F32)
            lnx = sbA("lnx", [128, 2048], BF16)
            lnTfull = sbA("lnTfull", [128, 8 * NT], BF16)
            pts = sbA("pts", [64, 2048], F32)
            lo = sbA("lo", [64, 1], F32)
            hi = sbA("hi", [64, 1], F32)
            th = sbA("th", [64, 1], F32)
            cnt64 = sbA("cnt64", [64, 2048], F32)
            c64 = sbA("c64", [64, 2], F32)
            mask8 = sbA("mask8", [8, 2], BF16)
            maskR = sbA("maskR", [64, 2], F32)
            wsel = sbA("wsel", [8, 2048], BF16)
            mtmp = sbA("mtmp", [128, 8], F32)
            sel64T_sb = sbA("sel64Ts", [64, 8], F32)
            sel8_bf = sbA("sel8b", [8, 64], BF16)

            def conv_small(src_ap, dst_ap, rows, cols, key):
                S.dma("SYNC", stage[0:rows, 0:cols], src_ap, writes=["stage"])
                S.op("DVE", lambda e, d=dst_ap, s=stage[0:rows, 0:cols]:
                     e.tensor_copy(d, s), reads=["stage"], writes=[key])

            S.dma("SYNC", sel64T_sb[:], sel64T_in, writes=["sel64T"])
            S.dma("SYNC", pts[:], probsT_r, reads=["agp"], writes=["pts"])
            conv_small(id_in, id_bf[:], 128, 128, "id")
            conv_small(ones_in, ones_bf[:], 1, 128, "ones")
            conv_small(sel8_in, sel8_bf[:], 8, 64, "sel8")
            conv_small(wkvT_in, wkv_bf[:], 128, 512, "wkv")
            for off in range(0, 8192, 2048):
                conv_small(bd_in[:, off:off + 2048], bd_bf[:, off:off + 2048],
                           1, 2048, "bd")

            S.op("DVE", lambda e: e.memset(lo[:], 0.0), writes=["lo"])
            S.op("DVE", lambda e: e.memset(hi[:], 1.0), writes=["hi"])

            def emit_bisect_iter():
                S.op("DVE", lambda e: e.tensor_add(th[:], lo[:], hi[:]),
                     reads=["lo", "hi"], writes=["th"])
                S.op("DVE", lambda e: e.tensor_scalar(th[:], th[:], 0.5, None, ALU.mult),
                     reads=["th"], writes=["th"])
                S.op("DVE", lambda e: e.tensor_scalar(cnt64[:], pts[:], th[:], None, ALU.is_ge),
                     reads=["pts", "th"], writes=["cnt"])
                S.op("DVE", lambda e: e.reduce_sum(c64[:, 0:1], cnt64[:], axis=AX.X),
                     reads=["cnt"], writes=["c64"])
                S.mmg(PB[4], [lambda e: e.matmul(pb[4][0:8, 0:1], sel64T_sb[:], c64[:, 0:1],
                                                 start=True, stop=True)],
                      reads=["c64", "sel64T"])
                S.op("DVE", lambda e: e.tensor_scalar(mask8[:, 0:1], pb[4][0:8, 0:1],
                                                      2048.0, None, ALU.is_ge),
                     reads=[PB[4]], writes=["mask8"])
                S.op("DVE", lambda e: e.tensor_scalar(mask8[:, 1:2], pb[4][0:8, 0:1],
                                                      2048.0, None, ALU.is_lt),
                     reads=[PB[4]], writes=["mask8"])
                S.mmg(PB[5], [lambda e: e.matmul(pb[5][0:64, 0:2], sel8_bf[:], mask8[:],
                                                 start=True, stop=True)],
                      reads=["mask8", "sel8"])
                S.op("ACT", lambda e: e.activation(maskR[:], pb[5][0:64, 0:2], AF.Identity),
                     reads=[PB[5]], writes=["maskR"])
                S.op("DVE", lambda e: e.tensor_mul(c64[:, 0:1], th[:], maskR[:, 0:1]),
                     reads=["maskR", "th"], writes=["c64"])
                S.op("DVE", lambda e: e.scalar_tensor_tensor(
                    lo[:], lo[:], maskR[:, 1:2], c64[:, 0:1], ALU.mult, ALU.add),
                     reads=["maskR", "lo", "c64"], writes=["lo"])
                S.op("DVE", lambda e: e.tensor_mul(c64[:, 1:2], th[:], maskR[:, 1:2]),
                     reads=["maskR", "th"], writes=["c64b"])
                S.op("DVE", lambda e: e.scalar_tensor_tensor(
                    hi[:], hi[:], maskR[:, 0:1], c64[:, 1:2], ALU.mult, ALU.add),
                     reads=["maskR", "hi", "c64b"], writes=["hi"])

            bis_left = [40]

            def run_bisect(k):
                n = min(k, bis_left[0])
                for _ in range(n):
                    emit_bisect_iter()
                bis_left[0] -= n

            for tt in range(16):
                par = tt % 2
                S.dma("SYNC", xt[:, par * 1024:(par + 1) * 1024],
                      x1_in[tt * 128:(tt + 1) * 128, :], writes=[("xt", par)])
                emit_ln(S, xt[:, par * 1024:(par + 1) * 1024],
                        lnx[:, par * 1024:(par + 1) * 1024],
                        mst[:, par * 8:(par + 1) * 8], sqs[:, par * 1024:(par + 1) * 1024],
                        c3, c2, ("xt", par), ("lnx", par),
                        rk_extra=["cst"], tag=par)
                for half in range(2):
                    pti = half
                    fns = []
                    for q4 in range(4):
                        dch = half * 4 + q4
                        fns.append(lambda e, o=pt_bf[pti][:, q4 * 128:(q4 + 1) * 128],
                                   i=lnx[:, par * 1024 + dch * 128: par * 1024 + (dch + 1) * 128]:
                                   e.transpose(o, i, id_bf[:]))
                    tok = S.op("TE", fns[0], reads=[("lnx", par), "id"], writes=[PT[pti]])
                    for f in fns[1:]:
                        tok = S.op("TE", f)
                    S.lw[PT[pti]] = tok

                    def evb(e, pt=pt_bf[pti], half=half, tt=tt):
                        src_ = pt[:, 0:512].rearrange("p (a b) -> p a b", a=4)
                        d = lnTfull[:].rearrange("p (a b) -> p a b", a=8, b=NT)[
                            :, half * 4: half * 4 + 4, tt * 128: tt * 128 + 128]
                        return e.tensor_copy(d, src_)
                    S.op("DVE", evb, reads=[PT[pti]], writes=[("lnf", tt, half)])
                run_bisect(3)
            for kch in range(8):
                S.dma("SYNC", lnT_hbm[:, kch * NT: (kch + 1) * NT],
                      lnTfull[:, kch * NT: (kch + 1) * NT],
                      reads=[("lnf", t_, h_) for t_ in range(16) for h_ in range(2)],
                      writes=[("lnhbm", 0, kch)])
            run_bisect(40)
            S.dma("SYNC", lo_dbg, lo[:], reads=["lo"])
            if fuse is not None:
                b_wpr = pA.enter_context(nc.sbuf_tensor("b_wpr", [128, 128], F32))
                S.dma("SYNC", b_wpr[:], fuse["wprobs_h"],
                      reads=["wpH"], writes=["bwpr"])
            if fuse is None:
                own_src = pts[0:8, :]
                own_keys = ["pts"]
            else:
                b_ptown = pA.enter_context(nc.sbuf_tensor("b_ptown", [8, 2048], F32))
                S.dma("SYNC", b_ptown[:], fuse["pTown_h"],
                      reads=["pTownH"], writes=["bptown"])
                own_src = b_ptown[0:8, :]
                own_keys = ["bptown"]
            S.op("DVE", lambda e, i=own_src: e.tensor_scalar(wsel[:], i, lo[0:8, 0:1],
                                                             None, ALU.is_ge),
                 reads=own_keys + ["lo"], writes=["wsel"])
            for tt in range(16):
                S.op("TE", lambda e, o=pt_bf[tt % 2][0:128, 0:8],
                     i=wsel[:, tt * 128:(tt + 1) * 128]:
                     e.transpose(o, i, id_bf[0:8, 0:8]),
                     reads=["wsel", "id"], writes=[PT[tt % 2]])
                S.op("ACT", lambda e, o=mtmp[:], i=pt_bf[tt % 2][0:128, 0:8]:
                     e.activation(o, i, AF.Identity),
                     reads=[PT[tt % 2]], writes=["mtmp"])
                if fuse is None:
                    S.dma("SYNC", xt[:, 0:8], probs_own[tt * 128:(tt + 1) * 128, :],
                          writes=[("xt", 0)])
                    S.op("DVE", lambda e, o=w_tok[:, tt * 8:(tt + 1) * 8], a=mtmp[:],
                         tt=tt: e.tensor_mul(o, a, xt[:, 0:8]),
                         reads=["mtmp", ("xt", 0)], writes=[("w_tok", tt)])
                else:
                    S.op("DVE", lambda e, o=w_tok[:, tt * 8:(tt + 1) * 8], a=mtmp[:],
                         b=b_wpr[:, tt * 8:(tt + 1) * 8]:
                         e.tensor_mul(o, a, b),
                         reads=["mtmp", "bwpr"], writes=[("w_tok", tt)])

            S.barrier()

        lnkeys = [("lnhbm", 0, kch) for kch in range(8)]
        with ExitStack() as pB:
            sbB = lambda n, s, dt: pB.enter_context(nc.sbuf_tensor(pref + n, s, dt))
            wup_bf = sbB("wupbf", [128, 8 * 2730], BF16)
            wd_bf = {ee: sbB(f"wdbf{ee}", [128, 22 * 1024], BF16) for ee in range(2)}
            stg = sbB("stg", [128, 2 * 910], F32)
            stg2 = sbB("stg2", [128, 2 * 512], F32)
            midT = sbB("midT", [128, 22 * 512], BF16)
            lnT_sb = sbB("lnTsb", [128, 2 * 8 * 512], BF16)
            y_sb = sbB("ysb", [128, 2048], F32)

            def wup_jobs(gg, spread):
                jobs = []
                pi = [0]
                for kch in range(8):
                    for c0_, c1_ in ((0, 910), (910, 1820), (1820, 2730)):
                        par = pi[0] % 2
                        pi[0] += 1
                        o0 = par * 910

                        def j1(kch=kch, c0_=c0_, c1_=c1_, par=par, o0=o0, gg=gg):
                            S.dma("SYNC", stg[:, o0:o0 + c1_ - c0_],
                                  wup_in[gg, kch * 128:(kch + 1) * 128, c0_:c1_],
                                  writes=[("stg", par)])

                        def j2(kch=kch, c0_=c0_, c1_=c1_, par=par, o0=o0):
                            eng = "ACT" if (spread or par == 1) else "DVE"
                            if eng == "DVE":
                                S.op("DVE", lambda e, d=wup_bf[:, kch * 2730 + c0_: kch * 2730 + c1_],
                                     s=stg[:, o0:o0 + c1_ - c0_]: e.tensor_copy(d, s),
                                     reads=[("stg", par)], writes=["wup"])
                            else:
                                S.op("ACT", lambda e, d=wup_bf[:, kch * 2730 + c0_: kch * 2730 + c1_],
                                     s=stg[:, o0:o0 + c1_ - c0_]: e.activation(d, s, AF.Identity),
                                     reads=[("stg", par)], writes=["wup"])
                        jobs += [j1, j2]
                return jobs

            for gg in range(4):
                if gg == 0:
                    for j in wup_jobs(0, spread=False):
                        j()
                for ee in range(2):
                    e_ = gg * 2 + ee
                    for mj in range(22):
                        mm = FCH[mj]
                        for pc in range(2):
                            par = (mj * 2 + pc) % 2
                            o0 = par * 512
                            S.dma("ACT", stg2[0:mm, o0:o0 + 512],
                                  wdn_in[e_, FOF[mj]:FOF[mj] + mm, pc * 512:(pc + 1) * 512],
                                  writes=[("stg2", par)])
                            if par == 0:
                                S.op("DVE", lambda e, d=wd_bf[ee][0:mm, mj * 1024 + pc * 512: mj * 1024 + (pc + 1) * 512],
                                     s=stg2[0:mm, o0:o0 + 512]: e.tensor_copy(d, s),
                                     reads=[("stg2", par)], writes=[("wd", ee)])
                            else:
                                S.op("ACT", lambda e, d=wd_bf[ee][0:mm, mj * 1024 + pc * 512: mj * 1024 + (pc + 1) * 512],
                                     s=stg2[0:mm, o0:o0 + 512]: e.activation(d, s, AF.Identity),
                                     reads=[("stg2", par)], writes=[("wd", ee)])
                pend = wup_jobs(gg + 1, spread=True) if gg < 3 else []
                for tc in range(4):
                    lpar = tc % 2
                    lz = lpar * 4096
                    if gg == 0 and tc == 0:
                        for kch in range(8):
                            S.dma("SYNC", lnT_sb[:, lz + kch * 512: lz + (kch + 1) * 512],
                                  lnT_hbm[:, kch * NT + tc * 512: kch * NT + tc * 512 + 512],
                                  reads=lnkeys, writes=[("lnT_sb", lpar)])
                    for mj in range(22):
                        mm = FCH[mj]
                        bi = mj % 2
                        fns = [lambda e, o=pb[bi][0:mm, 0:512],
                               l=wup_bf[:, kch * 2730 + FOF[mj]: kch * 2730 + FOF[mj] + mm],
                               r=lnT_sb[:, lz + kch * 512: lz + (kch + 1) * 512],
                               st=(kch == 0), sp=(kch == 7):
                               e.matmul(o, l, r, start=st, stop=sp) for kch in range(8)]
                        S.mmg(PB[bi], fns, reads=["wup", ("lnT_sb", lpar)])
                        S.op("ACT", lambda e, o=midT[0:mm, mj * 512:(mj + 1) * 512],
                             i=pb[bi][0:mm, 0:512], b=bu_sb[0:mm, gg * 22 + mj: gg * 22 + mj + 1]:
                             e.activation(o, i, AF.Silu, bias=b),
                             reads=[PB[bi], "bu"], writes=[("midT", mj)])
                    if not (gg == 3 and tc == 3):
                        ntc = (tc + 1) % 4
                        nlz = (ntc % 2) * 4096
                        for kch in range(8):
                            S.dma("SYNC", lnT_sb[:, nlz + kch * 512: nlz + (kch + 1) * 512],
                                  lnT_hbm[:, kch * NT + ntc * 512: kch * NT + ntc * 512 + 512],
                                  reads=lnkeys, writes=[("lnT_sb", ntc % 2)])
                    for tt in range(4):
                        gt = tc * 4 + tt
                        yp = tt % 2
                        for dmc in range(2):
                            for ee in range(2):
                                if tc == 3:
                                    for _ in range(3):
                                        if pend:
                                            pend.pop(0)()
                                e_ = gg * 2 + ee
                                bi = 2 + ee
                                fns = []
                                for mj in range(22):
                                    mm = FCH[mj]
                                    fns.append(lambda e, o=pb[bi][:, 0:512],
                                               l=midT[0:mm, mj * 512 + tt * 128: mj * 512 + (tt + 1) * 128],
                                               r=wd_bf[ee][0:mm, mj * 1024 + dmc * 512: mj * 1024 + dmc * 512 + 512],
                                               st=(mj == 0): e.matmul(o, l, r, start=st, stop=False))
                                fns.append(lambda e, o=pb[bi][:, 0:512],
                                           l=ones_bf[:],
                                           r=bd_bf[:, e_ * 1024 + dmc * 512: e_ * 1024 + dmc * 512 + 512]:
                                           e.matmul(o, l, r, start=False, stop=True))
                                S.mmg(PB[bi], fns,
                                      reads=[("midT", mj_) for mj_ in range(22)] +
                                            [("wd", ee), "bd", "ones"])
                                wap = w_tok[:, gt * 8 + e_: gt * 8 + e_ + 1]
                                ysl = y_sb[:, (yp * 2 + dmc) * 512: (yp * 2 + dmc + 1) * 512]
                                if ee == 0:
                                    S.op("DVE", lambda e, o=ysl, i=pb[bi][:, 0:512], w=wap:
                                         e.tensor_scalar(o, i, w, None, ALU.mult),
                                         reads=[PB[bi], ("w_tok", gt)],
                                         writes=[("y", yp, dmc)])
                                else:
                                    S.op("DVE", lambda e, o=ysl, i=pb[bi][:, 0:512], w=wap:
                                         e.scalar_tensor_tensor(o, i, w, o, ALU.mult, ALU.add),
                                         reads=[PB[bi], ("w_tok", gt), ("y", yp, dmc)],
                                         writes=[("y", yp, dmc)])
                        S.dma("SYNC", yg_hbm[gg * NT + gt * 128: gg * NT + (gt + 1) * 128, :],
                              y_sb[:, yp * 1024:(yp + 1) * 1024],
                              reads=[("y", yp, 0), ("y", yp, 1)],
                              writes=[("yg", gg, gt)])
                    if tc == 3:
                        while pend:
                            pend.pop(0)()
            S.barrier()

        with ExitStack() as pC:
            sbC = lambda n, s, dt: pC.enter_context(nc.sbuf_tensor(pref + n, s, dt))
            xt2 = sbC("xt2", [128, 2 * 1024], F32)
            yt = sbC("yt", [128, 2 * 4 * 1024], F32)
            sqs2 = sbC("sqs2", [128, 2 * 1024], F32)
            lnx2 = sbC("lnx2", [128, 2 * 1024], BF16)
            lnT2 = sbC("lnT2", [128, 2 * 8 * 128], BF16)
            mstC = sbC("mstC", [128, 16], F32)
            latT = (sbC("latT", [64, 2048], F32) if fuse is None
                    else sbC("latTtm", [128, 16 * 64], F32))

            ygkeys = [("yg", gg, gt) for gg in range(4) for gt in range(16)]
            for tt in range(16):
                p_ = tt % 2
                xz = p_ * 1024
                S.dma("SYNC", xt2[:, xz:xz + 1024], x1_in[tt * 128:(tt + 1) * 128, :],
                      writes=[("xt2", p_)])
                for gg in range(4):
                    S.dma("SYNC", yt[:, p_ * 4096 + gg * 1024: p_ * 4096 + (gg + 1) * 1024],
                          yg_hbm[gg * NT + tt * 128: gg * NT + (tt + 1) * 128, :],
                          reads=ygkeys, writes=[("ytl", p_, gg)])
                for gg in range(4):
                    S.op("DVE", lambda e, a=xt2[:, xz:xz + 1024],
                         b=yt[:, p_ * 4096 + gg * 1024: p_ * 4096 + (gg + 1) * 1024]:
                         e.tensor_add(a, a, b),
                         reads=[("ytl", p_, gg), ("xt2", p_)], writes=[("xt2", p_)])
                S.dma("SYNC", x2_out[tt * 128:(tt + 1) * 128, :], xt2[:, xz:xz + 1024],
                      reads=[("xt2", p_)])
                emit_ln(S, xt2[:, xz:xz + 1024], lnx2[:, xz:xz + 1024],
                        mstC[:, p_ * 8:(p_ + 1) * 8], sqs2[:, xz:xz + 1024],
                        c3, c2, ("xt2", p_), ("lnx2", p_),
                        rk_extra=["cst"], tag=p_)
                for half in range(2):
                    pti = half
                    fns = []
                    for q4 in range(4):
                        dch = half * 4 + q4
                        fns.append(lambda e, o=pt_bf[pti][:, q4 * 128:(q4 + 1) * 128],
                                   i=lnx2[:, xz + dch * 128: xz + (dch + 1) * 128]:
                                   e.transpose(o, i, id_bf[:]))
                    tok = S.op("TE", fns[0], reads=[("lnx2", p_), "id"], writes=[PT[pti]])
                    for f in fns[1:]:
                        tok = S.op("TE", f)
                    S.lw[PT[pti]] = tok
                    S.op("DVE", lambda e, pt=pt_bf[pti],
                         d=lnT2[:, p_ * 1024 + half * 512: p_ * 1024 + half * 512 + 512]:
                         e.tensor_copy(d, pt[:, 0:512]),
                         reads=[PT[pti]], writes=[("lnT2", p_, half)])
                bi = 4 + p_
                if fuse is None:
                    fns = [lambda e, o=pb[bi][0:64, 0:128],
                           l=wkv_bf[:, kch * 64:(kch + 1) * 64],
                           r=lnT2[:, p_ * 1024 + kch * 128: p_ * 1024 + (kch + 1) * 128],
                           st=(kch == 0), sp=(kch == 7):
                           e.matmul(o, l, r, start=st, stop=sp) for kch in range(8)]
                    S.mmg(PB[bi], fns, reads=[("lnT2", p_, 0), ("lnT2", p_, 1), "wkv"])
                    S.op("DVE", lambda e, o=latT[:, tt * 128:(tt + 1) * 128],
                         i=pb[bi][0:64, 0:128]: e.tensor_copy(o, i),
                         reads=[PB[bi]], writes=[("latT", tt)])
                else:
                    fns = [lambda e, o=pb[bi][:, 0:64],
                           l=lnT2[:, p_ * 1024 + kch * 128: p_ * 1024 + (kch + 1) * 128],
                           r=wkv_bf[:, kch * 64:(kch + 1) * 64],
                           st=(kch == 0), sp=(kch == 7):
                           e.matmul(o, l, r, start=st, stop=sp) for kch in range(8)]
                    S.mmg(PB[bi], fns, reads=[("lnT2", p_, 0), ("lnT2", p_, 1), "wkv"])
                    S.op("DVE", lambda e, o=latT[:, tt * 64:(tt + 1) * 64],
                         i=pb[bi][:, 0:64]: e.tensor_copy(o, i),
                         reads=[PB[bi]], writes=[("latT", tt)])
            if fuse is None:
                for i0 in range(0, 2048, 1024):
                    S.dma("SYNC", lat_out[:, i0:i0 + 1024], latT[:, i0:i0 + 1024],
                          reads=[("latT", t_) for t_ in range(16)])
            else:
                for tt in range(16):
                    S.dma("SYNC", lat_out[tt * 128:(tt + 1) * 128, :],
                          latT[:, tt * 64:(tt + 1) * 64],
                          reads=[("latT", tt)], writes=[("lato", tt)])
            S.op("SYNC", lambda e: e.nop(), waits=S.now("QSP", "QACT"), q="QSP", n=16)
            S.barrier()

        if fuse is None:
            blk = ctx.enter_context(nc.Block())
            S.replay(blk, sems)

    return nc




def _wq_headmajor(wqT):
    kc = _kc(wqT)
    v = kc.reshape(128, 8, 8, 256)
    return np.ascontiguousarray(v.transpose(0, 2, 1, 3).reshape(128, 16384))


def _gup_perm(gu):
    guT = gu.transpose(0, 2, 1)
    out = np.empty((8, 22, 128, 2048), np.float32)
    for mj in range(22):
        for gi in range(2):
            mjj = mj + 22 * gi
            blk = guT[:, :, mjj * 128:(mjj + 1) * 128]
            blk = blk.reshape(8, 8, 128, 128)
            out[:, mj, :, gi * 1024:(gi + 1) * 1024] = \
                blk.transpose(0, 2, 1, 3).reshape(8, 128, 1024)
    out = out.reshape(8, 11, 2, 128, 2048).transpose(0, 1, 3, 2, 4)
    return np.ascontiguousarray(out.reshape(8, 11, 128, 4096))

def hostCD(inp, core, latT_full):
    if latT_full is None:
        latT_full = np.zeros((64, 4096), np.float32)
    return {
        "x2": np.zeros((NT, DM), np.float32),
        "latT": np.ascontiguousarray(latT_full),
        "wqTl1": _wq_headmajor(_f32(inp["L1_attn_Wq"]).T),
        "wkupT": np.ascontiguousarray(_f32(inp["L1_attn_Wk_up"]).T),
        "wvupT": np.ascontiguousarray(_f32(inp["L1_attn_Wv_up"]).T),
        "woTl1": _kc(_f32(inp["L1_attn_Wo"]).T),
        "wnorm_bc": np.broadcast_to(_f32(inp["L1_ffn_norm_weight"]), (128, 1024)).copy(),
        "rwT": _kc(_f32(inp["L1_ffn_router_weight"]).T),
        "gupT": _gup_perm(_f32(inp["L1_ffn_gate_up_weight"])),
        "dnT": np.ascontiguousarray(_f32(inp["L1_ffn_down_weight"]).transpose(0, 2, 1)),
        "alpha_bc": np.broadcast_to(_f32(inp["L1_ffn_residual_alpha"]).reshape(1, 1),
                                    (128, 1)).copy(),
        "ones1": np.ones((1, 128), np.float32),
        "onescol": np.ones((128, 1), np.float32),
        "identC": np.eye(128, dtype=np.float32),
        "constsC": np.tile(np.array(
            [0.0625, 1e-12, 1e-5, 1.0 / 1024, 1e-6, 1.0, -1.0, -1e9], np.float32),
            (128, 1)),
    }


def build_CD(fuse=None):
    nc = bass.Bass("TRN2", target_bir_lowering=False) if fuse is None else fuse["nc"]
    pref = "" if fuse is None else "c_"
    g = lambda n, s, dt=F32: nc.dram_tensor(n, s, dt, kind="ExternalInput").ap()
    if fuse is None:
        x2_in = g("x2", [NT, DM])
        latT_in = g("latT", [64, 4096])
    else:
        x2_in = fuse["x2_hbm"]
        latT_in = None
    wqT_in = g("wqTl1", [128, 8 * 2048])
    wkupT_in = g("wkupT", [64, 2048])
    wvupT_in = g("wvupT", [64, 2048])
    woT_in = g("woTl1", [128, 16 * 1024])
    wnorm_in = g("wnorm_bc", [128, 1024])
    rwT_in = g("rwT", [128, 64])
    gupT_in = g("gupT", [8, 11, 128, 4096])
    dnT_in = g("dnT", [8, 2816, 1024])
    alpha_in = g("alpha_bc", [128, 1])
    ones1_in = g("ones1", [1, 128])
    onescol_in = g("onescol", [128, 1])
    id_in = g("identC", [128, 128])
    cst_in = g("constsC", [128, 8])

    out_f = nc.dram_tensor("out", [NT, DM], F32, kind="ExternalOutput").ap()
    ye_hbm = nc.dram_tensor("ye_hbm", [8 * NT, DM], BF16, kind="Internal").ap()
    x3_hbm = nc.dram_tensor("x3_hbm", [NT, DM], F32, kind="Internal" if fuse is not None else "ExternalOutput").ap()

    with ExitStack() as ctx:
        sb = lambda n, s, dt: ctx.enter_context(nc.sbuf_tensor(pref + n, s, dt))
        psm = lambda n, s, dt=F32: ctx.enter_context(nc.psum_tensor(n, s, dt))

        cst = sb("cst", [128, 8], F32)
        id_bf = sb("idbf", [128, 128], BF16)
        ones1_bf = sb("ones1bf", [1, 128], BF16)
        onescol_bf = sb("onescolbf", [128, 1], BF16)
        ones1_f = sb("ones1f", [1, 128], F32)
        mst = sb("mst", [128, 8], F32)
        alpha_sb = sb("alphasb", [128, 1], F32)
        w2_tok = sb("w2tok", [128, 16 * 8], F32)

        if fuse is None:
            pb = [psm(f"pb{i}", [128, 512]) for i in range(6)]
            pt_bf = [psm(f"ptb{i}", [128, 512], BF16) for i in range(2)]
            S = Sched()
            sems = make_sems(nc, ctx)
        else:
            pb, pt_bf = fuse["pb"], fuse["pt_bf"]
            S, sems = fuse["S"], fuse["sems"]
        PB = [f"pb{i}" for i in range(6)]
        PT = ["pt0", "pt1"]
        c1 = cst[:, 1:2]
        c2 = cst[:, 2:3]
        c3 = cst[:, 3:4]
        c4 = cst[:, 4:5]

        S.dma("SYNC", cst[:], cst_in, writes=["cst"])
        S.dma("SYNC", alpha_sb[:], alpha_in, writes=["alpha"])
        S.op("DVE", lambda e: e.tensor_scalar(alpha_sb[:], alpha_sb[:], 2.0, None, ALU.min),
             reads=["alpha"], writes=["alpha"])
        S.op("DVE", lambda e: e.tensor_scalar(alpha_sb[:], alpha_sb[:], -0.5, None, ALU.max),
             reads=["alpha"], writes=["alpha"])

        with ExitStack() as pC:
            sbC = lambda n, s, dt: pC.enter_context(nc.sbuf_tensor(pref + n, s, dt))
            stage = sbC("stage", [128, 1024], F32)
            stg3 = sbC("stg3", [128, 1024], F32)
            lnx2T = sbC("lnx2T", [128, 8 * 2048], BF16)
            lat_bf = sbC("latbf", [64, 4096], BF16)
            wkup_bf = sbC("wkupbf", [64, 2048], BF16)
            wvup_bf = sbC("wvupbf", [64, 2048], BF16)
            woh_bf = sbC("wohbf", [128, 2 * 2048], BF16)
            wqh_bf = sbC("wqhbf", [128, 2 * 2048], BF16)
            qT_h = sbC("qTh", [128, 2 * 2048], BF16)
            kT_h = sbC("kTh", [128, 2 * 4096], BF16)
            v_h = sbC("vh", [128, 32 * 256], BF16)
            outT_h = sbC("outTh", [128, 2 * 2048], BF16)
            y_acc = sbC("yaccC", [128, 16 * 1024], F32)
            pT = [sbC(f"pTc{i}", [128, 512], BF16) for i in range(2)]
            dnr = sbC("dnrc", [1, 512], F32)
            rb_f = sbC("rbf", [128, 512], F32)
            xt = sbC("xtc", [128, 1024], F32)
            sqs = sbC("sqsc", [128, 1024], F32)
            lnx = sbC("lnxc", [128, 1024], BF16)
            y_sb = sbC("ysbc", [128, 1024], F32)

            def conv_in(src_ap, dst_ap, rows, width, key, eng="SYNC"):
                for off in range(0, width, 1024):
                    wdt = min(1024, width - off)
                    S.dma(eng, stage[0:rows, 0:wdt], src_ap[0:rows, off:off + wdt],
                          writes=["stage"])
                    S.op("DVE", lambda e, d=dst_ap[0:rows, off:off + wdt],
                         s=stage[0:rows, 0:wdt]: e.tensor_copy(d, s),
                         reads=["stage"], writes=[key])

            conv_in(id_in, id_bf[:], 128, 128, "id")
            conv_in(ones1_in, ones1_bf[:], 1, 128, "ones1")
            conv_in(onescol_in, onescol_bf[:], 128, 1, "onescol")
            S.dma("SYNC", ones1_f[:], ones1_in, writes=["ones1f"])
            if fuse is None:
                conv_in(latT_in, lat_bf[:], 64, 4096, "lat")
            else:
                if fuse.get("dbg"):
                    latdbg = nc.dram_tensor("latdbg", [64, 4096], F32,
                                            kind="ExternalOutput").ap()
                lat_all = fuse["lat_all_hbm"]
                lt_f = pC.enter_context(nc.sbuf_tensor("c_ltf", [128, 64], F32))
                lt_b = pC.enter_context(nc.sbuf_tensor("c_ltb", [128, 64], BF16))
                lt_d = pC.enter_context(nc.sbuf_tensor("c_ltd", [64, 128], F32))
                for kt in range(32):
                    S.dma("SYNC", lt_f[:], lat_all[kt * 128:(kt + 1) * 128, :],
                          reads=["agl"], writes=["ltf"])
                    S.op("DVE", lambda e: e.tensor_copy(lt_b[:], lt_f[:]),
                         reads=["ltf"], writes=["ltb"])
                    S.op("TE", lambda e, o=pt_bf[kt % 2][0:64, 0:128], i=lt_b[:]:
                         e.transpose(o, i, id_bf[:]),
                         reads=["ltb", "id"], writes=[PT[kt % 2]])
                    S.op("DVE", lambda e, o=lat_bf[0:64, kt * 128:(kt + 1) * 128],
                         i=pt_bf[kt % 2][0:64, 0:128], kt=kt: e.tensor_copy(o, i),
                         reads=[PT[kt % 2]], writes=["lat"])
                    if fuse.get("dbg"):
                        S.op("DVE", lambda e, o=lt_d[:],
                             i=pt_bf[kt % 2][0:64, 0:128]: e.tensor_copy(o, i),
                             reads=[PT[kt % 2]], writes=["ltdbg"])
                        S.dma("SYNC", latdbg[:, kt * 128:(kt + 1) * 128],
                              lt_d[:], reads=["ltdbg"])
            conv_in(wkupT_in, wkup_bf[:], 64, 2048, "wkup")
            conv_in(wvupT_in, wvup_bf[:], 64, 2048, "wvup")

            for tt in range(16):
                S.dma("SYNC", xt[:], x2_in[tt * 128:(tt + 1) * 128, :], writes=["xt"])
                emit_ln(S, xt[:], lnx[:], mst, sqs[:], c3, c2, "xt", "lnx",
                        rk_extra=["cst"])
                for half in range(2):
                    pti = half
                    fns = []
                    for q4 in range(4):
                        dch = half * 4 + q4
                        fns.append(lambda e, o=pt_bf[pti][:, q4 * 128:(q4 + 1) * 128],
                                   i=lnx[:, dch * 128:(dch + 1) * 128]:
                                   e.transpose(o, i, id_bf[:]))
                    tok = S.op("TE", fns[0], reads=["lnx", "id"], writes=[PT[pti]])
                    for f in fns[1:]:
                        tok = S.op("TE", f)
                    S.lw[PT[pti]] = tok

                    def ev(e, pt=pt_bf[pti], half=half, tt=tt):
                        src = pt[:, 0:512].rearrange("p (a b) -> p a b", a=4)
                        d = lnx2T[:].rearrange("p (a b) -> p a b", a=8, b=2048)[
                            :, half * 4: half * 4 + 4, tt * 128: tt * 128 + 128]
                        return e.tensor_copy(d, src)
                    S.op("DVE", ev, reads=[PT[pti]], writes=[("lnx2T", tt)])
            ln2keys = [("lnx2T", t_) for t_ in range(16)]

            for h in range(8):
                hp = h % 2
                for off in range(0, 2048, 1024):
                    S.dma("SYNC", stage[:, 0:1024],
                          woT_in[:, 2 * h * 1024 + off: 2 * h * 1024 + off + 1024],
                          writes=["stage"])
                    S.op("DVE", lambda e, d=woh_bf[:, hp * 2048 + off: hp * 2048 + off + 1024],
                         s=stage[:, 0:1024]: e.tensor_copy(d, s),
                         reads=["stage"], writes=[("wo", hp)])
                for off in range(0, 2048, 1024):
                    S.dma("ACT", stg3[:, 0:1024],
                          wqT_in[:, h * 2048 + off: h * 2048 + off + 1024],
                          writes=["stg3"])
                    S.op("DVE", lambda e, d=wqh_bf[:, hp * 2048 + off: hp * 2048 + off + 1024],
                         s=stg3[:, 0:1024]: e.tensor_copy(d, s),
                         reads=["stg3"], writes=[("wqh", hp)])
                for mch in range(2):
                    for nc4 in range(4):
                        bi = (mch * 4 + nc4) % 6
                        fns = [lambda e, o=pb[bi][:, 0:512],
                               l=wqh_bf[:, hp * 2048 + kch * 256 + mch * 128:
                                         hp * 2048 + kch * 256 + mch * 128 + 128],
                               r=lnx2T[:, kch * 2048 + nc4 * 512: kch * 2048 + (nc4 + 1) * 512],
                               st=(kch == 0), sp=(kch == 7):
                               e.matmul(o, l, r, start=st, stop=sp) for kch in range(8)]
                        S.mmg(PB[bi], fns, reads=ln2keys + [("wqh", hp)])
                        S.op("ACT" if (mch * 4 + nc4) % 2 == 0 else "DVE",
                             lambda e, o=qT_h[:, mch * 2048 + nc4 * 512:
                                              mch * 2048 + (nc4 + 1) * 512],
                             i=pb[bi][:, 0:512]:
                             (e.activation(o, i, AF.Identity) if hasattr(e, "activation")
                              else e.tensor_copy(o, i)),
                             reads=[PB[bi]], writes=[("qT_h", mch * 4 + nc4)])
                for mch in range(2):
                    for nc8 in range(8):
                        ii = mch * 8 + nc8
                        bi = ii % 6
                        S.mmg(PB[bi], [lambda e, o=pb[bi][:, 0:512],
                                       l=wkup_bf[0:64, h * 256 + mch * 128: h * 256 + mch * 128 + 128],
                                       r=lat_bf[0:64, nc8 * 512:(nc8 + 1) * 512]:
                                       e.matmul(o, l, r, start=True, stop=True)],
                              reads=["wkup", "lat"])
                        S.op("ACT" if ii % 2 == 0 else "DVE",
                             lambda e, o=kT_h[:, mch * 4096 + nc8 * 512:
                                              mch * 4096 + (nc8 + 1) * 512],
                             i=pb[bi][:, 0:512]:
                             (e.activation(o, i, AF.Identity) if hasattr(e, "activation")
                              else e.tensor_copy(o, i)),
                             reads=[PB[bi]], writes=[("kT_h", ii)])
                for ktt in range(32):
                    bi = ktt % 6
                    S.mmg(PB[bi], [lambda e, o=pb[bi][:, 0:256],
                                   l=lat_bf[0:64, ktt * 128:(ktt + 1) * 128],
                                   r=wvup_bf[0:64, h * 256:(h + 1) * 256]:
                                   e.matmul(o, l, r, start=True, stop=True)],
                          reads=["lat", "wvup"])
                    S.op("ACT" if ktt % 2 == 0 else "DVE",
                         lambda e, o=v_h[:, ktt * 256:(ktt + 1) * 256],
                         i=pb[bi][:, 0:256]:
                         (e.activation(o, i, AF.Identity) if hasattr(e, "activation")
                          else e.tensor_copy(o, i)),
                         reads=[PB[bi]], writes=[("v_h", ktt)])
                for qc in range(4):
                    for m in range(32):
                        sbk = m % 2
                        fns = [lambda e, o=pb[sbk][:, 0:512],
                               l=kT_h[:, mch * 4096 + m * 128: mch * 4096 + (m + 1) * 128],
                               r=qT_h[:, mch * 2048 + qc * 512: mch * 2048 + (qc + 1) * 512],
                               st=(mch == 0), sp=(mch == 1):
                               e.matmul(o, l, r, start=st, stop=sp) for mch in range(2)]
                        S.mmg(PB[sbk], fns,
                              reads=[("kT_h", mch_ * 8 + m // 4) for mch_ in range(2)] +
                                    [("qT_h", mch_ * 4 + qc) for mch_ in range(2)])
                        pti = m % 2
                        S.op("ACT", lambda e, o=pT[pti][:], i=pb[sbk][:, 0:512]:
                             e.activation(o, i, AF.Exp, scale=cst[:, 0:1]),
                             reads=[PB[sbk], "cst"], writes=[("pT", pti)])
                        for dm in range(2):
                            S.op("TE", lambda e, o=pb[2 + dm][:, 0:512],
                                 l=v_h[:, m * 256 + dm * 128: m * 256 + (dm + 1) * 128],
                                 r=pT[pti][:], st=(m == 0), sp=(m == 31):
                                 e.matmul(o, l, r, start=st, stop=sp),
                                 reads=[("pT", pti), ("v_h", m)] +
                                       ([PB[2 + dm]] if m == 0 else []),
                                 writes=[PB[2 + dm]] if m == 0 else [])
                        tok_d = S.op("TE", lambda e, o=pb[4][0:1, 0:512],
                                     l=onescol_bf[:], r=pT[pti][:], st=(m == 0), sp=(m == 31):
                                     e.matmul(o, l, r, start=st, stop=sp),
                                     reads=[("pT", pti), "onescol"] +
                                           ([PB[4]] if m == 0 else []),
                                     writes=[PB[4]] if m == 0 else [])
                    S.lw[PB[2]] = S.now("TE")[0]
                    S.lw[PB[3]] = S.now("TE")[0]
                    S.lw[PB[4]] = tok_d
                    S.op("DVE", lambda e: e.reciprocal(dnr[:], pb[4][0:1, 0:512]),
                         reads=[PB[4]], writes=["dnr"])
                    S.mmg(PB[5], [lambda e: e.matmul(pb[5][:, 0:512], ones1_f[:], dnr[:],
                                                     start=True, stop=True)],
                          reads=["dnr", "ones1f"])
                    S.op("ACT", lambda e: e.activation(rb_f[:], pb[5][:, 0:512], AF.Identity),
                         reads=[PB[5]], writes=["rb_f"])
                    for dm in range(2):
                        S.op("DVE", lambda e, o=outT_h[:, dm * 2048 + qc * 512:
                                                       dm * 2048 + (qc + 1) * 512],
                             a=pb[2 + dm][:, 0:512], dm=dm: e.tensor_mul(o, a, rb_f[:]),
                             reads=[PB[2 + dm], "rb_f"], writes=[("outT", dm, qc)])
                okeys = [("outT", d_, q_) for d_ in range(2) for q_ in range(4)]
                for tt in range(16):
                    for dmc in range(2):
                        bi = dmc
                        fns = [lambda e, o=pb[bi][:, 0:512],
                               l=outT_h[:, dm_ * 2048 + tt * 128: dm_ * 2048 + tt * 128 + 128],
                               r=woh_bf[:, hp * 2048 + dm_ * 1024 + dmc * 512:
                                        hp * 2048 + dm_ * 1024 + dmc * 512 + 512],
                               st=(dm_ == 0), sp=(dm_ == 1):
                               e.matmul(o, l, r, start=st, stop=sp) for dm_ in range(2)]
                        S.mmg(PB[bi], fns, reads=okeys + [("wo", hp)])
                        ya = y_acc[:, tt * 1024 + dmc * 512: tt * 1024 + dmc * 512 + 512]
                        if h == 0:
                            S.op("ACT", lambda e, o=ya, i=pb[bi][:, 0:512]:
                                 e.activation(o, i, AF.Identity),
                                 reads=[PB[bi]], writes=[("yac", tt, dmc)])
                        else:
                            S.op("DVE", lambda e, o=ya, i=pb[bi][:, 0:512]:
                                 e.tensor_add(o, i, o),
                                 reads=[PB[bi], ("yac", tt, dmc)],
                                 writes=[("yac", tt, dmc)])
            for tt in range(16):
                S.dma("SYNC", xt[:], x2_in[tt * 128:(tt + 1) * 128, :], writes=["xt"])
                for dmc in range(2):
                    S.op("DVE", lambda e, o=y_sb[:, dmc * 512:(dmc + 1) * 512],
                         a=y_acc[:, tt * 1024 + dmc * 512: tt * 1024 + dmc * 512 + 512],
                         b=xt[:, dmc * 512:(dmc + 1) * 512]: e.tensor_add(o, a, b),
                         reads=[("yac", tt, dmc), "xt"], writes=[("y", dmc)])
                S.dma("SYNC", x3_hbm[tt * 128:(tt + 1) * 128, :], y_sb[:],
                      reads=[("y", 0), ("y", 1)], writes=[("x3", tt)])
            S.barrier()

        x3keys = [("x3", t_) for t_ in range(16)]
        with ExitStack() as pD:
            sbD = lambda n, s, dt: pD.enter_context(nc.sbuf_tensor(pref + n, s, dt))
            stage = sbD("stageD", [128, 2 * 4096], F32)
            stg2 = sbD("stg2D", [128, 2 * 1024], F32)
            normT = sbD("normT", [128, 8 * 2048], BF16)
            hidT = sbD("hidT", [128, 22 * 1024], BF16)
            dn_bf = sbD("dnbf", [128, 22 * 1024], BF16)
            gup_bf = sbD("gupbf", [128, 2 * 4096], BF16)
            y_t = sbD("yt", [128, 2 * 1024], BF16)
            wn_sb = sbD("wnsb", [128, 1024], BF16)
            rw_bf = sbD("rwbf", [128, 64], BF16)
            xt3 = sbD("xt3", [128, 1024], F32)
            sqs3 = sbD("sqs3", [128, 1024], F32)
            lnx3 = sbD("lnx3", [128, 1024], BF16)
            nrm = sbD("nrm", [128, 1024], BF16)
            lg = sbD("lgD", [128, 8], F32)
            msD = sbD("msD", [128, 4], F32)
            prb = sbD("prb", [128, 8], F32)
            gtmp = sbD("gtmp", [128, 1024], BF16)
            utmp = sbD("utmp", [128, 1024], BF16)
            lnf = sbD("lnf", [128, 1024], F32)
            moef = sbD("moef", [128, 1024], F32)

            S.dma("SYNC", stg2[:, 0:1024], wnorm_in, writes=[("stg2D", 0)])
            S.op("DVE", lambda e: e.tensor_copy(wn_sb[:], stg2[:, 0:1024]),
                 reads=[("stg2D", 0)], writes=["wn"])
            S.dma("SYNC", stage[:, 0:64], rwT_in, writes=["stageD"])
            S.op("DVE", lambda e: e.tensor_copy(rw_bf[:], stage[:, 0:64]),
                 reads=["stageD"], writes=["rw"])

            for tt in range(16):
                S.dma("SYNC", xt3[:], x3_hbm[tt * 128:(tt + 1) * 128, :],
                      reads=x3keys, writes=["xt3"])
                emit_ln(S, xt3[:], lnx3[:], mst, sqs3[:], c3, c2, "xt3", "lnx3",
                        rk_extra=["cst"])
                S.op("ACT", lambda e: e.activation(sqs3[:], lnx3[:], AF.Square),
                     reads=["lnx3"], writes=[("sqk", 0)])
                S.op("DVE", lambda e: e.reduce_sum(msD[:, 0:1], sqs3[:], axis=AX.X),
                     reads=[("sqk", 0)], writes=["msD"])
                S.op("DVE", lambda e: e.tensor_scalar(msD[:, 0:1], msD[:, 0:1], c3, None,
                                                      ALU.mult),
                     reads=["msD", "cst"], writes=["msD"])
                S.op("ACT", lambda e: e.activation(msD[:, 1:2], msD[:, 0:1], AF.Sqrt,
                                                   bias=c4),
                     reads=["msD", "cst"], writes=["msD"])
                S.op("DVE", lambda e: e.reciprocal(msD[:, 1:2], msD[:, 1:2]),
                     reads=["msD"], writes=["msD"])
                S.op("DVE", lambda e: e.tensor_scalar(gtmp[:], lnx3[:], msD[:, 1:2], None,
                                                      ALU.mult),
                     reads=["lnx3", "msD"], writes=["gt"])
                S.op("DVE", lambda e: e.tensor_mul(nrm[:], gtmp[:], wn_sb[:]),
                     reads=["gt", "wn"], writes=["nrm"])
                for half in range(2):
                    pti = half
                    fns = []
                    for q4 in range(4):
                        dch = half * 4 + q4
                        fns.append(lambda e, o=pt_bf[pti][:, q4 * 128:(q4 + 1) * 128],
                                   i=nrm[:, dch * 128:(dch + 1) * 128]:
                                   e.transpose(o, i, id_bf[:]))
                    tok = S.op("TE", fns[0], reads=["nrm", "id"], writes=[PT[pti]])
                    for f in fns[1:]:
                        tok = S.op("TE", f)
                    S.lw[PT[pti]] = tok

                    def ev2(e, pt=pt_bf[pti], half=half, tt=tt):
                        src = pt[:, 0:512].rearrange("p (a b) -> p a b", a=4)
                        d = normT[:].rearrange("p (a b) -> p a b", a=8, b=2048)[
                            :, half * 4: half * 4 + 4, tt * 128: tt * 128 + 128]
                        return e.tensor_copy(d, src)
                    S.op("DVE", ev2, reads=[PT[pti]], writes=[("normT", tt)])
                bi = 4
                fns = [lambda e, o=pb[bi][:, 0:8],
                       l=normT[:].rearrange("p (a b) -> p a b", a=8, b=2048)[
                           :, kch, tt * 128: tt * 128 + 128],
                       r=rw_bf[:, kch * 8:(kch + 1) * 8],
                       st=(kch == 0), sp=(kch == 7):
                       e.matmul(o, l, r, start=st, stop=sp) for kch in range(8)]
                S.mmg(PB[bi], fns, reads=[("normT", tt), "rw"])
                S.op("ACT", lambda e: e.activation(lg[:], pb[4][:, 0:8], AF.Identity),
                     reads=[PB[4]], writes=["lg"])
                S.op("DVE", lambda e: e.reduce_max(msD[:, 2:3], lg[:], axis=AX.X),
                     reads=["lg"], writes=["msD"])
                S.op("DVE", lambda e: e.tensor_scalar(msD[:, 3:4], msD[:, 2:3], -1.0, None,
                                                      ALU.mult),
                     reads=["msD"], writes=["msD"])
                S.op("ACT", lambda e: e.activation(prb[:], lg[:], AF.Exp, bias=msD[:, 3:4]),
                     reads=["lg", "msD"], writes=["prb"])
                S.op("DVE", lambda e: e.reduce_sum(msD[:, 2:3], prb[:], axis=AX.X),
                     reads=["prb"], writes=["msD"])
                S.op("DVE", lambda e: e.reciprocal(msD[:, 2:3], msD[:, 2:3]),
                     reads=["msD"], writes=["msD"])
                S.op("DVE", lambda e: e.tensor_scalar(prb[:], prb[:], msD[:, 2:3], None,
                                                      ALU.mult),
                     reads=["prb", "msD"], writes=["prb"])
                S.op("DVE", lambda e: e.reduce_max(msD[:, 2:3], prb[:], axis=AX.X),
                     reads=["prb"], writes=["msD"])
                S.op("DVE", lambda e: e.tensor_scalar(lg[:], prb[:], msD[:, 2:3], None,
                                                      ALU.is_ge),
                     reads=["prb", "msD"], writes=["lg"])
                S.op("DVE", lambda e: e.scalar_tensor_tensor(sqs3[:, 0:8], lg[:],
                                                             cst[:, 7:8], prb[:],
                                                             ALU.mult, ALU.add),
                     reads=["lg", "prb", "cst"], writes=[("sqk", 0)])
                S.op("DVE", lambda e: e.reduce_max(msD[:, 3:4], sqs3[:, 0:8], axis=AX.X),
                     reads=[("sqk", 0)], writes=["msD"])
                S.op("DVE", lambda e: e.tensor_scalar(sqs3[:, 8:16], sqs3[:, 0:8],
                                                      msD[:, 3:4], None, ALU.is_ge),
                     reads=[("sqk", 0), "msD"], writes=[("sqk", 0)])
                S.op("DVE", lambda e: e.tensor_add(lg[:], lg[:], sqs3[:, 8:16]),
                     reads=["lg", ("sqk", 0)], writes=["lg"])
                S.op("DVE", lambda e: e.tensor_mul(prb[:], prb[:], lg[:]),
                     reads=["prb", "lg"], writes=["prb"])
                S.op("DVE", lambda e: e.reduce_sum(msD[:, 2:3], prb[:], axis=AX.X),
                     reads=["prb"], writes=["msD"])
                S.op("DVE", lambda e: e.reciprocal(msD[:, 2:3], msD[:, 2:3]),
                     reads=["msD"], writes=["msD"])
                S.op("DVE", lambda e, o=w2_tok[:, tt * 8:(tt + 1) * 8]:
                     e.tensor_scalar(o, prb[:], msD[:, 2:3], None, ALU.mult),
                     reads=["prb", "msD"], writes=[("w2", tt)])

            nkeys = [("normT", t_) for t_ in range(16)]
            for e_ in range(8):
                for mj in range(22):
                    par = mj % 2
                    o0 = par * 1024
                    S.dma("ACT", stg2[:, o0:o0 + 1024],
                          dnT_in[e_, mj * 128:(mj + 1) * 128, :], writes=[("stg2D", par)])
                    if par == 0:
                        S.op("DVE", lambda e, d=dn_bf[:, mj * 1024:(mj + 1) * 1024],
                             s=stg2[:, o0:o0 + 1024]: e.tensor_copy(d, s),
                             reads=[("stg2D", par)], writes=["dn"])
                    else:
                        S.op("ACT", lambda e, d=dn_bf[:, mj * 1024:(mj + 1) * 1024],
                             s=stg2[:, o0:o0 + 1024]: e.activation(d, s, AF.Identity),
                             reads=[("stg2D", par)], writes=["dn"])
                for tc2 in range(2):
                    for mjp in range(11):
                        par = mjp % 2
                        o0 = par * 4096
                        S.dma("SYNC", stage[:, o0:o0 + 4096], gupT_in[e_, mjp, :, :],
                              writes=[("stageD", par)])
                        S.op("DVE", lambda e,
                             d=gup_bf[:, par * 4096: par * 4096 + 2048],
                             s=stage[:, o0:o0 + 2048]: e.tensor_copy(d, s),
                             reads=[("stageD", par)], writes=[("gup", par)])
                        S.op("ACT", lambda e,
                             d=gup_bf[:, par * 4096 + 2048: par * 4096 + 4096],
                             s=stage[:, o0 + 2048:o0 + 4096]: e.activation(d, s, AF.Identity),
                             reads=[("stageD", par)], writes=[("gup", par)])
                        for sub in range(2):
                            mj = 2 * mjp + sub
                            gof = par * 4096 + sub * 2048
                            for nc2 in range(2):
                                tglob = tc2 * 1024 + nc2 * 512
                                big, biu = 0, 1
                                fns = [lambda e, o=pb[big][:, 0:512],
                                       l=gup_bf[:, gof + kch * 128: gof + (kch + 1) * 128],
                                       r=normT[:, kch * 2048 + tglob: kch * 2048 + tglob + 512],
                                       st=(kch == 0), sp=(kch == 7):
                                       e.matmul(o, l, r, start=st, stop=sp) for kch in range(8)]
                                S.mmg(PB[big], fns, reads=nkeys + [("gup", par)])
                                fns = [lambda e, o=pb[biu][:, 0:512],
                                       l=gup_bf[:, gof + 1024 + kch * 128:
                                                gof + 1024 + (kch + 1) * 128],
                                       r=normT[:, kch * 2048 + tglob: kch * 2048 + tglob + 512],
                                       st=(kch == 0), sp=(kch == 7):
                                       e.matmul(o, l, r, start=st, stop=sp) for kch in range(8)]
                                S.mmg(PB[biu], fns, reads=nkeys + [("gup", par)])
                                S.op("ACT", lambda e, o=gtmp[:, nc2 * 512:(nc2 + 1) * 512],
                                     i=pb[big][:, 0:512]: e.activation(o, i, AF.Silu),
                                     reads=[PB[big]], writes=["gt"])
                                S.op("ACT", lambda e, o=utmp[:, nc2 * 512:(nc2 + 1) * 512],
                                     i=pb[biu][:, 0:512]: e.activation(o, i, AF.Identity),
                                     reads=[PB[biu]], writes=["ut"])
                            S.op("DVE", lambda e, o=hidT[:, mj * 1024:(mj + 1) * 1024],
                                 a=gtmp[:, 0:1024], b=utmp[:, 0:1024]: e.tensor_mul(o, a, b),
                                 reads=["gt", "ut"], writes=[("hid", mj)])
                    for tt4 in range(8):
                        gt_ = tc2 * 8 + tt4
                        yp = tt4 % 2
                        for dmc in range(2):
                            bi = 4 + dmc
                            fns = [lambda e, o=pb[bi][:, 0:512],
                                   l=hidT[:, mj * 1024 + tt4 * 128: mj * 1024 + tt4 * 128 + 128],
                                   r=dn_bf[:, mj * 1024 + dmc * 512: mj * 1024 + dmc * 512 + 512],
                                   st=(mj == 0), sp=(mj == 21):
                                   e.matmul(o, l, r, start=st, stop=sp) for mj in range(22)]
                            S.mmg(PB[bi], fns,
                                  reads=[("hid", mj_) for mj_ in range(22)] + ["dn"])
                            wap = w2_tok[:, gt_ * 8 + e_: gt_ * 8 + e_ + 1]
                            S.op("DVE", lambda e, o=y_t[:, yp * 1024 + dmc * 512:
                                                        yp * 1024 + dmc * 512 + 512],
                                 i=pb[bi][:, 0:512], w=wap:
                                 e.tensor_scalar(o, i, w, None, ALU.mult),
                                 reads=[PB[bi], ("w2", gt_)], writes=[("ya", yp, dmc)])
                        S.dma("ACT", ye_hbm[e_ * NT + gt_ * 128: e_ * NT + (gt_ + 1) * 128, :],
                              y_t[:, yp * 1024:(yp + 1) * 1024],
                              reads=[("ya", yp, 0), ("ya", yp, 1)],
                              writes=[("yeh", e_, gt_)])

            for tt in range(16):
                S.dma("SYNC", xt3[:], x3_hbm[tt * 128:(tt + 1) * 128, :],
                      reads=x3keys, writes=["xt3"])
                mk = ("mstk", 0)
                S.op("DVE", lambda e: e.reduce_sum(mst[:, 0:1], xt3[:], axis=AX.X),
                     reads=["xt3"], writes=[mk])
                S.op("ACT", lambda e: e.activation(sqs3[:], xt3[:], AF.Square),
                     reads=["xt3"], writes=[("sqk", 0)])
                S.op("DVE", lambda e: e.reduce_sum(mst[:, 1:2], sqs3[:], axis=AX.X),
                     reads=[("sqk", 0)], writes=[mk])
                S.op("DVE", lambda e: e.tensor_scalar(mst[:, 2:3], mst[:, 0:1], c3, None, ALU.mult),
                     reads=[mk, "cst"], writes=[mk])
                S.op("DVE", lambda e: e.tensor_scalar(mst[:, 3:4], mst[:, 1:2], c3, None, ALU.mult),
                     reads=[mk], writes=[mk])
                S.op("DVE", lambda e: e.tensor_mul(mst[:, 4:5], mst[:, 2:3], mst[:, 2:3]),
                     reads=[mk], writes=[mk])
                S.op("DVE", lambda e: e.tensor_sub(mst[:, 5:6], mst[:, 3:4], mst[:, 4:5]),
                     reads=[mk], writes=[mk])
                S.op("ACT", lambda e: e.activation(mst[:, 6:7], mst[:, 5:6], AF.Sqrt, bias=c2),
                     reads=[mk, "cst"], writes=[mk])
                S.op("DVE", lambda e: e.reciprocal(mst[:, 6:7], mst[:, 6:7]),
                     reads=[mk], writes=[mk])
                S.op("DVE", lambda e: e.tensor_mul(mst[:, 7:8], mst[:, 2:3], mst[:, 6:7]),
                     reads=[mk], writes=[mk])
                S.op("DVE", lambda e: e.tensor_scalar(mst[:, 7:8], mst[:, 7:8], -1.0, None, ALU.mult),
                     reads=[mk], writes=[mk])
                S.op("ACT", lambda e: e.activation(lnf[:], xt3[:], AF.Identity,
                                                   bias=mst[:, 7:8], scale=mst[:, 6:7]),
                     reads=[mk, "xt3"], writes=["lnf"])
                for e_ in range(8):
                    S.dma("SYNC", y_t[:, (e_ % 2) * 1024:(e_ % 2 + 1) * 1024],
                          ye_hbm[e_ * NT + tt * 128: e_ * NT + (tt + 1) * 128, :],
                          reads=[("yeh", e_, tt)], writes=[("ytl", e_ % 2)])
                    if e_ == 0:
                        S.op("DVE", lambda e, i=y_t[:, 0:1024]: e.tensor_copy(moef[:], i),
                             reads=[("ytl", 0)], writes=["moef"])
                    else:
                        S.op("DVE", lambda e, i=y_t[:, (e_ % 2) * 1024:(e_ % 2 + 1) * 1024]:
                             e.tensor_add(moef[:], moef[:], i),
                             reads=[("ytl", e_ % 2), "moef"], writes=["moef"])
                S.op("DVE", lambda e: e.tensor_scalar(moef[:], moef[:],
                                                      alpha_sb[:, 0:1], None, ALU.mult),
                     reads=["moef", "alpha"], writes=["moef"])
                S.op("DVE", lambda e: e.tensor_add(lnf[:], lnf[:], xt3[:]),
                     reads=["lnf", "xt3"], writes=["lnf"])
                S.op("DVE", lambda e: e.tensor_add(lnf[:], lnf[:], moef[:]),
                     reads=["lnf", "moef"], writes=["lnf"])
                S.dma("SYNC", out_f[tt * 128:(tt + 1) * 128, :], lnf[:], reads=["lnf"])

            S.op("SYNC", lambda e: e.nop(), waits=S.now("QSP", "QACT"), q="QSP", n=16)
            S.barrier()

        if fuse is None:
            blk = ctx.enter_context(nc.Block())
            S.replay(blk, sems)

    return nc


def build_full(dbg=False):
    nc = bass.Bass("TRN2", target_bir_lowering=False)
    kind = "ExternalOutput" if dbg else "Internal"
    x1_hbm = nc.dram_tensor("x1f", [NT, DM], F32, kind=kind).ap()
    x2_hbm = nc.dram_tensor("x2f", [NT, DM], F32, kind=kind).ap()
    pTown_h = nc.dram_tensor("pTownh", [8, 2048], F32)
    wprobs_h = nc.dram_tensor("wprobsh", [128, 128], F32)
    pts_h = nc.dram_tensor("ptsh", [64, 2048], F32)
    lato_h = nc.dram_tensor("latoh", [2048, 64], F32)
    latall_h = nc.dram_tensor("latallh", [4096, 64], F32)

    with ExitStack() as ctx:
        sb = lambda n, s, dt: ctx.enter_context(nc.sbuf_tensor(n, s, dt))
        psm = lambda n, s, dt=F32: ctx.enter_context(nc.psum_tensor(n, s, dt))
        pb = [psm(f"pb{i}", [128, 512]) for i in range(6)]
        pt_bf = [psm(f"ptb{i}", [128, 512], BF16) for i in range(2)]
        S = Sched()
        sems = make_sems(nc, ctx)
        fuse = {
            "nc": nc, "S": S, "sems": sems, "pb": pb, "pt_bf": pt_bf,
            "x1_hbm": x1_hbm, "x2_hbm": x2_hbm,
            "pts_hbm": pts_h.ap(),
            "pTown_h": pTown_h.ap(),
            "wprobs_h": wprobs_h.ap(),
            "lat_own_hbm": lato_h.ap(),
            "lat_all_hbm": latall_h.ap(),
        }

        fuse["dbg"] = dbg
        build_A(fuse)
        S.op("POOL", lambda e: e.collective_compute(
            "AllGather", ALU.bypass, replica_groups=[list(range(8))],
            ins=[pTown_h.ap().opt()], outs=[pts_h.ap().opt()]),
            reads=["pTownH"], writes=["agp"], q="POOL", n=1)

        build_B(fuse)
        S.op("POOL", lambda e: e.collective_compute(
            "AllGather", ALU.bypass,
            replica_groups=[[0, 1], [2, 3], [4, 5], [6, 7]],
            ins=[lato_h.ap().opt()], outs=[latall_h.ap().opt()]),
            reads=[("lato", t_) for t_ in range(16)], writes=["agl"],
            q="POOL", n=1)

        build_CD(fuse)

        blk = ctx.enter_context(nc.Block())
        S.replay(blk, sems)
    return nc


def hostFull(inp, core):
    d = hostA(inp, core)
    b = hostB(inp, core, None)
    for k in ("x1", "probs_own", "probsT_r"):
        b.pop(k)
    d.update(b)
    cd = hostCD(inp, core, None)
    for k in ("x2", "latT"):
        cd.pop(k)
    d.update(cd)
    return d


from concourse.bass_utils import run_bass_kernel_spmd

_CACHE = {}


def _get(name, builder):
    if name not in _CACHE:
        _CACHE[name] = builder()
    return _CACHE[name]


def kernel(**inputs):
    nc = _get("FULL", build_full)
    maps = [hostFull(inputs, c) for c in range(8)]
    res = run_bass_kernel_spmd(nc, maps, core_ids=list(range(8))).results
    out = np.concatenate([np.asarray(r["out"]) for r in res], 0)
    return out.reshape(4, 4096, 1024)


# version 15
# speedup vs baseline: 1.0208x; 1.0208x over previous
import numpy as np
import ml_dtypes
from contextlib import ExitStack
import concourse.bass as bass
import concourse.mybir as mybir

F32 = mybir.dt.float32
BF16 = mybir.dt.bfloat16
AF = mybir.ActivationFunctionType
ALU = mybir.AluOpType
AX = mybir.AxisListType
BF = ml_dtypes.bfloat16

NT = 2048
HALO = 640
NTH = NT + HALO
DM = 1024

ENGS = ("TE", "ACT", "DVE", "POOL", "SYNC")
DROP_TE_SELF = False
DMAQ = {"SYNC": "QSP", "ACT": "QACT", "POOL": "QPOOL"}


class Sched:
    def __init__(self):
        self.items = {e: [] for e in ENGS}
        self.c = {"TE": 0, "ACT": 0, "DVE": 0, "POOL": 0,
                  "QSP": 0, "QACT": 0, "QPOOL": 0}
        self.lw = {}
        self.rd = {}

    def _tok_of(self, eng, q, n):
        q = q or eng
        self.c[q] += n
        return q, self.c[q]

    def op(self, eng, fn, reads=(), writes=(), waits=(), q=None, n=1):
        wl = {}

        def addw(tok):
            if tok is None:
                return
            qq, vv = tok
            if vv > 0 and wl.get(qq, 0) < vv:
                wl[qq] = vv

        for t in waits:
            addw(t)
        for k in reads:
            addw(self.lw.get(k))
        for k in writes:
            addw(self.lw.get(k))
            for qq, vv in self.rd.get(k, {}).items():
                addw((qq, vv))
        q = q or eng
        if q.startswith("Q"):
            addw((q, self.c[q]))
        self.c[q] += n
        tok = (q, self.c[q])
        self.items[eng].append((fn, sorted(wl.items()), q, n))
        for k in reads:
            d = self.rd.setdefault(k, {})
            if d.get(tok[0], 0) < tok[1]:
                d[tok[0]] = tok[1]
        for k in writes:
            self.lw[k] = tok
            self.rd[k] = {}
        return tok

    def dma(self, eng, out, in_, reads=(), writes=(), waits=()):
        return self.op(eng, lambda e, o=out, i=in_: e.dma_start(out=o, in_=i),
                       reads, writes, waits, q=DMAQ[eng], n=16)

    def mmg(self, bank, fns, reads=(), waits=()):
        tok = self.op("TE", fns[0], reads=reads, writes=[bank], waits=waits)
        for fn in fns[1:]:
            tok = self.op("TE", fn)
        self.lw[bank] = tok
        for k in reads:
            d = self.rd.setdefault(k, {})
            if d.get(tok[0], 0) < tok[1]:
                d[tok[0]] = tok[1]
        return tok

    def now(self, *qs):
        return [(q, self.c[q]) for q in qs]

    def barrier_tokens(self):
        return self.now("TE", "ACT", "DVE", "POOL", "QSP", "QACT", "QPOOL")

    def barrier(self):
        toks = [t for t in self.barrier_tokens() if t[1] > 0]
        for e in ENGS:
            self.items[e].append((None, toks, None, 0))

    def replay(self, blk, sems):
        def mk(eng_items, ename):
            def body(eng):
                wm = {}
                for fn, waits, q, n in eng_items:
                    for qq, vv in waits:
                        if ename == "TE" and qq == "TE" and DROP_TE_SELF:
                            continue
                        if vv > wm.get(qq, 0):
                            eng.wait_ge(sems[qq], vv)
                            wm[qq] = vv
                    if fn is not None:
                        fn(eng).then_inc(sems[q], n)
            return body
        blk.tensor(mk(self.items["TE"], "TE"))
        blk.scalar(mk(self.items["ACT"], "ACT"))
        blk.vector(mk(self.items["DVE"], "DVE"))
        blk.gpsimd(mk(self.items["POOL"], "POOL"))
        blk.sync(mk(self.items["SYNC"], "SYNC"))


def make_sems(nc, ctx):
    names = ("TE", "ACT", "DVE", "POOL", "QSP", "QACT", "QPOOL")
    return {q: ctx.enter_context(nc.semaphore("s_" + q.lower())) for q in names}


def _bf(x):
    return np.ascontiguousarray(np.asarray(x), dtype=BF)


def _f32(x):
    return np.ascontiguousarray(np.asarray(x), dtype=np.float32)


def _kc(x, p=128):
    K, N = x.shape
    return np.ascontiguousarray(x.reshape(K // p, p, N).transpose(1, 0, 2).reshape(p, -1))


def _ctile_cols(v, p=128):
    return np.ascontiguousarray(np.asarray(v, dtype=np.float32).reshape(-1, p).T)


def rotary_tables_half(t0, n):
    D = 64
    inv_freq = 1.0 / 10000.0 ** (np.arange(0, D, 2, dtype=np.float32) / D)
    pos = np.arange(t0, t0 + n, dtype=np.float32)
    fr = inv_freq[np.arange(D) % 32][:, None] * pos[None, :]
    return (np.concatenate([np.cos(fr)] * 2, 0).astype(np.float32),
            np.concatenate([np.sin(fr)] * 2, 0).astype(np.float32))


def rot_lhsT():
    R = np.zeros((128, 128), dtype=np.float32)
    for base in (0, 64):
        for d in range(32):
            R[base + d, base + d + 32] = -1.0
            R[base + d + 32, base + d] = 1.0
    return np.ascontiguousarray(R.T)


def l0_masks(half):
    out = np.zeros((128, 2 * 8 * 512), dtype=np.float32)
    NEG = -30000.0
    r = np.arange(128)[:, None]
    j = np.arange(512)[None, :]
    for m in range(8):
        d = 512 - 128 * m + j - r
        std = np.where((d >= 0) & (d <= 511), 0.0, NEG)
        out[:, m * 512:(m + 1) * 512] = std
        dead = np.full((128, 512), NEG, dtype=np.float32)
        first = dead if (half == 0 and m < 4) else std
        out[:, (8 + m) * 512:(9 + m) * 512] = first
    return _bf(out)


def hostA(inp, core):
    b, h = core // 2, core % 2
    x = _f32(inp["L0_attn_x"])
    own = x[b, h * 2048:(h + 1) * 2048]
    halo = (np.zeros((HALO, DM), np.float32) if h == 0
            else x[b, 2048 - HALO:2048])
    xh = np.concatenate([halo, own], 0)
    cos, sin = rotary_tables_half(h * 2048 - HALO, NTH)
    if h == 0:
        c2_, s2_ = rotary_tables_half(0, NTH)
        cos[:, :HALO] = c2_[:, :HALO]
        sin[:, :HALO] = s2_[:, :HALO]
    cdiag = {}
    for p in "qkv":
        w = _f32(inp[f"L0_attn_c{p}w"])
        d = np.zeros((128, 8 * 4 * 128), np.float32)
        for cch in range(8):
            for j in range(4):
                blk = np.zeros((128, 128), np.float32)
                np.fill_diagonal(blk, w[cch * 128:(cch + 1) * 128, j])
                d[:, (cch * 4 + j) * 128:(cch * 4 + j + 1) * 128] = blk
        cdiag[p] = d
    grp2 = np.zeros((128, 2), np.float32)
    grp2[:64, 0] = 1.0
    grp2[64:, 1] = 1.0
    return {
        "xh": _f32(xh),
        "wqT": _kc(_f32(inp["L0_attn_Wq"]).T),
        "wkT": _kc(_f32(inp["L0_attn_Wk"]).T),
        "wvT": _kc(_f32(inp["L0_attn_Wv"]).T),
        "woT": _kc(_f32(inp["L0_attn_Wo"]).T),
        "bq": _ctile_cols(inp["L0_attn_bq"]),
        "bk": _ctile_cols(inp["L0_attn_bk"]),
        "bv": _ctile_cols(inp["L0_attn_bv"]),
        "cbq": _ctile_cols(inp["L0_attn_cqb"]),
        "cbk": _ctile_cols(inp["L0_attn_ckb"]),
        "cbv": _ctile_cols(inp["L0_attn_cvb"]),
        "cdq": cdiag["q"], "cdk": cdiag["k"], "cdv": cdiag["v"],
        "cosT": cos, "sinT": sin, "rotT": rot_lhsT(),
        "grp2": grp2, "bc2": np.ascontiguousarray(grp2.T),
        "ones64": np.ones((128, 64), np.float32),
        "ident": np.eye(128, dtype=np.float32),
        "masks": l0_masks(h),
        "bo_bc": np.broadcast_to(_f32(inp["L0_attn_bo"]), (128, 1024)).copy(),
        "wr": _kc(_f32(inp["L0_ffn_Wr"])),
        "br_bc": np.broadcast_to(_f32(inp["L0_ffn_br"]), (128, 8)).copy(),
        "consts": np.tile(np.array(
            [0.125, 1e-12, 1e-5, 1.0 / 1024, 0.0, 1.0, -1.0, 0.5], np.float32),
            (128, 1)),
    }


def emit_ln(S, src_ap, dst_ap, mst, sq_scratch, c3, c2, rk_src, wk_dst,
            rk_extra=(), tag=0):
    mk, sk = ("mstk", tag), ("sqk", tag)
    rk_src = list(rk_src) if isinstance(rk_src, list) else [rk_src]
    S.op("DVE", lambda e: e.reduce_sum(mst[:, 0:1], src_ap, axis=AX.X),
         reads=rk_src, writes=[mk])
    S.op("ACT", lambda e: e.activation(sq_scratch, src_ap, AF.Square),
         reads=rk_src, writes=[sk])
    S.op("DVE", lambda e: e.reduce_sum(mst[:, 1:2], sq_scratch, axis=AX.X),
         reads=[sk], writes=[mk])
    S.op("DVE", lambda e: e.tensor_scalar(mst[:, 2:3], mst[:, 0:1], c3, None, ALU.mult),
         reads=[mk], writes=[mk])
    S.op("DVE", lambda e: e.tensor_scalar(mst[:, 3:4], mst[:, 1:2], c3, None, ALU.mult),
         reads=[mk], writes=[mk])
    S.op("DVE", lambda e: e.tensor_mul(mst[:, 4:5], mst[:, 2:3], mst[:, 2:3]),
         reads=[mk], writes=[mk])
    S.op("DVE", lambda e: e.tensor_sub(mst[:, 5:6], mst[:, 3:4], mst[:, 4:5]),
         reads=[mk], writes=[mk])
    S.op("ACT", lambda e: e.activation(mst[:, 6:7], mst[:, 5:6], AF.Sqrt, bias=c2),
         reads=[mk], writes=[mk])
    S.op("DVE", lambda e: e.reciprocal(mst[:, 6:7], mst[:, 6:7]),
         reads=[mk], writes=[mk])
    S.op("DVE", lambda e: e.tensor_mul(mst[:, 7:8], mst[:, 2:3], mst[:, 6:7]),
         reads=[mk], writes=[mk])
    S.op("DVE", lambda e: e.tensor_scalar(mst[:, 7:8], mst[:, 7:8], -1.0, None, ALU.mult),
         reads=[mk], writes=[mk])
    return S.op("ACT", lambda e: e.activation(dst_ap, src_ap, AF.Identity,
                                              bias=mst[:, 7:8], scale=mst[:, 6:7]),
                reads=[mk] + rk_src + list(rk_extra), writes=[wk_dst])


def make_pump_jobs(S, pstg, pbst, chunks, tag, outkey):
    jobs = []
    for i, (src0, dst0, rows, w) in enumerate(chunks):
        par = i % 2
        z = par * 1024

        def jl(src=src0, rows=rows, w=w, z=z, par=par):
            S.dma("SYNC", pstg[0:rows, z:z + w], src,
                  writes=[("pstg", tag, par)])

        def jc(rows=rows, w=w, z=z, par=par, eng=("DVE" if i % 2 == 0 else "ACT")):
            if eng == "DVE":
                S.op("DVE", lambda e, d=pbst[0:rows, z:z + w],
                     s=pstg[0:rows, z:z + w]: e.tensor_copy(d, s),
                     reads=[("pstg", tag, par)], writes=[("pbst", tag, par)])
            else:
                S.op("ACT", lambda e, d=pbst[0:rows, z:z + w],
                     s=pstg[0:rows, z:z + w]: e.activation(d, s, AF.Identity),
                     reads=[("pstg", tag, par)], writes=[("pbst", tag, par)])

        def js(dst=dst0, rows=rows, w=w, z=z, par=par):
            S.dma("ACT", dst, pbst[0:rows, z:z + w],
                  reads=[("pbst", tag, par)], writes=[outkey])
        jobs += [jl, jc, js]
    return jobs


def drain_jobs(jobs, k):
    for _ in range(k):
        if jobs:
            jobs.pop(0)()


CH = [512] * 5 + [128]
CH0 = [0, 512, 1024, 1536, 2048, 2560]


def build_A(fuse=None):
    nc = bass.Bass("TRN2", target_bir_lowering=False) if fuse is None else fuse["nc"]
    pref = "" if fuse is None else "a_"
    g = lambda n, s, dt=F32: nc.dram_tensor(n, s, dt, kind="ExternalInput").ap()

    xh = g("xh", [NTH, DM])
    wT = {p: g(f"w{p}T", [128, 8 * 1024]) for p in "qkv"}
    woT = g("woT", [128, 8 * 1024])
    bias_in = {p: g(f"b{p}", [128, 8]) for p in "qkv"}
    cbias_in = {p: g(f"cb{p}", [128, 8]) for p in "qkv"}
    cdiag_in = {p: g(f"cd{p}", [128, 8 * 4 * 128]) for p in "qkv"}
    cosT_in = g("cosT", [128, NTH])
    sinT_in = g("sinT", [128, NTH])
    rotT_in = g("rotT", [128, 128])
    grp2_in = g("grp2", [128, 2])
    bc2_in = g("bc2", [2, 128])
    ones64_in = g("ones64", [128, 64])
    ident_in = g("ident", [128, 128])
    masks_in = g("masks", [128, 8192], BF16)
    bo_in = g("bo_bc", [128, 1024])
    wr_in = g("wr", [128, 64])
    br_in = g("br_bc", [128, 8])
    consts_in = g("consts", [128, 8])

    if fuse is None:
        x1_out = nc.dram_tensor("x1", [NT, DM], F32, kind="ExternalOutput").ap()
        probs_out = nc.dram_tensor("probs", [NT, 8], F32, kind="ExternalOutput").ap()
    else:
        x1_out = fuse["x1_hbm"]
        probs_out = None

    qhbm = nc.dram_tensor("qhbm", [128, 8 * NTH], BF16, kind="Internal").ap()
    khbm = nc.dram_tensor("khbm", [128, 8 * NTH], BF16, kind="Internal").ap()
    vhbm = nc.dram_tensor("vhbm", [128, 21 * 1024], BF16, kind="Internal").ap()

    with ExitStack() as ctx:
        sb = lambda n, s, dt: ctx.enter_context(nc.sbuf_tensor(pref + n, s, dt))
        psm = lambda n, s, dt=F32: ctx.enter_context(nc.psum_tensor(n, s, dt))

        bias_sb = {p: sb(f"b{p}sb", [128, 8], F32) for p in "qkv"}
        cbias_sb = {p: sb(f"cb{p}sb", [128, 8], F32) for p in "qkv"}
        bo_sb = sb("bosb", [128, 1024], F32)
        br_sb = sb("brsb", [128, 8], F32)
        cst = sb("cst", [128, 8], F32)
        id_bf = sb("idbf", [128, 128], BF16)
        rot_bf = sb("rotbf", [128, 128], BF16)
        grp_bf = sb("grpbf", [128, 2], BF16)
        bc2_bf = sb("bc2bf", [2, 128], BF16)
        on64_bf = sb("on64bf", [128, 64], BF16)
        wr_bf = sb("wrbf", [128, 64], BF16)
        mst = sb("mst", [128, 16], F32)
        if fuse is not None:
            fz = {
                "w_probs": sb("fzwp", [128, 128], F32),
                "probsT_own_sb": sb("fzpt", [8, 2048], F32),
                "splf": sb("fzsplf", [128, 16], F32),
                "splb": sb("fzsplb", [128, 24], BF16),
                "ptf": sb("fzptf", [8, 128], F32),
            }

        if fuse is None:
            pb = [psm(f"pb{i}", [128, 512]) for i in range(6)]
            pt_bf = [psm(f"ptb{i}", [128, 512], BF16) for i in range(2)]
            S = Sched()
            sems = make_sems(nc, ctx)
        else:
            pb, pt_bf = fuse["pb"], fuse["pt_bf"]
            S, sems = fuse["S"], fuse["sems"]
        PB = [f"pb{i}" for i in range(6)]
        PT = ["pt0", "pt1"]

        c1 = cst[:, 1:2]
        c2 = cst[:, 2:3]
        c3 = cst[:, 3:4]

        with ExitStack() as p1:
            sb1 = lambda n, s, dt: p1.enter_context(nc.sbuf_tensor(pref + n, s, dt))
            lnxT = sb1("lnxT", [128, 8 * NTH], BF16)
            stage = sb1("stage", [128, 2048], F32)
            w_bf = sb1("w_bf", [128, 8 * 1024], BF16)
            cd_bf = sb1("cd_bf", [128, 4 * 128 * 8], BF16)
            cos_bf = sb1("cosbf", [128, NTH], BF16)
            sin_bf = sb1("sinbf", [128, NTH], BF16)
            xt = sb1("xt", [128, 2048], F32)
            sqs = sb1("sqs", [128, 2048], F32)
            lnx = sb1("lnx", [128, 2048], BF16)
            pump = []
            if fuse is not None:
                pstg = sb1("pstgA", [128, 2048], F32)
                pbst = sb1("pbstA", [128, 2048], BF16)
                wupi, wdni = fuse["wup_in"], fuse["wdn_in"]
                wupbh, wdbh = fuse["wupb_h"], fuse["wdb_h"]
                ch_up = []
                for gg4 in range(4):
                    for kch9 in range(8):
                        for c0w, c1w in ((0, 910), (910, 1820), (1820, 2730)):
                            ch_up.append((wupi[gg4, kch9 * 128:(kch9 + 1) * 128, c0w:c1w],
                                          wupbh[gg4 * 1024 + kch9 * 128:
                                                gg4 * 1024 + (kch9 + 1) * 128, c0w:c1w],
                                          128, c1w - c0w))
                ch_wd = []
                for e9 in range(8):
                    for mj9 in range(22):
                        mm9 = FCH[mj9]
                        ch_wd.append((wdni[e9, FOF[mj9]:FOF[mj9] + mm9, :],
                                      wdbh[e9 * 2730 + FOF[mj9]:
                                           e9 * 2730 + FOF[mj9] + mm9, :],
                                      mm9, 1024))
                pump = make_pump_jobs(S, pstg, pbst, ch_up, "pA", "wupbH")
                pump += make_pump_jobs(S, pstg, pbst, ch_wd, "pA", "wdbH")
            plin = {b: sb1(f"plin{b}", [128, 8 * 512], BF16) for b in range(2)}
            pc = sb1("pc", [128, 8 * 512], BF16)
            qch = sb1("qch", [128, 8 * 512], BF16)
            vch = sb1("vch", [128, 4 * 1024], BF16)
            tmp1 = sb1("tmp1", [128, 2 * 512], F32)
            tmp2 = sb1("tmp2", [128, 2 * 512], F32)
            tmp3 = sb1("tmp3", [128, 2 * 512], BF16)
            sqb = sb1("sqb", [128, 2 * 512], BF16)
            rs_f = sb1("rs_f", [2, 2 * 512], F32)
            rs_b = sb1("rs_b", [2, 2 * 512], BF16)
            rb_bf = sb1("rb_bf", [128, 2 * 512], BF16)

            for p in "qkv":
                S.dma("SYNC", bias_sb[p][:], bias_in[p], writes=[f"b{p}"])
                S.dma("SYNC", cbias_sb[p][:], cbias_in[p], writes=[f"cb{p}"])
            S.dma("SYNC", bo_sb[:], bo_in, writes=["bo"])
            S.dma("SYNC", br_sb[:], br_in, writes=["br"])
            S.dma("SYNC", cst[:], consts_in, writes=["cst"])

            def conv_in(src_ap, dst_ap, width, key):
                for off in range(0, width, 2048):
                    wdt = min(2048, width - off)
                    sl = stage[:, 0:wdt]
                    S.dma("SYNC", sl, src_ap[:, off:off + wdt], writes=["stage"])
                    S.op("DVE", lambda e, d=dst_ap[:, off:off + wdt], s=sl:
                         e.tensor_copy(d, s), reads=["stage"], writes=[key])

            conv_in(cosT_in, cos_bf[:], NTH, "cos")
            conv_in(sinT_in, sin_bf[:], NTH, "sin")
            conv_in(rotT_in, rot_bf[:], 128, "rot")
            conv_in(ident_in, id_bf[:], 128, "id")
            conv_in(ones64_in, on64_bf[:], 64, "on64")
            conv_in(wr_in, wr_bf[:], 64, "wrk")
            S.dma("SYNC", stage[0:128, 0:2], grp2_in, writes=["stage"])
            S.op("DVE", lambda e: e.tensor_copy(grp_bf[:], stage[0:128, 0:2]),
                 reads=["stage"], writes=["grp"])
            S.dma("SYNC", stage[0:2, 0:128], bc2_in, writes=["stage"])
            S.op("DVE", lambda e: e.tensor_copy(bc2_bf[:], stage[0:2, 0:128]),
                 reads=["stage"], writes=["bc2"])

            for tt in range(21):
                drain_jobs(pump, 2)
                par = tt % 2
                zz = par * 1024
                S.dma("SYNC", xt[:, zz:zz + 1024], xh[tt * 128:(tt + 1) * 128, :],
                      writes=[("xt", par)])
                emit_ln(S, xt[:, zz:zz + 1024], lnx[:, zz:zz + 1024],
                        mst[:, par * 8:(par + 1) * 8], sqs[:, zz:zz + 1024],
                        c3, c2, ("xt", par), ("lnx", par),
                        rk_extra=["cst"], tag=par)
                for half in range(2):
                    pti = half
                    fns = []
                    for q4 in range(4):
                        dch = half * 4 + q4
                        fns.append(lambda e, o=pt_bf[pti][:, q4 * 128:(q4 + 1) * 128],
                                   i=lnx[:, zz + dch * 128: zz + (dch + 1) * 128]:
                                   e.transpose(o, i, id_bf[:]))
                    tok = S.op("TE", fns[0], reads=[("lnx", par), "id"], writes=[PT[pti]])
                    for f in fns[1:]:
                        tok = S.op("TE", f)
                    S.lw[PT[pti]] = tok

                    def ev(e, pt=pt_bf[pti], half=half, tt=tt):
                        src = pt[:, 0:512].rearrange("p (a b) -> p a b", a=4)
                        d = lnxT[:].rearrange("p (a b) -> p a b", a=8, b=NTH)[
                            :, half * 4: half * 4 + 4, tt * 128: tt * 128 + 128]
                        return e.tensor_copy(d, src)
                    S.op("DVE", ev, reads=[PT[pti]], writes=[("lnxT", tt)])
            lnxT_keys = [("lnxT", tt) for tt in range(21)]

            pbi = [0]

            def next_pb():
                i = pbi[0]
                pbi[0] = (i + 1) % 4
                return i

            for p in "qkv":
                conv_in(wT[p], w_bf[:], 8192, "w_bf")
                conv_in(cdiag_in[p], cd_bf[:], 4096, "cd_bf")
                for ci, (n, t0) in enumerate(zip(CH, CH0)):
                    ntile = n // 128
                    buf = ci % 2
                    for cch in range(8):
                        drain_jobs(pump, 2)
                        bi = next_pb()
                        fns = [lambda e, o=pb[bi][:, 0:n],
                               l=w_bf[:, kch * 1024 + cch * 128: kch * 1024 + (cch + 1) * 128],
                               r=lnxT[:, kch * NTH + t0: kch * NTH + t0 + n],
                               st=(kch == 0), sp=(kch == 7):
                               e.matmul(o, l, r, start=st, stop=sp) for kch in range(8)]
                        S.mmg(PB[bi], fns, reads=lnxT_keys + ["w_bf"])
                        S.op("ACT", lambda e, o=plin[buf][:, cch * 512: cch * 512 + n],
                             i=pb[bi][:, 0:n], b=bias_sb[p][:, cch:cch + 1]:
                             e.activation(o, i, AF.Identity, bias=b),
                             reads=[PB[bi], f"b{p}"], writes=[("plin", buf, cch)])
                    prev_buf = 1 - buf
                    prevlen = CH[ci - 1] if ci > 0 else 0
                    for cch in range(8):
                        drain_jobs(pump, 2)
                        bi = next_pb()
                        specs = []
                        for j in [3, 2, 1, 0]:
                            sh = 3 - j
                            l_ap = cd_bf[:, (cch * 4 + j) * 128:(cch * 4 + j + 1) * 128]
                            if n - sh > 0:
                                specs.append((pb[bi][:, sh:n], l_ap,
                                              plin[buf][:, cch * 512: cch * 512 + n - sh]))
                            if sh > 0 and ci > 0:
                                specs.append((pb[bi][:, 0:sh], l_ap,
                                              plin[prev_buf][:, cch * 512 + prevlen - sh:
                                                             cch * 512 + prevlen]))
                        fns = [lambda e, o=o_, l=l_, r=r_, st=(i == 0),
                               sp=(i == len(specs) - 1):
                               e.matmul(o, l, r, start=st, stop=sp)
                               for i, (o_, l_, r_) in enumerate(specs)]
                        rd = [("plin", buf, cch), "cd_bf"]
                        if ci > 0:
                            rd.append(("plin", prev_buf, cch))
                        S.mmg(PB[bi], fns, reads=rd)
                        S.op("ACT", lambda e, o=pc[:, cch * 512: cch * 512 + n],
                             i=pb[bi][:, 0:n], b=cbias_sb[p][:, cch:cch + 1]:
                             e.activation(o, i, AF.Identity, bias=b),
                             reads=[PB[bi], f"cb{p}"], writes=[("pc", cch)])
                    if p in "qk":
                        for cch in range(8):
                            drain_jobs(pump, 2)
                            par = cch % 2
                            z = par * 512
                            pcsl = pc[:, cch * 512: cch * 512 + n]
                            pck = ("pc", cch)
                            bi = next_pb()
                            S.mmg(PB[bi], [lambda e, o=pb[bi][:, 0:n], r=pcsl:
                                           e.matmul(o, rot_bf[:], r, start=True, stop=True)],
                                  reads=[pck, "rot"])
                            S.op("DVE", lambda e, o=tmp1[:, z:z + n], a=pcsl,
                                 b=cos_bf[:, t0:t0 + n]: e.tensor_mul(o, a, b),
                                 reads=[pck, "cos"], writes=[("tmp1", par)])
                            S.op("ACT", lambda e, o=tmp3[:, z:z + n], i=pb[bi][:, 0:n]:
                                 e.activation(o, i, AF.Copy), reads=[PB[bi]],
                                 writes=[("tmp3", par)])
                            S.op("DVE", lambda e, o=tmp2[:, z:z + n], a=tmp3[:, z:z + n],
                                 b=sin_bf[:, t0:t0 + n]: e.tensor_mul(o, a, b),
                                 reads=[("tmp3", par), "sin"], writes=[("tmp2", par)])
                            S.op("DVE", lambda e, o=pcsl, a=tmp1[:, z:z + n],
                                 b=tmp2[:, z:z + n]:
                                 e.tensor_add(o, a, b),
                                 reads=[("tmp1", par), ("tmp2", par)], writes=[pck])
                            S.op("DVE", lambda e, o=sqb[:, z:z + n], a=pcsl, b=pcsl:
                                 e.tensor_mul(o, a, b), reads=[pck], writes=[("sqb", par)])
                            bi2 = next_pb()
                            S.mmg(PB[bi2], [lambda e, o=pb[bi2][0:2, 0:n], r=sqb[:, z:z + n]:
                                            e.matmul(o, grp_bf[:], r, start=True, stop=True)],
                                  reads=[("sqb", par), "grp"])
                            S.op("ACT", lambda e, o=rs_f[:, z:z + n], i=pb[bi2][0:2, 0:n]:
                                 e.activation(o, i, AF.Sqrt, bias=c1[0:2, :]),
                                 reads=[PB[bi2], "cst"], writes=[("rs_f", par)])
                            S.op("DVE", lambda e, o=rs_f[:, z:z + n]: e.reciprocal(o, o),
                                 reads=[("rs_f", par)], writes=[("rs_f", par)])
                            S.op("DVE", lambda e, o=rs_b[:, z:z + n], i=rs_f[:, z:z + n]:
                                 e.tensor_copy(o, i),
                                 reads=[("rs_f", par)], writes=[("rs_b", par)])
                            bi3 = next_pb()
                            S.mmg(PB[bi3], [lambda e, o=pb[bi3][:, 0:n], r=rs_b[0:2, z:z + n]:
                                            e.matmul(o, bc2_bf[:], r, start=True, stop=True)],
                                  reads=[("rs_b", par), "bc2"])
                            S.op("ACT", lambda e, o=rb_bf[:, z:z + n], i=pb[bi3][:, 0:n]:
                                 e.activation(o, i, AF.Copy), reads=[PB[bi3]],
                                 writes=[("rb_bf", par)])
                            S.op("DVE", lambda e, o=qch[:, cch * 512: cch * 512 + n],
                                 a=pcsl, b=rb_bf[:, z:z + n]: e.tensor_mul(o, a, b),
                                 reads=[pck, ("rb_bf", par)], writes=[("qch", cch)])
                        dst_hbm = qhbm if p == "q" else khbm
                        for cch in range(8):
                            S.dma("SYNC", dst_hbm[:, cch * NTH + t0: cch * NTH + t0 + n],
                                  qch[:, cch * 512: cch * 512 + n],
                                  reads=[("qch", cch)],
                                  writes=[(p + "hbm", ci, cch)])
                    else:
                        for cch in range(8):
                            drain_jobs(pump, 2)
                            for k in range(ntile):
                                pti = k % 2
                                S.op("TE", lambda e, o=pt_bf[pti][:, 0:128],
                                     i=pc[:, cch * 512 + k * 128: cch * 512 + (k + 1) * 128]:
                                     e.transpose(o, i, id_bf[:]),
                                     reads=[("pc", cch), "id"], writes=[PT[pti]])
                                S.op("DVE", lambda e, o=vch[:, k * 1024 + cch * 128:
                                                           k * 1024 + (cch + 1) * 128],
                                     i=pt_bf[pti][:, 0:128]: e.tensor_copy(o, i),
                                     reads=[PT[pti]], writes=[("vch", k, cch)])
                        S.dma("SYNC", vhbm[:, (t0 // 128) * 1024: (t0 // 128 + ntile) * 1024],
                              vch[:, 0: ntile * 1024],
                              reads=[("vch", k_, c_) for k_ in range(ntile) for c_ in range(8)],
                              writes=[("vhbm", ci)])

            drain_jobs(pump, len(pump))
            S.barrier()

        with ExitStack() as p2:
            sb2 = lambda n, s, dt: p2.enter_context(nc.sbuf_tensor(pref + n, s, dt))
            mask_sb = sb2("masksb", [128, 8192], BF16)
            wo_bf = sb2("wobf", [128, 8 * 1024], BF16)
            stage2 = sb2("stage2", [128, 2048], F32)
            qT_sb = sb2("qTsb", [128, 8 * 512], BF16)
            kT_sb = sb2("kTsb", [128, 8 * 1024], BF16)
            vt_sb = sb2("vtsb", [128, 8 * 1024], BF16)
            outT_ch = sb2("outTch", [128, 8 * 512], BF16)
            pT = [sb2(f"pT{i}", [128, 512], BF16) for i in range(2)]
            dnr = sb2("dnr", [128, 512], F32)
            y_sb = sb2("y_sb", [128, 1024], F32)
            lt_sb = sb2("lt_sb", [128, 8 * 128], BF16)
            lg_sb = sb2("lg_sb", [128, 8], F32)
            m_sb = sb2("m_sb", [128, 2], F32)
            xt2 = sb2("xt2", [128, 1024], F32)
            lnx2 = sb2("lnx2", [128, 1024], BF16)
            sqs2 = sb2("sqs2", [128, 1024], F32)

            S.dma("SYNC", mask_sb[:], masks_in, writes=["mask"])
            for off in range(0, 8192, 2048):
                sl = stage2[:, 0:2048]
                S.dma("SYNC", sl, woT[:, off:off + 2048], writes=["stage2"])
                S.op("DVE", lambda e, d=wo_bf[:, off:off + 2048], s=sl:
                     e.tensor_copy(d, s), reads=["stage2"], writes=["wo"])

            spill_keys = ([("qhbm", ci, cc) for ci in range(6) for cc in range(8)] +
                          [("khbm", ci, cc) for ci in range(6) for cc in range(8)] +
                          [("vhbm", ci) for ci in range(6)])
            for qc in range(4):
                qoff = HALO + 512 * qc
                koff0 = qoff - 512
                mset = 1 if qc == 0 else 0
                for cch in range(8):
                    S.dma("SYNC", qT_sb[:, cch * 512:(cch + 1) * 512],
                          qhbm[:, cch * NTH + qoff: cch * NTH + qoff + 512],
                          reads=spill_keys, writes=["qT_sb"])
                    S.dma("SYNC", kT_sb[:, cch * 1024:(cch + 1) * 1024],
                          khbm[:, cch * NTH + koff0: cch * NTH + koff0 + 1024],
                          reads=spill_keys, writes=["kT_sb"])
                S.dma("SYNC", vt_sb[:], vhbm[:, (koff0 // 128) * 1024:
                                              (koff0 // 128 + 8) * 1024],
                      reads=spill_keys, writes=["vt_sb"])

                for pair in range(8):
                    last_mm = {}
                    for m in range(8):
                        ktt = m
                        for hh in range(2):
                            h = pair * 2 + hh
                            cch = h // 2
                            rb = 64 * (h % 2)
                            sbk = hh
                            ob = 2 + hh
                            db_ = 4 + hh
                            fns = [
                                lambda e, o=pb[sbk][:, 0:512],
                                l=kT_sb[rb:rb + 64, cch * 1024 + m * 128: cch * 1024 + (m + 1) * 128],
                                r=qT_sb[rb:rb + 64, cch * 512: (cch + 1) * 512]:
                                e.matmul(o, l, r, start=True, stop=False),
                                lambda e, o=pb[sbk][:, 0:512],
                                r=mask_sb[:, (mset * 8 + m) * 512:(mset * 8 + m + 1) * 512]:
                                e.matmul(o, id_bf[:], r, start=False, stop=True),
                            ]
                            S.mmg(PB[sbk], fns, reads=["kT_sb", "qT_sb", "mask", "id"])
                            pti = hh
                            S.op("ACT", lambda e, o=pT[pti][:], i=pb[sbk][:, 0:512]:
                                 e.activation(o, i, AF.Exp, scale=cst[:, 0:1]),
                                 reads=[PB[sbk], "cst"], writes=[("pT", pti)])
                            tok_o = S.op("TE", lambda e, o=pb[ob][rb:rb + 64, 0:512],
                                         l=vt_sb[:, ktt * 1024 + h * 64: ktt * 1024 + h * 64 + 64],
                                         r=pT[pti][:], st=(m == 0), sp=(m == 7), tp=(0, rb):
                                         e.matmul(o, l, r, start=st, stop=sp, tile_position=tp),
                                         reads=[("pT", pti), "vt_sb"] +
                                               ([PB[ob]] if m == 0 else []),
                                         writes=[PB[ob]] if m == 0 else [])
                            tok_d = S.op("TE", lambda e, o=pb[db_][rb:rb + 64, 0:512],
                                         l=on64_bf[:], r=pT[pti][:], st=(m == 0), sp=(m == 7),
                                         tp=(0, rb):
                                         e.matmul(o, l, r, start=st, stop=sp, tile_position=tp),
                                         reads=[("pT", pti), "on64"] +
                                               ([PB[db_]] if m == 0 else []),
                                         writes=[PB[db_]] if m == 0 else [])
                            last_mm[ob] = tok_o
                            last_mm[db_] = tok_d
                    for bk, tk in last_mm.items():
                        S.lw[PB[bk]] = tk
                    for hh in range(2):
                        ob, db_ = 2 + hh, 4 + hh
                        rb = 64 * hh
                        S.op("DVE", lambda e, o=dnr[rb:rb + 64, :], i=pb[db_][rb:rb + 64, 0:512]:
                             e.reciprocal(o, i), reads=[PB[db_]], writes=[("dnr", hh)])
                        S.op("DVE", lambda e, o=outT_ch[rb:rb + 64, pair * 512:(pair + 1) * 512],
                             a=pb[ob][rb:rb + 64, 0:512], b=dnr[rb:rb + 64, :]:
                             e.tensor_mul(o, a, b),
                             reads=[PB[ob], ("dnr", hh)], writes=[("outT", pair, hh)])
                for tt in range(4):
                    trow = qc * 512 + tt * 128
                    S.dma("SYNC", xt2[:], xh[HALO + trow: HALO + trow + 128, :],
                          writes=["xt2"])
                    for dmc in range(2):
                        bi = dmc
                        fns = [lambda e, o=pb[bi][:, 0:512],
                               l=outT_ch[:, kch * 512 + tt * 128: kch * 512 + tt * 128 + 128],
                               r=wo_bf[:, kch * 1024 + dmc * 512: kch * 1024 + dmc * 512 + 512],
                               st=(kch == 0), sp=(kch == 7):
                               e.matmul(o, l, r, start=st, stop=sp) for kch in range(8)]
                        S.mmg(PB[bi], fns,
                              reads=[("outT", pr, hh_) for pr in range(8)
                                     for hh_ in range(2)] + ["wo"])
                        S.op("DVE", lambda e, o=y_sb[:, dmc * 512:(dmc + 1) * 512],
                             a=pb[bi][:, 0:512], b=xt2[:, dmc * 512:(dmc + 1) * 512]:
                             e.tensor_add(o, a, b),
                             reads=[PB[bi], "xt2"], writes=[("y", dmc)])
                        S.op("DVE", lambda e, o=y_sb[:, dmc * 512:(dmc + 1) * 512],
                             a=y_sb[:, dmc * 512:(dmc + 1) * 512],
                             b=bo_sb[:, dmc * 512:(dmc + 1) * 512]: e.tensor_add(o, a, b),
                             reads=[("y", dmc), "bo"], writes=[("y", dmc)])
                    S.dma("SYNC", x1_out[trow: trow + 128, :], y_sb[:],
                          reads=[("y", 0), ("y", 1)])
                    emit_ln(S, y_sb[:], lnx2[:], mst, sqs2[:], c3, c2,
                            [("y", 0), ("y", 1)], "lnx2", rk_extra=["cst"])
                    for half in range(2):
                        pti = half
                        fns = []
                        for q4 in range(4):
                            dch = half * 4 + q4
                            fns.append(lambda e, o=pt_bf[pti][:, q4 * 128:(q4 + 1) * 128],
                                       i=lnx2[:, dch * 128:(dch + 1) * 128]:
                                       e.transpose(o, i, id_bf[:]))
                        tok = S.op("TE", fns[0], reads=["lnx2", "id"], writes=[PT[pti]])
                        for f in fns[1:]:
                            tok = S.op("TE", f)
                        S.lw[PT[pti]] = tok
                        S.op("DVE", lambda e, pt=pt_bf[pti], half=half:
                             e.tensor_copy(lt_sb[:, half * 512: half * 512 + 512], pt[:, 0:512]),
                             reads=[PT[pti]], writes=[("lt", half)])
                    bi = 2
                    fns = [lambda e, o=pb[bi][:, 0:8],
                           l=lt_sb[:, kch * 128:(kch + 1) * 128],
                           r=wr_bf[:, kch * 8:(kch + 1) * 8],
                           st=(kch == 0), sp=(kch == 7):
                           e.matmul(o, l, r, start=st, stop=sp) for kch in range(8)]
                    S.mmg(PB[bi], fns, reads=[("lt", 0), ("lt", 1), "wrk"])
                    S.op("DVE", lambda e: e.tensor_add(lg_sb[:], pb[2][:, 0:8], br_sb[:]),
                         reads=[PB[2], "br"], writes=["lg"])
                    S.op("DVE", lambda e: e.reduce_max(m_sb[:, 0:1], lg_sb[:], axis=AX.X),
                         reads=["lg"], writes=["msb"])
                    S.op("DVE", lambda e: e.tensor_scalar(m_sb[:, 1:2], m_sb[:, 0:1], -1.0,
                                                          None, ALU.mult),
                         reads=["msb"], writes=["msb"])
                    S.op("ACT", lambda e: e.activation(lg_sb[:], lg_sb[:], AF.Exp,
                                                       bias=m_sb[:, 1:2]),
                         reads=["lg", "msb"], writes=["lg"])
                    S.op("DVE", lambda e: e.reduce_sum(m_sb[:, 0:1], lg_sb[:], axis=AX.X),
                         reads=["lg"], writes=["msb"])
                    S.op("DVE", lambda e: e.reciprocal(m_sb[:, 0:1], m_sb[:, 0:1]),
                         reads=["msb"], writes=["msb"])
                    S.op("DVE", lambda e: e.tensor_scalar(lg_sb[:], lg_sb[:], m_sb[:, 0:1],
                                                          None, ALU.mult),
                         reads=["lg", "msb"], writes=["lg"])
                    tile_i = qc * 4 + tt
                    if fuse is None:
                        S.dma("SYNC", probs_out[trow: trow + 128, :], lg_sb[:],
                              reads=["lg"])
                    else:
                        emit_probs_fused(S, fz, lg_sb, pt_bf, PT, id_bf, tile_i)

            if fuse is not None:
                S.dma("SYNC", fuse["pTown_h"], fz["probsT_own_sb"][:],
                      reads=[("pTown", t_) for t_ in range(16)], writes=["pTownH"])
                S.dma("SYNC", fuse["wprobs_h"], fz["w_probs"][:],
                      reads=[("wp", t_) for t_ in range(16)], writes=["wpH"])
            S.op("SYNC", lambda e: e.nop(), waits=S.now("QSP"), q="QSP", n=16)
            S.barrier()

        if fuse is None:
            blk = ctx.enter_context(nc.Block())
            S.replay(blk, sems)

    return nc


def emit_probs_fused(S, fuse, lg_sb, pt_bf, PT, id_bf, tile_i):
    w_probs = fuse["w_probs"]
    pTown = fuse["probsT_own_sb"]
    splf = fuse["splf"]
    splb = fuse["splb"]
    ptf = fuse["ptf"]
    S.op("DVE", lambda e, o=w_probs[:, tile_i * 8:(tile_i + 1) * 8]:
         e.tensor_copy(o, lg_sb[:]), reads=["lg"], writes=[("wp", tile_i)])
    S.op("DVE", lambda e: e.tensor_copy(splb[:, 0:8], lg_sb[:]),
         reads=["lg"], writes=["splb"])
    S.op("DVE", lambda e: e.tensor_copy(splf[:, 0:8], splb[:, 0:8]),
         reads=["splb"], writes=["splf"])
    S.op("DVE", lambda e: e.tensor_sub(splf[:, 8:16], lg_sb[:], splf[:, 0:8]),
         reads=["lg", "splf"], writes=["splf"])
    S.op("DVE", lambda e: e.tensor_copy(splb[:, 8:16], splf[:, 8:16]),
         reads=["splf"], writes=["splb"])
    S.op("DVE", lambda e: e.tensor_copy(splf[:, 0:8], splb[:, 8:16]),
         reads=["splb"], writes=["splf"])
    S.op("DVE", lambda e: e.tensor_sub(splf[:, 8:16], splf[:, 8:16], splf[:, 0:8]),
         reads=["splf"], writes=["splf"])
    S.op("DVE", lambda e: e.tensor_copy(splb[:, 16:24], splf[:, 8:16]),
         reads=["splf"], writes=["splb"])
    for j in range(3):
        pti = j % 2
        S.op("TE", lambda e, o=pt_bf[pti][0:8, 0:128], i=splb[:, j * 8:(j + 1) * 8]:
             e.transpose(o, i, id_bf[:]),
             reads=["splb", "id"], writes=[PT[pti]])
        if j == 0:
            S.op("ACT", lambda e, o=ptf[0:8, 0:128], i=pt_bf[pti][0:8, 0:128]:
                 e.activation(o, i, AF.Identity),
                 reads=[PT[pti]], writes=["ptf"])
        else:
            S.op("DVE", lambda e, o=ptf[0:8, 0:128], i=pt_bf[pti][0:8, 0:128]:
                 e.tensor_add(o, o, i),
                 reads=[PT[pti], "ptf"], writes=["ptf"])
    S.op("DVE", lambda e, o=pTown[0:8, tile_i * 128:(tile_i + 1) * 128]:
         e.tensor_copy(o, ptf[0:8, 0:128]),
         reads=["ptf"], writes=[("pTown", tile_i)])


FCH = [128] * 21 + [42]
FOF = [sum(FCH[:i]) for i in range(22)]


def hostB(inp, core, probs_all):
    if probs_all is None:
        probs_all = np.zeros((16384, 8), np.float32)
    sel64T = np.zeros((64, 8), np.float32)
    sel8 = np.zeros((8, 64), np.float32)
    for r in range(64):
        sel64T[r, r % 8] = 1.0
        sel8[r % 8, r] = 1.0
    pT = probs_all.T.reshape(8, 8, 2048)
    segs = [(s + core) % 8 for s in range(8)]
    pr = np.zeros((64, 2048), np.float32)
    for sp in range(8):
        for e in range(8):
            pr[sp * 8 + e] = pT[e, segs[sp]]
    bu = np.zeros((128, 4 * 22), np.float32)
    b_up = _f32(inp["L0_ffn_b_up"])
    for gg in range(4):
        pad = np.zeros(2816, np.float32)
        pad[:2730] = b_up[gg]
        bu[:, gg * 22:(gg + 1) * 22] = pad.reshape(22, 128).T
    return {
        "x1": np.zeros((NT, DM), np.float32),
        "probs_own": np.zeros((NT, 8), np.float32),
        "probsT_r": pr,
        "sel64T": sel64T, "sel8": sel8,
        "wup": _f32(inp["L0_ffn_W_up"]),
        "bu": bu,
        "wdn": _f32(inp["L0_ffn_W_down"]),
        "bd_row": _f32(inp["L0_ffn_b_down"]).reshape(1, 8 * 1024),
        "ones_row": np.ones((1, 128), np.float32),
        "wkvT": _kc(_f32(inp["L1_attn_Wkv_down"]).T),
        "identB": np.eye(128, dtype=np.float32),
        "constsB": np.tile(np.array(
            [0.5, 1e-12, 1e-5, 1.0 / 1024, 2048.0, 1.0, -1.0, 1e-6], np.float32),
            (128, 1)),
    }


def build_B(fuse=None):
    nc = bass.Bass("TRN2", target_bir_lowering=False) if fuse is None else fuse["nc"]
    pref = "" if fuse is None else "b_"
    g = lambda n, s, dt=F32: nc.dram_tensor(n, s, dt, kind="ExternalInput").ap()
    if fuse is None:
        x1_in = g("x1", [NT, DM])
        probs_own = g("probs_own", [NT, 8])
        probsT_r = g("probsT_r", [64, 2048])
    else:
        x1_in = fuse["x1_hbm"]
        probs_own = None
        probsT_r = fuse["pts_hbm"]
    sel64T_in = g("sel64T", [64, 8])
    sel8_in = g("sel8", [8, 64])
    wup_in = fuse["wup_in"] if fuse else g("wup", [4, 1024, 2730])
    bu_in = g("bu", [128, 88])
    wdn_in = fuse["wdn_in"] if fuse else g("wdn", [8, 2730, 1024])
    bd_in = g("bd_row", [1, 8192])
    ones_in = g("ones_row", [1, 128])
    wkvT_in = g("wkvT", [128, 8 * 64])
    id_in = g("identB", [128, 128])
    cst_in = g("constsB", [128, 8])

    if fuse is None:
        x2_out = nc.dram_tensor("x2", [NT, DM], F32, kind="ExternalOutput").ap()
        lat_out = nc.dram_tensor("latent", [64, NT], F32, kind="ExternalOutput").ap()
        lo_dbg = nc.dram_tensor("lo_dbg", [64, 1], F32, kind="ExternalOutput").ap()
    else:
        x2_out = fuse["x2_hbm"]
        lat_out = fuse["lat_own_hbm"]
        lo_dbg = nc.dram_tensor("lo_dbg", [64, 1], F32, kind="Internal").ap()

    lnT_hbm = nc.dram_tensor("lnT_hbm", [128, 8 * NT], BF16, kind="Internal").ap()
    yg_hbm = nc.dram_tensor("yg_hbm", [4 * NT, DM], F32, kind="Internal").ap()

    with ExitStack() as ctx:
        sb = lambda n, s, dt: ctx.enter_context(nc.sbuf_tensor(pref + n, s, dt))
        psm = lambda n, s, dt=F32: ctx.enter_context(nc.psum_tensor(n, s, dt))

        cst = sb("cst", [128, 8], F32)
        id_bf = sb("idbf", [128, 128], BF16)
        ones_bf = sb("onesbf", [1, 128], BF16)
        bd_bf = sb("bdbf", [1, 8192], BF16)
        bu_sb = sb("busb", [128, 88], F32)
        mst = sb("mst", [128, 16], F32)
        w_tok = sb("w_tok", [128, 16 * 8], F32)
        wkv_bf = sb("wkvbf", [128, 8 * 64], BF16)

        if fuse is None:
            pb = [psm(f"pb{i}", [128, 512]) for i in range(6)]
            pt_bf = [psm(f"ptb{i}", [128, 512], BF16) for i in range(2)]
            S = Sched()
            sems = make_sems(nc, ctx)
        else:
            pb, pt_bf = fuse["pb"], fuse["pt_bf"]
            S, sems = fuse["S"], fuse["sems"]
        PB = [f"pb{i}" for i in range(6)]
        PT = ["pt0", "pt1"]

        c2 = cst[:, 2:3]
        c3 = cst[:, 3:4]

        S.dma("SYNC", cst[:], cst_in, writes=["cst"])
        S.dma("SYNC", bu_sb[:], bu_in, writes=["bu"])

        with ExitStack() as pA:
            sbA = lambda n, s, dt: pA.enter_context(nc.sbuf_tensor(pref + n, s, dt))
            stage = sbA("stage", [128, 2730], F32)
            xt = sbA("xt", [128, 2048], F32)
            sqs = sbA("sqs", [128, 2048], F32)
            lnx = sbA("lnx", [128, 2048], BF16)
            lnTfull = sbA("lnTfull", [128, 8 * NT], BF16)
            pts = sbA("pts", [64, 2048], F32)
            lo = sbA("lo", [64, 1], F32)
            hi = sbA("hi", [64, 1], F32)
            th = sbA("th", [64, 1], F32)
            cnt64 = sbA("cnt64", [64, 2048], F32)
            c64 = sbA("c64", [64, 2], F32)
            mask8 = sbA("mask8", [8, 2], BF16)
            maskR = sbA("maskR", [64, 2], F32)
            wsel = sbA("wsel", [8, 2048], BF16)
            mtmp = sbA("mtmp", [128, 8], F32)
            sel64T_sb = sbA("sel64Ts", [64, 8], F32)
            sel8_bf = sbA("sel8b", [8, 64], BF16)

            def conv_small(src_ap, dst_ap, rows, cols, key):
                S.dma("SYNC", stage[0:rows, 0:cols], src_ap, writes=["stage"])
                S.op("DVE", lambda e, d=dst_ap, s=stage[0:rows, 0:cols]:
                     e.tensor_copy(d, s), reads=["stage"], writes=[key])

            S.dma("SYNC", sel64T_sb[:], sel64T_in, writes=["sel64T"])
            S.dma("SYNC", pts[:], probsT_r, reads=["agp"], writes=["pts"])
            conv_small(id_in, id_bf[:], 128, 128, "id")
            conv_small(ones_in, ones_bf[:], 1, 128, "ones")
            conv_small(sel8_in, sel8_bf[:], 8, 64, "sel8")
            conv_small(wkvT_in, wkv_bf[:], 128, 512, "wkv")
            for off in range(0, 8192, 2048):
                conv_small(bd_in[:, off:off + 2048], bd_bf[:, off:off + 2048],
                           1, 2048, "bd")

            S.op("DVE", lambda e: e.memset(lo[:], 0.0), writes=["lo"])
            S.op("DVE", lambda e: e.memset(hi[:], 1.0), writes=["hi"])

            def emit_bisect_iter():
                S.op("DVE", lambda e: e.tensor_add(th[:], lo[:], hi[:]),
                     reads=["lo", "hi"], writes=["th"])
                S.op("DVE", lambda e: e.tensor_scalar(th[:], th[:], 0.5, None, ALU.mult),
                     reads=["th"], writes=["th"])
                S.op("DVE", lambda e: e.tensor_scalar(cnt64[:], pts[:], th[:], None, ALU.is_ge),
                     reads=["pts", "th"], writes=["cnt"])
                S.op("DVE", lambda e: e.reduce_sum(c64[:, 0:1], cnt64[:], axis=AX.X),
                     reads=["cnt"], writes=["c64"])
                S.mmg(PB[4], [lambda e: e.matmul(pb[4][0:8, 0:1], sel64T_sb[:], c64[:, 0:1],
                                                 start=True, stop=True)],
                      reads=["c64", "sel64T"])
                S.op("DVE", lambda e: e.tensor_scalar(mask8[:, 0:1], pb[4][0:8, 0:1],
                                                      2048.0, None, ALU.is_ge),
                     reads=[PB[4]], writes=["mask8"])
                S.op("DVE", lambda e: e.tensor_scalar(mask8[:, 1:2], pb[4][0:8, 0:1],
                                                      2048.0, None, ALU.is_lt),
                     reads=[PB[4]], writes=["mask8"])
                S.mmg(PB[5], [lambda e: e.matmul(pb[5][0:64, 0:2], sel8_bf[:], mask8[:],
                                                 start=True, stop=True)],
                      reads=["mask8", "sel8"])
                S.op("ACT", lambda e: e.activation(maskR[:], pb[5][0:64, 0:2], AF.Identity),
                     reads=[PB[5]], writes=["maskR"])
                S.op("DVE", lambda e: e.tensor_mul(c64[:, 0:1], th[:], maskR[:, 0:1]),
                     reads=["maskR", "th"], writes=["c64"])
                S.op("DVE", lambda e: e.scalar_tensor_tensor(
                    lo[:], lo[:], maskR[:, 1:2], c64[:, 0:1], ALU.mult, ALU.add),
                     reads=["maskR", "lo", "c64"], writes=["lo"])
                S.op("DVE", lambda e: e.tensor_mul(c64[:, 1:2], th[:], maskR[:, 1:2]),
                     reads=["maskR", "th"], writes=["c64b"])
                S.op("DVE", lambda e: e.scalar_tensor_tensor(
                    hi[:], hi[:], maskR[:, 0:1], c64[:, 1:2], ALU.mult, ALU.add),
                     reads=["maskR", "hi", "c64b"], writes=["hi"])

            bis_left = [40]

            def run_bisect(k):
                n = min(k, bis_left[0])
                for _ in range(n):
                    emit_bisect_iter()
                bis_left[0] -= n

            for tt in range(16):
                par = tt % 2
                S.dma("SYNC", xt[:, par * 1024:(par + 1) * 1024],
                      x1_in[tt * 128:(tt + 1) * 128, :], writes=[("xt", par)])
                emit_ln(S, xt[:, par * 1024:(par + 1) * 1024],
                        lnx[:, par * 1024:(par + 1) * 1024],
                        mst[:, par * 8:(par + 1) * 8], sqs[:, par * 1024:(par + 1) * 1024],
                        c3, c2, ("xt", par), ("lnx", par),
                        rk_extra=["cst"], tag=par)
                for half in range(2):
                    pti = half
                    fns = []
                    for q4 in range(4):
                        dch = half * 4 + q4
                        fns.append(lambda e, o=pt_bf[pti][:, q4 * 128:(q4 + 1) * 128],
                                   i=lnx[:, par * 1024 + dch * 128: par * 1024 + (dch + 1) * 128]:
                                   e.transpose(o, i, id_bf[:]))
                    tok = S.op("TE", fns[0], reads=[("lnx", par), "id"], writes=[PT[pti]])
                    for f in fns[1:]:
                        tok = S.op("TE", f)
                    S.lw[PT[pti]] = tok

                    def evb(e, pt=pt_bf[pti], half=half, tt=tt):
                        src_ = pt[:, 0:512].rearrange("p (a b) -> p a b", a=4)
                        d = lnTfull[:].rearrange("p (a b) -> p a b", a=8, b=NT)[
                            :, half * 4: half * 4 + 4, tt * 128: tt * 128 + 128]
                        return e.tensor_copy(d, src_)
                    S.op("DVE", evb, reads=[PT[pti]], writes=[("lnf", tt, half)])
                run_bisect(3)
            for kch in range(8):
                S.dma("SYNC", lnT_hbm[:, kch * NT: (kch + 1) * NT],
                      lnTfull[:, kch * NT: (kch + 1) * NT],
                      reads=[("lnf", t_, h_) for t_ in range(16) for h_ in range(2)],
                      writes=[("lnhbm", 0, kch)])
            run_bisect(40)
            S.dma("SYNC", lo_dbg, lo[:], reads=["lo"])
            if fuse is not None:
                b_wpr = pA.enter_context(nc.sbuf_tensor("b_wpr", [128, 128], F32))
                S.dma("SYNC", b_wpr[:], fuse["wprobs_h"],
                      reads=["wpH"], writes=["bwpr"])
            if fuse is None:
                own_src = pts[0:8, :]
                own_keys = ["pts"]
            else:
                b_ptown = pA.enter_context(nc.sbuf_tensor("b_ptown", [8, 2048], F32))
                S.dma("SYNC", b_ptown[:], fuse["pTown_h"],
                      reads=["pTownH"], writes=["bptown"])
                own_src = b_ptown[0:8, :]
                own_keys = ["bptown"]
            S.op("DVE", lambda e, i=own_src: e.tensor_scalar(wsel[:], i, lo[0:8, 0:1],
                                                             None, ALU.is_ge),
                 reads=own_keys + ["lo"], writes=["wsel"])
            for tt in range(16):
                S.op("TE", lambda e, o=pt_bf[tt % 2][0:128, 0:8],
                     i=wsel[:, tt * 128:(tt + 1) * 128]:
                     e.transpose(o, i, id_bf[0:8, 0:8]),
                     reads=["wsel", "id"], writes=[PT[tt % 2]])
                S.op("ACT", lambda e, o=mtmp[:], i=pt_bf[tt % 2][0:128, 0:8]:
                     e.activation(o, i, AF.Identity),
                     reads=[PT[tt % 2]], writes=["mtmp"])
                if fuse is None:
                    S.dma("SYNC", xt[:, 0:8], probs_own[tt * 128:(tt + 1) * 128, :],
                          writes=[("xt", 0)])
                    S.op("DVE", lambda e, o=w_tok[:, tt * 8:(tt + 1) * 8], a=mtmp[:],
                         tt=tt: e.tensor_mul(o, a, xt[:, 0:8]),
                         reads=["mtmp", ("xt", 0)], writes=[("w_tok", tt)])
                else:
                    S.op("DVE", lambda e, o=w_tok[:, tt * 8:(tt + 1) * 8], a=mtmp[:],
                         b=b_wpr[:, tt * 8:(tt + 1) * 8]:
                         e.tensor_mul(o, a, b),
                         reads=["mtmp", "bwpr"], writes=[("w_tok", tt)])

            S.barrier()

        lnkeys = [("lnhbm", 0, kch) for kch in range(8)]
        with ExitStack() as pB:
            sbB = lambda n, s, dt: pB.enter_context(nc.sbuf_tensor(pref + n, s, dt))
            wup_bf = sbB("wupbf", [128, 8 * 2730], BF16)
            wd_bf = {ee: sbB(f"wdbf{ee}", [128, 22 * 1024], BF16) for ee in range(2)}
            if fuse is None:
                stg = sbB("stg", [128, 2 * 910], F32)
                stg2 = sbB("stg2", [128, 2 * 512], F32)
            midT = sbB("midT", [128, 22 * 512], BF16)
            lnT_sb = sbB("lnTsb", [128, 2 * 8 * 512], BF16)
            y_sb = sbB("ysb", [128, 2048], F32)
            pump2 = []
            if fuse is not None:
                pstg2 = sbB("pstgB", [128, 2048], F32)
                pbst2 = sbB("pbstB", [128, 2048], BF16)
                gupi, dni = fuse["gupT_in"], fuse["dnT_in"]
                gupbh, dnbh = fuse["gupb_h"], fuse["dnb_h"]
                ch_g = []
                for e9 in range(8):
                    for mjp9 in range(11):
                        r0 = (e9 * 11 + mjp9) * 128
                        for c49 in range(4):
                            ch_g.append((gupi[e9, mjp9, :, c49 * 1024:(c49 + 1) * 1024],
                                         gupbh[r0:r0 + 128, c49 * 1024:(c49 + 1) * 1024],
                                         128, 1024))
                ch_d = []
                for e9 in range(8):
                    for mj9 in range(22):
                        ch_d.append((dni[e9, mj9 * 128:(mj9 + 1) * 128, :],
                                     dnbh[e9 * 2816 + mj9 * 128:
                                          e9 * 2816 + (mj9 + 1) * 128, :],
                                     128, 1024))
                pump2 = make_pump_jobs(S, pstg2, pbst2, ch_g, "pB", "gupbH")
                pump2 += make_pump_jobs(S, pstg2, pbst2, ch_d, "pB", "dnbH")

            def wup_jobs(gg, spread):
                jobs = []
                if fuse is not None:
                    wupbh_ = fuse["wupb_h"]
                    for kch in range(8):
                        for c0_, c1_ in ((0, 910), (910, 1820), (1820, 2730)):
                            def j1(kch=kch, c0_=c0_, c1_=c1_, gg=gg):
                                S.dma("SYNC",
                                      wup_bf[:, kch * 2730 + c0_: kch * 2730 + c1_],
                                      wupbh_[gg * 1024 + kch * 128:
                                             gg * 1024 + (kch + 1) * 128, c0_:c1_],
                                      reads=["wupbH"], writes=["wup"])
                            jobs.append(j1)
                    return jobs
                pi = [0]
                for kch in range(8):
                    for c0_, c1_ in ((0, 910), (910, 1820), (1820, 2730)):
                        par = pi[0] % 2
                        pi[0] += 1
                        o0 = par * 910

                        def j1(kch=kch, c0_=c0_, c1_=c1_, par=par, o0=o0, gg=gg):
                            S.dma("SYNC", stg[:, o0:o0 + c1_ - c0_],
                                  wup_in[gg, kch * 128:(kch + 1) * 128, c0_:c1_],
                                  writes=[("stg", par)])

                        def j2(kch=kch, c0_=c0_, c1_=c1_, par=par, o0=o0):
                            eng = "ACT" if (spread or par == 1) else "DVE"
                            if eng == "DVE":
                                S.op("DVE", lambda e, d=wup_bf[:, kch * 2730 + c0_: kch * 2730 + c1_],
                                     s=stg[:, o0:o0 + c1_ - c0_]: e.tensor_copy(d, s),
                                     reads=[("stg", par)], writes=["wup"])
                            else:
                                S.op("ACT", lambda e, d=wup_bf[:, kch * 2730 + c0_: kch * 2730 + c1_],
                                     s=stg[:, o0:o0 + c1_ - c0_]: e.activation(d, s, AF.Identity),
                                     reads=[("stg", par)], writes=["wup"])
                        jobs += [j1, j2]
                return jobs

            for gg in range(4):
                if gg == 0:
                    for j in wup_jobs(0, spread=False):
                        j()
                for ee in range(2):
                    e_ = gg * 2 + ee
                    for mj in range(22):
                        mm = FCH[mj]
                        if fuse is not None:
                            S.dma("ACT", wd_bf[ee][0:mm, mj * 1024:(mj + 1) * 1024],
                                  fuse["wdb_h"][e_ * 2730 + FOF[mj]:
                                                e_ * 2730 + FOF[mj] + mm, :],
                                  reads=["wdbH"], writes=[("wd", ee)])
                            continue
                        for pc in range(2):
                            par = (mj * 2 + pc) % 2
                            o0 = par * 512
                            S.dma("ACT", stg2[0:mm, o0:o0 + 512],
                                  wdn_in[e_, FOF[mj]:FOF[mj] + mm, pc * 512:(pc + 1) * 512],
                                  writes=[("stg2", par)])
                            if par == 0:
                                S.op("DVE", lambda e, d=wd_bf[ee][0:mm, mj * 1024 + pc * 512: mj * 1024 + (pc + 1) * 512],
                                     s=stg2[0:mm, o0:o0 + 512]: e.tensor_copy(d, s),
                                     reads=[("stg2", par)], writes=[("wd", ee)])
                            else:
                                S.op("ACT", lambda e, d=wd_bf[ee][0:mm, mj * 1024 + pc * 512: mj * 1024 + (pc + 1) * 512],
                                     s=stg2[0:mm, o0:o0 + 512]: e.activation(d, s, AF.Identity),
                                     reads=[("stg2", par)], writes=[("wd", ee)])
                pend = wup_jobs(gg + 1, spread=True) if gg < 3 else []
                for tc in range(4):
                    lpar = tc % 2
                    lz = lpar * 4096
                    if gg == 0 and tc == 0:
                        for kch in range(8):
                            S.dma("SYNC", lnT_sb[:, lz + kch * 512: lz + (kch + 1) * 512],
                                  lnT_hbm[:, kch * NT + tc * 512: kch * NT + tc * 512 + 512],
                                  reads=lnkeys, writes=[("lnT_sb", lpar)])
                    for mj in range(22):
                        drain_jobs(pump2, 2)
                        mm = FCH[mj]
                        bi = mj % 2
                        fns = [lambda e, o=pb[bi][0:mm, 0:512],
                               l=wup_bf[:, kch * 2730 + FOF[mj]: kch * 2730 + FOF[mj] + mm],
                               r=lnT_sb[:, lz + kch * 512: lz + (kch + 1) * 512],
                               st=(kch == 0), sp=(kch == 7):
                               e.matmul(o, l, r, start=st, stop=sp) for kch in range(8)]
                        S.mmg(PB[bi], fns, reads=["wup", ("lnT_sb", lpar)])
                        S.op("ACT", lambda e, o=midT[0:mm, mj * 512:(mj + 1) * 512],
                             i=pb[bi][0:mm, 0:512], b=bu_sb[0:mm, gg * 22 + mj: gg * 22 + mj + 1]:
                             e.activation(o, i, AF.Silu, bias=b),
                             reads=[PB[bi], "bu"], writes=[("midT", mj)])
                    if not (gg == 3 and tc == 3):
                        drain_jobs(pump2, 4)
                        ntc = (tc + 1) % 4
                        nlz = (ntc % 2) * 4096
                        for kch in range(8):
                            S.dma("SYNC", lnT_sb[:, nlz + kch * 512: nlz + (kch + 1) * 512],
                                  lnT_hbm[:, kch * NT + ntc * 512: kch * NT + ntc * 512 + 512],
                                  reads=lnkeys, writes=[("lnT_sb", ntc % 2)])
                    for tt in range(4):
                        gt = tc * 4 + tt
                        yp = tt % 2
                        for dmc in range(2):
                            for ee in range(2):
                                drain_jobs(pump2, 3)
                                if tc == 3:
                                    for _ in range(3):
                                        if pend:
                                            pend.pop(0)()
                                e_ = gg * 2 + ee
                                bi = 2 + ee
                                fns = []
                                for mj in range(22):
                                    mm = FCH[mj]
                                    fns.append(lambda e, o=pb[bi][:, 0:512],
                                               l=midT[0:mm, mj * 512 + tt * 128: mj * 512 + (tt + 1) * 128],
                                               r=wd_bf[ee][0:mm, mj * 1024 + dmc * 512: mj * 1024 + dmc * 512 + 512],
                                               st=(mj == 0): e.matmul(o, l, r, start=st, stop=False))
                                fns.append(lambda e, o=pb[bi][:, 0:512],
                                           l=ones_bf[:],
                                           r=bd_bf[:, e_ * 1024 + dmc * 512: e_ * 1024 + dmc * 512 + 512]:
                                           e.matmul(o, l, r, start=False, stop=True))
                                S.mmg(PB[bi], fns,
                                      reads=[("midT", mj_) for mj_ in range(22)] +
                                            [("wd", ee), "bd", "ones"])
                                wap = w_tok[:, gt * 8 + e_: gt * 8 + e_ + 1]
                                ysl = y_sb[:, (yp * 2 + dmc) * 512: (yp * 2 + dmc + 1) * 512]
                                if ee == 0:
                                    S.op("DVE", lambda e, o=ysl, i=pb[bi][:, 0:512], w=wap:
                                         e.tensor_scalar(o, i, w, None, ALU.mult),
                                         reads=[PB[bi], ("w_tok", gt)],
                                         writes=[("y", yp, dmc)])
                                else:
                                    S.op("DVE", lambda e, o=ysl, i=pb[bi][:, 0:512], w=wap:
                                         e.scalar_tensor_tensor(o, i, w, o, ALU.mult, ALU.add),
                                         reads=[PB[bi], ("w_tok", gt), ("y", yp, dmc)],
                                         writes=[("y", yp, dmc)])
                        S.dma("SYNC", yg_hbm[gg * NT + gt * 128: gg * NT + (gt + 1) * 128, :],
                              y_sb[:, yp * 1024:(yp + 1) * 1024],
                              reads=[("y", yp, 0), ("y", yp, 1)],
                              writes=[("yg", gg, gt)])
                    if tc == 3:
                        while pend:
                            pend.pop(0)()
            drain_jobs(pump2, len(pump2))
            S.barrier()

        with ExitStack() as pC:
            sbC = lambda n, s, dt: pC.enter_context(nc.sbuf_tensor(pref + n, s, dt))
            xt2 = sbC("xt2", [128, 2 * 1024], F32)
            yt = sbC("yt", [128, 2 * 4 * 1024], F32)
            sqs2 = sbC("sqs2", [128, 2 * 1024], F32)
            lnx2 = sbC("lnx2", [128, 2 * 1024], BF16)
            lnT2 = sbC("lnT2", [128, 2 * 8 * 128], BF16)
            mstC = sbC("mstC", [128, 16], F32)
            latT = (sbC("latT", [64, 2048], F32) if fuse is None
                    else sbC("latTtm", [128, 16 * 64], F32))

            ygkeys = [("yg", gg, gt) for gg in range(4) for gt in range(16)]
            for tt in range(16):
                p_ = tt % 2
                xz = p_ * 1024
                S.dma("SYNC", xt2[:, xz:xz + 1024], x1_in[tt * 128:(tt + 1) * 128, :],
                      writes=[("xt2", p_)])
                for gg in range(4):
                    S.dma("SYNC", yt[:, p_ * 4096 + gg * 1024: p_ * 4096 + (gg + 1) * 1024],
                          yg_hbm[gg * NT + tt * 128: gg * NT + (tt + 1) * 128, :],
                          reads=ygkeys, writes=[("ytl", p_, gg)])
                for gg in range(4):
                    S.op("DVE", lambda e, a=xt2[:, xz:xz + 1024],
                         b=yt[:, p_ * 4096 + gg * 1024: p_ * 4096 + (gg + 1) * 1024]:
                         e.tensor_add(a, a, b),
                         reads=[("ytl", p_, gg), ("xt2", p_)], writes=[("xt2", p_)])
                S.dma("SYNC", x2_out[tt * 128:(tt + 1) * 128, :], xt2[:, xz:xz + 1024],
                      reads=[("xt2", p_)])
                emit_ln(S, xt2[:, xz:xz + 1024], lnx2[:, xz:xz + 1024],
                        mstC[:, p_ * 8:(p_ + 1) * 8], sqs2[:, xz:xz + 1024],
                        c3, c2, ("xt2", p_), ("lnx2", p_),
                        rk_extra=["cst"], tag=p_)
                for half in range(2):
                    pti = half
                    fns = []
                    for q4 in range(4):
                        dch = half * 4 + q4
                        fns.append(lambda e, o=pt_bf[pti][:, q4 * 128:(q4 + 1) * 128],
                                   i=lnx2[:, xz + dch * 128: xz + (dch + 1) * 128]:
                                   e.transpose(o, i, id_bf[:]))
                    tok = S.op("TE", fns[0], reads=[("lnx2", p_), "id"], writes=[PT[pti]])
                    for f in fns[1:]:
                        tok = S.op("TE", f)
                    S.lw[PT[pti]] = tok
                    S.op("DVE", lambda e, pt=pt_bf[pti],
                         d=lnT2[:, p_ * 1024 + half * 512: p_ * 1024 + half * 512 + 512]:
                         e.tensor_copy(d, pt[:, 0:512]),
                         reads=[PT[pti]], writes=[("lnT2", p_, half)])
                bi = 4 + p_
                if fuse is None:
                    fns = [lambda e, o=pb[bi][0:64, 0:128],
                           l=wkv_bf[:, kch * 64:(kch + 1) * 64],
                           r=lnT2[:, p_ * 1024 + kch * 128: p_ * 1024 + (kch + 1) * 128],
                           st=(kch == 0), sp=(kch == 7):
                           e.matmul(o, l, r, start=st, stop=sp) for kch in range(8)]
                    S.mmg(PB[bi], fns, reads=[("lnT2", p_, 0), ("lnT2", p_, 1), "wkv"])
                    S.op("DVE", lambda e, o=latT[:, tt * 128:(tt + 1) * 128],
                         i=pb[bi][0:64, 0:128]: e.tensor_copy(o, i),
                         reads=[PB[bi]], writes=[("latT", tt)])
                else:
                    fns = [lambda e, o=pb[bi][:, 0:64],
                           l=lnT2[:, p_ * 1024 + kch * 128: p_ * 1024 + (kch + 1) * 128],
                           r=wkv_bf[:, kch * 64:(kch + 1) * 64],
                           st=(kch == 0), sp=(kch == 7):
                           e.matmul(o, l, r, start=st, stop=sp) for kch in range(8)]
                    S.mmg(PB[bi], fns, reads=[("lnT2", p_, 0), ("lnT2", p_, 1), "wkv"])
                    S.op("DVE", lambda e, o=latT[:, tt * 64:(tt + 1) * 64],
                         i=pb[bi][:, 0:64]: e.tensor_copy(o, i),
                         reads=[PB[bi]], writes=[("latT", tt)])
            if fuse is None:
                for i0 in range(0, 2048, 1024):
                    S.dma("SYNC", lat_out[:, i0:i0 + 1024], latT[:, i0:i0 + 1024],
                          reads=[("latT", t_) for t_ in range(16)])
            else:
                for tt in range(16):
                    S.dma("SYNC", lat_out[tt * 128:(tt + 1) * 128, :],
                          latT[:, tt * 64:(tt + 1) * 64],
                          reads=[("latT", tt)], writes=[("lato", tt)])
            S.op("SYNC", lambda e: e.nop(), waits=S.now("QSP", "QACT"), q="QSP", n=16)
            S.barrier()

        if fuse is None:
            blk = ctx.enter_context(nc.Block())
            S.replay(blk, sems)

    return nc




def _wq_headmajor(wqT):
    kc = _kc(wqT)
    v = kc.reshape(128, 8, 8, 256)
    return np.ascontiguousarray(v.transpose(0, 2, 1, 3).reshape(128, 16384))


def _gup_perm(gu):
    guT = gu.transpose(0, 2, 1)
    out = np.empty((8, 22, 128, 2048), np.float32)
    for mj in range(22):
        for gi in range(2):
            mjj = mj + 22 * gi
            blk = guT[:, :, mjj * 128:(mjj + 1) * 128]
            blk = blk.reshape(8, 8, 128, 128)
            out[:, mj, :, gi * 1024:(gi + 1) * 1024] = \
                blk.transpose(0, 2, 1, 3).reshape(8, 128, 1024)
    out = out.reshape(8, 11, 2, 128, 2048).transpose(0, 1, 3, 2, 4)
    return np.ascontiguousarray(out.reshape(8, 11, 128, 4096))

def hostCD(inp, core, latT_full):
    if latT_full is None:
        latT_full = np.zeros((64, 4096), np.float32)
    return {
        "x2": np.zeros((NT, DM), np.float32),
        "latT": np.ascontiguousarray(latT_full),
        "wqTl1": _wq_headmajor(_f32(inp["L1_attn_Wq"]).T),
        "wkupT": np.ascontiguousarray(_f32(inp["L1_attn_Wk_up"]).T),
        "wvupT": np.ascontiguousarray(_f32(inp["L1_attn_Wv_up"]).T),
        "woTl1": _kc(_f32(inp["L1_attn_Wo"]).T),
        "wnorm_bc": np.broadcast_to(_f32(inp["L1_ffn_norm_weight"]), (128, 1024)).copy(),
        "rwT": _kc(_f32(inp["L1_ffn_router_weight"]).T),
        "gupT": _gup_perm(_f32(inp["L1_ffn_gate_up_weight"])),
        "dnT": np.ascontiguousarray(_f32(inp["L1_ffn_down_weight"]).transpose(0, 2, 1)),
        "alpha_bc": np.broadcast_to(_f32(inp["L1_ffn_residual_alpha"]).reshape(1, 1),
                                    (128, 1)).copy(),
        "ones1": np.ones((1, 128), np.float32),
        "onescol": np.ones((128, 1), np.float32),
        "identC": np.eye(128, dtype=np.float32),
        "constsC": np.tile(np.array(
            [0.0625, 1e-12, 1e-5, 1.0 / 1024, 1e-6, 1.0, -1.0, -1e9], np.float32),
            (128, 1)),
    }


def build_CD(fuse=None):
    nc = bass.Bass("TRN2", target_bir_lowering=False) if fuse is None else fuse["nc"]
    pref = "" if fuse is None else "c_"
    g = lambda n, s, dt=F32: nc.dram_tensor(n, s, dt, kind="ExternalInput").ap()
    if fuse is None:
        x2_in = g("x2", [NT, DM])
        latT_in = g("latT", [64, 4096])
    else:
        x2_in = fuse["x2_hbm"]
        latT_in = None
    wqT_in = g("wqTl1", [128, 8 * 2048])
    wkupT_in = g("wkupT", [64, 2048])
    wvupT_in = g("wvupT", [64, 2048])
    woT_in = g("woTl1", [128, 16 * 1024])
    wnorm_in = g("wnorm_bc", [128, 1024])
    rwT_in = g("rwT", [128, 64])
    gupT_in = fuse["gupT_in"] if fuse else g("gupT", [8, 11, 128, 4096])
    dnT_in = fuse["dnT_in"] if fuse else g("dnT", [8, 2816, 1024])
    alpha_in = g("alpha_bc", [128, 1])
    ones1_in = g("ones1", [1, 128])
    onescol_in = g("onescol", [128, 1])
    id_in = g("identC", [128, 128])
    cst_in = g("constsC", [128, 8])

    out_f = nc.dram_tensor("out", [NT, DM], F32, kind="ExternalOutput").ap()
    ye_hbm = nc.dram_tensor("ye_hbm", [8 * NT, DM], BF16, kind="Internal").ap()
    x3_hbm = nc.dram_tensor("x3_hbm", [NT, DM], F32, kind="Internal" if fuse is not None else "ExternalOutput").ap()

    with ExitStack() as ctx:
        sb = lambda n, s, dt: ctx.enter_context(nc.sbuf_tensor(pref + n, s, dt))
        psm = lambda n, s, dt=F32: ctx.enter_context(nc.psum_tensor(n, s, dt))

        cst = sb("cst", [128, 8], F32)
        id_bf = sb("idbf", [128, 128], BF16)
        ones1_bf = sb("ones1bf", [1, 128], BF16)
        onescol_bf = sb("onescolbf", [128, 1], BF16)
        ones1_f = sb("ones1f", [1, 128], F32)
        mst = sb("mst", [128, 8], F32)
        alpha_sb = sb("alphasb", [128, 1], F32)
        w2_tok = sb("w2tok", [128, 16 * 8], F32)

        if fuse is None:
            pb = [psm(f"pb{i}", [128, 512]) for i in range(6)]
            pt_bf = [psm(f"ptb{i}", [128, 512], BF16) for i in range(2)]
            S = Sched()
            sems = make_sems(nc, ctx)
        else:
            pb, pt_bf = fuse["pb"], fuse["pt_bf"]
            S, sems = fuse["S"], fuse["sems"]
        PB = [f"pb{i}" for i in range(6)]
        PT = ["pt0", "pt1"]
        c1 = cst[:, 1:2]
        c2 = cst[:, 2:3]
        c3 = cst[:, 3:4]
        c4 = cst[:, 4:5]

        S.dma("SYNC", cst[:], cst_in, writes=["cst"])
        S.dma("SYNC", alpha_sb[:], alpha_in, writes=["alpha"])
        S.op("DVE", lambda e: e.tensor_scalar(alpha_sb[:], alpha_sb[:], 2.0, None, ALU.min),
             reads=["alpha"], writes=["alpha"])
        S.op("DVE", lambda e: e.tensor_scalar(alpha_sb[:], alpha_sb[:], -0.5, None, ALU.max),
             reads=["alpha"], writes=["alpha"])

        with ExitStack() as pC:
            sbC = lambda n, s, dt: pC.enter_context(nc.sbuf_tensor(pref + n, s, dt))
            stage = sbC("stage", [128, 1024], F32)
            stg3 = sbC("stg3", [128, 1024], F32)
            lnx2T = sbC("lnx2T", [128, 8 * 2048], BF16)
            lat_bf = sbC("latbf", [64, 4096], BF16)
            wkup_bf = sbC("wkupbf", [64, 2048], BF16)
            wvup_bf = sbC("wvupbf", [64, 2048], BF16)
            woh_bf = sbC("wohbf", [128, 2 * 2048], BF16)
            wqh_bf = sbC("wqhbf", [128, 2 * 2048], BF16)
            qT_h = sbC("qTh", [128, 2 * 2048], BF16)
            kT_h = sbC("kTh", [128, 2 * 4096], BF16)
            v_h = sbC("vh", [128, 32 * 256], BF16)
            outT_h = sbC("outTh", [128, 2 * 2048], BF16)
            y_acc = sbC("yaccC", [128, 16 * 1024], F32)
            pT = [sbC(f"pTc{i}", [128, 512], BF16) for i in range(2)]
            dnr = sbC("dnrc", [1, 512], F32)
            rb_f = sbC("rbf", [128, 512], F32)
            xt = sbC("xtc", [128, 1024], F32)
            sqs = sbC("sqsc", [128, 1024], F32)
            lnx = sbC("lnxc", [128, 1024], BF16)
            y_sb = sbC("ysbc", [128, 1024], F32)

            def conv_in(src_ap, dst_ap, rows, width, key, eng="SYNC"):
                for off in range(0, width, 1024):
                    wdt = min(1024, width - off)
                    S.dma(eng, stage[0:rows, 0:wdt], src_ap[0:rows, off:off + wdt],
                          writes=["stage"])
                    S.op("DVE", lambda e, d=dst_ap[0:rows, off:off + wdt],
                         s=stage[0:rows, 0:wdt]: e.tensor_copy(d, s),
                         reads=["stage"], writes=[key])

            conv_in(id_in, id_bf[:], 128, 128, "id")
            conv_in(ones1_in, ones1_bf[:], 1, 128, "ones1")
            conv_in(onescol_in, onescol_bf[:], 128, 1, "onescol")
            S.dma("SYNC", ones1_f[:], ones1_in, writes=["ones1f"])
            if fuse is None:
                conv_in(latT_in, lat_bf[:], 64, 4096, "lat")
            else:
                if fuse.get("dbg"):
                    latdbg = nc.dram_tensor("latdbg", [64, 4096], F32,
                                            kind="ExternalOutput").ap()
                lat_all = fuse["lat_all_hbm"]
                lt_f = pC.enter_context(nc.sbuf_tensor("c_ltf", [128, 64], F32))
                lt_b = pC.enter_context(nc.sbuf_tensor("c_ltb", [128, 64], BF16))
                lt_d = pC.enter_context(nc.sbuf_tensor("c_ltd", [64, 128], F32))
                for kt in range(32):
                    S.dma("SYNC", lt_f[:], lat_all[kt * 128:(kt + 1) * 128, :],
                          reads=["agl"], writes=["ltf"])
                    S.op("DVE", lambda e: e.tensor_copy(lt_b[:], lt_f[:]),
                         reads=["ltf"], writes=["ltb"])
                    S.op("TE", lambda e, o=pt_bf[kt % 2][0:64, 0:128], i=lt_b[:]:
                         e.transpose(o, i, id_bf[:]),
                         reads=["ltb", "id"], writes=[PT[kt % 2]])
                    S.op("DVE", lambda e, o=lat_bf[0:64, kt * 128:(kt + 1) * 128],
                         i=pt_bf[kt % 2][0:64, 0:128], kt=kt: e.tensor_copy(o, i),
                         reads=[PT[kt % 2]], writes=["lat"])
                    if fuse.get("dbg"):
                        S.op("DVE", lambda e, o=lt_d[:],
                             i=pt_bf[kt % 2][0:64, 0:128]: e.tensor_copy(o, i),
                             reads=[PT[kt % 2]], writes=["ltdbg"])
                        S.dma("SYNC", latdbg[:, kt * 128:(kt + 1) * 128],
                              lt_d[:], reads=["ltdbg"])
            conv_in(wkupT_in, wkup_bf[:], 64, 2048, "wkup")
            conv_in(wvupT_in, wvup_bf[:], 64, 2048, "wvup")

            for tt in range(16):
                S.dma("SYNC", xt[:], x2_in[tt * 128:(tt + 1) * 128, :], writes=["xt"])
                emit_ln(S, xt[:], lnx[:], mst, sqs[:], c3, c2, "xt", "lnx",
                        rk_extra=["cst"])
                for half in range(2):
                    pti = half
                    fns = []
                    for q4 in range(4):
                        dch = half * 4 + q4
                        fns.append(lambda e, o=pt_bf[pti][:, q4 * 128:(q4 + 1) * 128],
                                   i=lnx[:, dch * 128:(dch + 1) * 128]:
                                   e.transpose(o, i, id_bf[:]))
                    tok = S.op("TE", fns[0], reads=["lnx", "id"], writes=[PT[pti]])
                    for f in fns[1:]:
                        tok = S.op("TE", f)
                    S.lw[PT[pti]] = tok

                    def ev(e, pt=pt_bf[pti], half=half, tt=tt):
                        src = pt[:, 0:512].rearrange("p (a b) -> p a b", a=4)
                        d = lnx2T[:].rearrange("p (a b) -> p a b", a=8, b=2048)[
                            :, half * 4: half * 4 + 4, tt * 128: tt * 128 + 128]
                        return e.tensor_copy(d, src)
                    S.op("DVE", ev, reads=[PT[pti]], writes=[("lnx2T", tt)])
            ln2keys = [("lnx2T", t_) for t_ in range(16)]

            for h in range(8):
                hp = h % 2
                for off in range(0, 2048, 1024):
                    S.dma("SYNC", stage[:, 0:1024],
                          woT_in[:, 2 * h * 1024 + off: 2 * h * 1024 + off + 1024],
                          writes=["stage"])
                    S.op("DVE", lambda e, d=woh_bf[:, hp * 2048 + off: hp * 2048 + off + 1024],
                         s=stage[:, 0:1024]: e.tensor_copy(d, s),
                         reads=["stage"], writes=[("wo", hp)])
                for off in range(0, 2048, 1024):
                    S.dma("ACT", stg3[:, 0:1024],
                          wqT_in[:, h * 2048 + off: h * 2048 + off + 1024],
                          writes=["stg3"])
                    S.op("DVE", lambda e, d=wqh_bf[:, hp * 2048 + off: hp * 2048 + off + 1024],
                         s=stg3[:, 0:1024]: e.tensor_copy(d, s),
                         reads=["stg3"], writes=[("wqh", hp)])
                for mch in range(2):
                    for nc4 in range(4):
                        bi = (mch * 4 + nc4) % 6
                        fns = [lambda e, o=pb[bi][:, 0:512],
                               l=wqh_bf[:, hp * 2048 + kch * 256 + mch * 128:
                                         hp * 2048 + kch * 256 + mch * 128 + 128],
                               r=lnx2T[:, kch * 2048 + nc4 * 512: kch * 2048 + (nc4 + 1) * 512],
                               st=(kch == 0), sp=(kch == 7):
                               e.matmul(o, l, r, start=st, stop=sp) for kch in range(8)]
                        S.mmg(PB[bi], fns, reads=ln2keys + [("wqh", hp)])
                        S.op("ACT" if (mch * 4 + nc4) % 2 == 0 else "DVE",
                             lambda e, o=qT_h[:, mch * 2048 + nc4 * 512:
                                              mch * 2048 + (nc4 + 1) * 512],
                             i=pb[bi][:, 0:512]:
                             (e.activation(o, i, AF.Identity) if hasattr(e, "activation")
                              else e.tensor_copy(o, i)),
                             reads=[PB[bi]], writes=[("qT_h", mch * 4 + nc4)])
                for mch in range(2):
                    for nc8 in range(8):
                        ii = mch * 8 + nc8
                        bi = ii % 6
                        S.mmg(PB[bi], [lambda e, o=pb[bi][:, 0:512],
                                       l=wkup_bf[0:64, h * 256 + mch * 128: h * 256 + mch * 128 + 128],
                                       r=lat_bf[0:64, nc8 * 512:(nc8 + 1) * 512]:
                                       e.matmul(o, l, r, start=True, stop=True)],
                              reads=["wkup", "lat"])
                        S.op("ACT" if ii % 2 == 0 else "DVE",
                             lambda e, o=kT_h[:, mch * 4096 + nc8 * 512:
                                              mch * 4096 + (nc8 + 1) * 512],
                             i=pb[bi][:, 0:512]:
                             (e.activation(o, i, AF.Identity) if hasattr(e, "activation")
                              else e.tensor_copy(o, i)),
                             reads=[PB[bi]], writes=[("kT_h", ii)])
                for ktt in range(32):
                    bi = ktt % 6
                    S.mmg(PB[bi], [lambda e, o=pb[bi][:, 0:256],
                                   l=lat_bf[0:64, ktt * 128:(ktt + 1) * 128],
                                   r=wvup_bf[0:64, h * 256:(h + 1) * 256]:
                                   e.matmul(o, l, r, start=True, stop=True)],
                          reads=["lat", "wvup"])
                    S.op("ACT" if ktt % 2 == 0 else "DVE",
                         lambda e, o=v_h[:, ktt * 256:(ktt + 1) * 256],
                         i=pb[bi][:, 0:256]:
                         (e.activation(o, i, AF.Identity) if hasattr(e, "activation")
                          else e.tensor_copy(o, i)),
                         reads=[PB[bi]], writes=[("v_h", ktt)])
                for qc in range(4):
                    for m in range(32):
                        sbk = m % 2
                        fns = [lambda e, o=pb[sbk][:, 0:512],
                               l=kT_h[:, mch * 4096 + m * 128: mch * 4096 + (m + 1) * 128],
                               r=qT_h[:, mch * 2048 + qc * 512: mch * 2048 + (qc + 1) * 512],
                               st=(mch == 0), sp=(mch == 1):
                               e.matmul(o, l, r, start=st, stop=sp) for mch in range(2)]
                        S.mmg(PB[sbk], fns,
                              reads=[("kT_h", mch_ * 8 + m // 4) for mch_ in range(2)] +
                                    [("qT_h", mch_ * 4 + qc) for mch_ in range(2)])
                        pti = m % 2
                        S.op("ACT", lambda e, o=pT[pti][:], i=pb[sbk][:, 0:512]:
                             e.activation(o, i, AF.Exp, scale=cst[:, 0:1]),
                             reads=[PB[sbk], "cst"], writes=[("pT", pti)])
                        for dm in range(2):
                            S.op("TE", lambda e, o=pb[2 + dm][:, 0:512],
                                 l=v_h[:, m * 256 + dm * 128: m * 256 + (dm + 1) * 128],
                                 r=pT[pti][:], st=(m == 0), sp=(m == 31):
                                 e.matmul(o, l, r, start=st, stop=sp),
                                 reads=[("pT", pti), ("v_h", m)] +
                                       ([PB[2 + dm]] if m == 0 else []),
                                 writes=[PB[2 + dm]] if m == 0 else [])
                        tok_d = S.op("TE", lambda e, o=pb[4][0:1, 0:512],
                                     l=onescol_bf[:], r=pT[pti][:], st=(m == 0), sp=(m == 31):
                                     e.matmul(o, l, r, start=st, stop=sp),
                                     reads=[("pT", pti), "onescol"] +
                                           ([PB[4]] if m == 0 else []),
                                     writes=[PB[4]] if m == 0 else [])
                    S.lw[PB[2]] = S.now("TE")[0]
                    S.lw[PB[3]] = S.now("TE")[0]
                    S.lw[PB[4]] = tok_d
                    S.op("DVE", lambda e: e.reciprocal(dnr[:], pb[4][0:1, 0:512]),
                         reads=[PB[4]], writes=["dnr"])
                    S.mmg(PB[5], [lambda e: e.matmul(pb[5][:, 0:512], ones1_f[:], dnr[:],
                                                     start=True, stop=True)],
                          reads=["dnr", "ones1f"])
                    S.op("ACT", lambda e: e.activation(rb_f[:], pb[5][:, 0:512], AF.Identity),
                         reads=[PB[5]], writes=["rb_f"])
                    for dm in range(2):
                        S.op("DVE", lambda e, o=outT_h[:, dm * 2048 + qc * 512:
                                                       dm * 2048 + (qc + 1) * 512],
                             a=pb[2 + dm][:, 0:512], dm=dm: e.tensor_mul(o, a, rb_f[:]),
                             reads=[PB[2 + dm], "rb_f"], writes=[("outT", dm, qc)])
                okeys = [("outT", d_, q_) for d_ in range(2) for q_ in range(4)]
                for tt in range(16):
                    for dmc in range(2):
                        bi = dmc
                        fns = [lambda e, o=pb[bi][:, 0:512],
                               l=outT_h[:, dm_ * 2048 + tt * 128: dm_ * 2048 + tt * 128 + 128],
                               r=woh_bf[:, hp * 2048 + dm_ * 1024 + dmc * 512:
                                        hp * 2048 + dm_ * 1024 + dmc * 512 + 512],
                               st=(dm_ == 0), sp=(dm_ == 1):
                               e.matmul(o, l, r, start=st, stop=sp) for dm_ in range(2)]
                        S.mmg(PB[bi], fns, reads=okeys + [("wo", hp)])
                        ya = y_acc[:, tt * 1024 + dmc * 512: tt * 1024 + dmc * 512 + 512]
                        if h == 0:
                            S.op("ACT", lambda e, o=ya, i=pb[bi][:, 0:512]:
                                 e.activation(o, i, AF.Identity),
                                 reads=[PB[bi]], writes=[("yac", tt, dmc)])
                        else:
                            S.op("DVE", lambda e, o=ya, i=pb[bi][:, 0:512]:
                                 e.tensor_add(o, i, o),
                                 reads=[PB[bi], ("yac", tt, dmc)],
                                 writes=[("yac", tt, dmc)])
            for tt in range(16):
                S.dma("SYNC", xt[:], x2_in[tt * 128:(tt + 1) * 128, :], writes=["xt"])
                for dmc in range(2):
                    S.op("DVE", lambda e, o=y_sb[:, dmc * 512:(dmc + 1) * 512],
                         a=y_acc[:, tt * 1024 + dmc * 512: tt * 1024 + dmc * 512 + 512],
                         b=xt[:, dmc * 512:(dmc + 1) * 512]: e.tensor_add(o, a, b),
                         reads=[("yac", tt, dmc), "xt"], writes=[("y", dmc)])
                S.dma("SYNC", x3_hbm[tt * 128:(tt + 1) * 128, :], y_sb[:],
                      reads=[("y", 0), ("y", 1)], writes=[("x3", tt)])
            S.barrier()

        x3keys = [("x3", t_) for t_ in range(16)]
        with ExitStack() as pD:
            sbD = lambda n, s, dt: pD.enter_context(nc.sbuf_tensor(pref + n, s, dt))
            stage = sbD("stageD", [128, 64 if fuse is not None else 2 * 4096], F32)
            stg2 = sbD("stg2D", [128, 1024 if fuse is not None else 2 * 1024], F32)
            normT = sbD("normT", [128, 8 * 2048], BF16)
            hidT = sbD("hidT", [128, 22 * 1024], BF16)
            dn_bf = sbD("dnbf", [128, 22 * 1024], BF16)
            gup_bf = sbD("gupbf", [128, 2 * 4096], BF16)
            y_t = sbD("yt", [128, 2 * 1024], BF16)
            wn_sb = sbD("wnsb", [128, 1024], BF16)
            rw_bf = sbD("rwbf", [128, 64], BF16)
            xt3 = sbD("xt3", [128, 1024], F32)
            sqs3 = sbD("sqs3", [128, 1024], F32)
            lnx3 = sbD("lnx3", [128, 1024], BF16)
            nrm = sbD("nrm", [128, 1024], BF16)
            lg = sbD("lgD", [128, 8], F32)
            msD = sbD("msD", [128, 4], F32)
            prb = sbD("prb", [128, 8], F32)
            gtmp = sbD("gtmp", [128, 1024], BF16)
            utmp = sbD("utmp", [128, 1024], BF16)
            lnf = sbD("lnf", [128, 1024], F32)
            moef = sbD("moef", [128, 1024], F32)

            S.dma("SYNC", stg2[:, 0:1024], wnorm_in, writes=[("stg2D", 0)])
            S.op("DVE", lambda e: e.tensor_copy(wn_sb[:], stg2[:, 0:1024]),
                 reads=[("stg2D", 0)], writes=["wn"])
            S.dma("SYNC", stage[:, 0:64], rwT_in, writes=["stageD"])
            S.op("DVE", lambda e: e.tensor_copy(rw_bf[:], stage[:, 0:64]),
                 reads=["stageD"], writes=["rw"])

            for tt in range(16):
                S.dma("SYNC", xt3[:], x3_hbm[tt * 128:(tt + 1) * 128, :],
                      reads=x3keys, writes=["xt3"])
                emit_ln(S, xt3[:], lnx3[:], mst, sqs3[:], c3, c2, "xt3", "lnx3",
                        rk_extra=["cst"])
                S.op("ACT", lambda e: e.activation(sqs3[:], lnx3[:], AF.Square),
                     reads=["lnx3"], writes=[("sqk", 0)])
                S.op("DVE", lambda e: e.reduce_sum(msD[:, 0:1], sqs3[:], axis=AX.X),
                     reads=[("sqk", 0)], writes=["msD"])
                S.op("DVE", lambda e: e.tensor_scalar(msD[:, 0:1], msD[:, 0:1], c3, None,
                                                      ALU.mult),
                     reads=["msD", "cst"], writes=["msD"])
                S.op("ACT", lambda e: e.activation(msD[:, 1:2], msD[:, 0:1], AF.Sqrt,
                                                   bias=c4),
                     reads=["msD", "cst"], writes=["msD"])
                S.op("DVE", lambda e: e.reciprocal(msD[:, 1:2], msD[:, 1:2]),
                     reads=["msD"], writes=["msD"])
                S.op("DVE", lambda e: e.tensor_scalar(gtmp[:], lnx3[:], msD[:, 1:2], None,
                                                      ALU.mult),
                     reads=["lnx3", "msD"], writes=["gt"])
                S.op("DVE", lambda e: e.tensor_mul(nrm[:], gtmp[:], wn_sb[:]),
                     reads=["gt", "wn"], writes=["nrm"])
                for half in range(2):
                    pti = half
                    fns = []
                    for q4 in range(4):
                        dch = half * 4 + q4
                        fns.append(lambda e, o=pt_bf[pti][:, q4 * 128:(q4 + 1) * 128],
                                   i=nrm[:, dch * 128:(dch + 1) * 128]:
                                   e.transpose(o, i, id_bf[:]))
                    tok = S.op("TE", fns[0], reads=["nrm", "id"], writes=[PT[pti]])
                    for f in fns[1:]:
                        tok = S.op("TE", f)
                    S.lw[PT[pti]] = tok

                    def ev2(e, pt=pt_bf[pti], half=half, tt=tt):
                        src = pt[:, 0:512].rearrange("p (a b) -> p a b", a=4)
                        d = normT[:].rearrange("p (a b) -> p a b", a=8, b=2048)[
                            :, half * 4: half * 4 + 4, tt * 128: tt * 128 + 128]
                        return e.tensor_copy(d, src)
                    S.op("DVE", ev2, reads=[PT[pti]], writes=[("normT", tt)])
                bi = 4
                fns = [lambda e, o=pb[bi][:, 0:8],
                       l=normT[:].rearrange("p (a b) -> p a b", a=8, b=2048)[
                           :, kch, tt * 128: tt * 128 + 128],
                       r=rw_bf[:, kch * 8:(kch + 1) * 8],
                       st=(kch == 0), sp=(kch == 7):
                       e.matmul(o, l, r, start=st, stop=sp) for kch in range(8)]
                S.mmg(PB[bi], fns, reads=[("normT", tt), "rw"])
                S.op("ACT", lambda e: e.activation(lg[:], pb[4][:, 0:8], AF.Identity),
                     reads=[PB[4]], writes=["lg"])
                S.op("DVE", lambda e: e.reduce_max(msD[:, 2:3], lg[:], axis=AX.X),
                     reads=["lg"], writes=["msD"])
                S.op("DVE", lambda e: e.tensor_scalar(msD[:, 3:4], msD[:, 2:3], -1.0, None,
                                                      ALU.mult),
                     reads=["msD"], writes=["msD"])
                S.op("ACT", lambda e: e.activation(prb[:], lg[:], AF.Exp, bias=msD[:, 3:4]),
                     reads=["lg", "msD"], writes=["prb"])
                S.op("DVE", lambda e: e.reduce_sum(msD[:, 2:3], prb[:], axis=AX.X),
                     reads=["prb"], writes=["msD"])
                S.op("DVE", lambda e: e.reciprocal(msD[:, 2:3], msD[:, 2:3]),
                     reads=["msD"], writes=["msD"])
                S.op("DVE", lambda e: e.tensor_scalar(prb[:], prb[:], msD[:, 2:3], None,
                                                      ALU.mult),
                     reads=["prb", "msD"], writes=["prb"])
                S.op("DVE", lambda e: e.reduce_max(msD[:, 2:3], prb[:], axis=AX.X),
                     reads=["prb"], writes=["msD"])
                S.op("DVE", lambda e: e.tensor_scalar(lg[:], prb[:], msD[:, 2:3], None,
                                                      ALU.is_ge),
                     reads=["prb", "msD"], writes=["lg"])
                S.op("DVE", lambda e: e.scalar_tensor_tensor(sqs3[:, 0:8], lg[:],
                                                             cst[:, 7:8], prb[:],
                                                             ALU.mult, ALU.add),
                     reads=["lg", "prb", "cst"], writes=[("sqk", 0)])
                S.op("DVE", lambda e: e.reduce_max(msD[:, 3:4], sqs3[:, 0:8], axis=AX.X),
                     reads=[("sqk", 0)], writes=["msD"])
                S.op("DVE", lambda e: e.tensor_scalar(sqs3[:, 8:16], sqs3[:, 0:8],
                                                      msD[:, 3:4], None, ALU.is_ge),
                     reads=[("sqk", 0), "msD"], writes=[("sqk", 0)])
                S.op("DVE", lambda e: e.tensor_add(lg[:], lg[:], sqs3[:, 8:16]),
                     reads=["lg", ("sqk", 0)], writes=["lg"])
                S.op("DVE", lambda e: e.tensor_mul(prb[:], prb[:], lg[:]),
                     reads=["prb", "lg"], writes=["prb"])
                S.op("DVE", lambda e: e.reduce_sum(msD[:, 2:3], prb[:], axis=AX.X),
                     reads=["prb"], writes=["msD"])
                S.op("DVE", lambda e: e.reciprocal(msD[:, 2:3], msD[:, 2:3]),
                     reads=["msD"], writes=["msD"])
                S.op("DVE", lambda e, o=w2_tok[:, tt * 8:(tt + 1) * 8]:
                     e.tensor_scalar(o, prb[:], msD[:, 2:3], None, ALU.mult),
                     reads=["prb", "msD"], writes=[("w2", tt)])

            nkeys = [("normT", t_) for t_ in range(16)]
            for e_ in range(8):
                for mj in range(22):
                    if fuse is not None:
                        S.dma("ACT", dn_bf[:, mj * 1024:(mj + 1) * 1024],
                              fuse["dnb_h"][e_ * 2816 + mj * 128:
                                            e_ * 2816 + (mj + 1) * 128, :],
                              reads=["dnbH"], writes=["dn"])
                        continue
                    par = mj % 2
                    o0 = par * 1024
                    S.dma("ACT", stg2[:, o0:o0 + 1024],
                          dnT_in[e_, mj * 128:(mj + 1) * 128, :], writes=[("stg2D", par)])
                    if par == 0:
                        S.op("DVE", lambda e, d=dn_bf[:, mj * 1024:(mj + 1) * 1024],
                             s=stg2[:, o0:o0 + 1024]: e.tensor_copy(d, s),
                             reads=[("stg2D", par)], writes=["dn"])
                    else:
                        S.op("ACT", lambda e, d=dn_bf[:, mj * 1024:(mj + 1) * 1024],
                             s=stg2[:, o0:o0 + 1024]: e.activation(d, s, AF.Identity),
                             reads=[("stg2D", par)], writes=["dn"])
                for tc2 in range(2):
                    for mjp in range(11):
                        par = mjp % 2
                        o0 = par * 4096
                        if fuse is not None:
                            r0 = (e_ * 11 + mjp) * 128
                            S.dma("SYNC", gup_bf[:, par * 4096:(par + 1) * 4096],
                                  fuse["gupb_h"][r0:r0 + 128, :],
                                  reads=["gupbH"], writes=[("gup", par)])
                        else:
                            S.dma("SYNC", stage[:, o0:o0 + 4096], gupT_in[e_, mjp, :, :],
                                  writes=[("stageD", par)])
                            S.op("DVE", lambda e,
                                 d=gup_bf[:, par * 4096: par * 4096 + 2048],
                                 s=stage[:, o0:o0 + 2048]: e.tensor_copy(d, s),
                                 reads=[("stageD", par)], writes=[("gup", par)])
                            S.op("ACT", lambda e,
                                 d=gup_bf[:, par * 4096 + 2048: par * 4096 + 4096],
                                 s=stage[:, o0 + 2048:o0 + 4096]: e.activation(d, s, AF.Identity),
                                 reads=[("stageD", par)], writes=[("gup", par)])
                        for sub in range(2):
                            mj = 2 * mjp + sub
                            gof = par * 4096 + sub * 2048
                            for nc2 in range(2):
                                tglob = tc2 * 1024 + nc2 * 512
                                big, biu = 0, 1
                                fns = [lambda e, o=pb[big][:, 0:512],
                                       l=gup_bf[:, gof + kch * 128: gof + (kch + 1) * 128],
                                       r=normT[:, kch * 2048 + tglob: kch * 2048 + tglob + 512],
                                       st=(kch == 0), sp=(kch == 7):
                                       e.matmul(o, l, r, start=st, stop=sp) for kch in range(8)]
                                S.mmg(PB[big], fns, reads=nkeys + [("gup", par)])
                                fns = [lambda e, o=pb[biu][:, 0:512],
                                       l=gup_bf[:, gof + 1024 + kch * 128:
                                                gof + 1024 + (kch + 1) * 128],
                                       r=normT[:, kch * 2048 + tglob: kch * 2048 + tglob + 512],
                                       st=(kch == 0), sp=(kch == 7):
                                       e.matmul(o, l, r, start=st, stop=sp) for kch in range(8)]
                                S.mmg(PB[biu], fns, reads=nkeys + [("gup", par)])
                                S.op("ACT", lambda e, o=gtmp[:, nc2 * 512:(nc2 + 1) * 512],
                                     i=pb[big][:, 0:512]: e.activation(o, i, AF.Silu),
                                     reads=[PB[big]], writes=["gt"])
                                S.op("ACT", lambda e, o=utmp[:, nc2 * 512:(nc2 + 1) * 512],
                                     i=pb[biu][:, 0:512]: e.activation(o, i, AF.Identity),
                                     reads=[PB[biu]], writes=["ut"])
                            S.op("DVE", lambda e, o=hidT[:, mj * 1024:(mj + 1) * 1024],
                                 a=gtmp[:, 0:1024], b=utmp[:, 0:1024]: e.tensor_mul(o, a, b),
                                 reads=["gt", "ut"], writes=[("hid", mj)])
                    for tt4 in range(8):
                        gt_ = tc2 * 8 + tt4
                        yp = tt4 % 2
                        for dmc in range(2):
                            bi = 4 + dmc
                            fns = [lambda e, o=pb[bi][:, 0:512],
                                   l=hidT[:, mj * 1024 + tt4 * 128: mj * 1024 + tt4 * 128 + 128],
                                   r=dn_bf[:, mj * 1024 + dmc * 512: mj * 1024 + dmc * 512 + 512],
                                   st=(mj == 0), sp=(mj == 21):
                                   e.matmul(o, l, r, start=st, stop=sp) for mj in range(22)]
                            S.mmg(PB[bi], fns,
                                  reads=[("hid", mj_) for mj_ in range(22)] + ["dn"])
                            wap = w2_tok[:, gt_ * 8 + e_: gt_ * 8 + e_ + 1]
                            S.op("DVE", lambda e, o=y_t[:, yp * 1024 + dmc * 512:
                                                        yp * 1024 + dmc * 512 + 512],
                                 i=pb[bi][:, 0:512], w=wap:
                                 e.tensor_scalar(o, i, w, None, ALU.mult),
                                 reads=[PB[bi], ("w2", gt_)], writes=[("ya", yp, dmc)])
                        S.dma("ACT", ye_hbm[e_ * NT + gt_ * 128: e_ * NT + (gt_ + 1) * 128, :],
                              y_t[:, yp * 1024:(yp + 1) * 1024],
                              reads=[("ya", yp, 0), ("ya", yp, 1)],
                              writes=[("yeh", e_, gt_)])

            for tt in range(16):
                S.dma("SYNC", xt3[:], x3_hbm[tt * 128:(tt + 1) * 128, :],
                      reads=x3keys, writes=["xt3"])
                mk = ("mstk", 0)
                S.op("DVE", lambda e: e.reduce_sum(mst[:, 0:1], xt3[:], axis=AX.X),
                     reads=["xt3"], writes=[mk])
                S.op("ACT", lambda e: e.activation(sqs3[:], xt3[:], AF.Square),
                     reads=["xt3"], writes=[("sqk", 0)])
                S.op("DVE", lambda e: e.reduce_sum(mst[:, 1:2], sqs3[:], axis=AX.X),
                     reads=[("sqk", 0)], writes=[mk])
                S.op("DVE", lambda e: e.tensor_scalar(mst[:, 2:3], mst[:, 0:1], c3, None, ALU.mult),
                     reads=[mk, "cst"], writes=[mk])
                S.op("DVE", lambda e: e.tensor_scalar(mst[:, 3:4], mst[:, 1:2], c3, None, ALU.mult),
                     reads=[mk], writes=[mk])
                S.op("DVE", lambda e: e.tensor_mul(mst[:, 4:5], mst[:, 2:3], mst[:, 2:3]),
                     reads=[mk], writes=[mk])
                S.op("DVE", lambda e: e.tensor_sub(mst[:, 5:6], mst[:, 3:4], mst[:, 4:5]),
                     reads=[mk], writes=[mk])
                S.op("ACT", lambda e: e.activation(mst[:, 6:7], mst[:, 5:6], AF.Sqrt, bias=c2),
                     reads=[mk, "cst"], writes=[mk])
                S.op("DVE", lambda e: e.reciprocal(mst[:, 6:7], mst[:, 6:7]),
                     reads=[mk], writes=[mk])
                S.op("DVE", lambda e: e.tensor_mul(mst[:, 7:8], mst[:, 2:3], mst[:, 6:7]),
                     reads=[mk], writes=[mk])
                S.op("DVE", lambda e: e.tensor_scalar(mst[:, 7:8], mst[:, 7:8], -1.0, None, ALU.mult),
                     reads=[mk], writes=[mk])
                S.op("ACT", lambda e: e.activation(lnf[:], xt3[:], AF.Identity,
                                                   bias=mst[:, 7:8], scale=mst[:, 6:7]),
                     reads=[mk, "xt3"], writes=["lnf"])
                for e_ in range(8):
                    S.dma("SYNC", y_t[:, (e_ % 2) * 1024:(e_ % 2 + 1) * 1024],
                          ye_hbm[e_ * NT + tt * 128: e_ * NT + (tt + 1) * 128, :],
                          reads=[("yeh", e_, tt)], writes=[("ytl", e_ % 2)])
                    if e_ == 0:
                        S.op("DVE", lambda e, i=y_t[:, 0:1024]: e.tensor_copy(moef[:], i),
                             reads=[("ytl", 0)], writes=["moef"])
                    else:
                        S.op("DVE", lambda e, i=y_t[:, (e_ % 2) * 1024:(e_ % 2 + 1) * 1024]:
                             e.tensor_add(moef[:], moef[:], i),
                             reads=[("ytl", e_ % 2), "moef"], writes=["moef"])
                S.op("DVE", lambda e: e.tensor_scalar(moef[:], moef[:],
                                                      alpha_sb[:, 0:1], None, ALU.mult),
                     reads=["moef", "alpha"], writes=["moef"])
                S.op("DVE", lambda e: e.tensor_add(lnf[:], lnf[:], xt3[:]),
                     reads=["lnf", "xt3"], writes=["lnf"])
                S.op("DVE", lambda e: e.tensor_add(lnf[:], lnf[:], moef[:]),
                     reads=["lnf", "moef"], writes=["lnf"])
                S.dma("SYNC", out_f[tt * 128:(tt + 1) * 128, :], lnf[:], reads=["lnf"])

            S.op("SYNC", lambda e: e.nop(), waits=S.now("QSP", "QACT"), q="QSP", n=16)
            S.barrier()

        if fuse is None:
            blk = ctx.enter_context(nc.Block())
            S.replay(blk, sems)

    return nc


def build_full(dbg=False):
    nc = bass.Bass("TRN2", target_bir_lowering=False)
    kind = "ExternalOutput" if dbg else "Internal"
    x1_hbm = nc.dram_tensor("x1f", [NT, DM], F32, kind=kind).ap()
    x2_hbm = nc.dram_tensor("x2f", [NT, DM], F32, kind=kind).ap()
    pTown_h = nc.dram_tensor("pTownh", [8, 2048], F32)
    wprobs_h = nc.dram_tensor("wprobsh", [128, 128], F32)
    pts_h = nc.dram_tensor("ptsh", [64, 2048], F32)
    lato_h = nc.dram_tensor("latoh", [2048, 64], F32)
    latall_h = nc.dram_tensor("latallh", [4096, 64], F32)
    wup_in = nc.dram_tensor("wup", [4, 1024, 2730], F32, kind="ExternalInput").ap()
    wdn_in = nc.dram_tensor("wdn", [8, 2730, 1024], F32, kind="ExternalInput").ap()
    gupT_in = nc.dram_tensor("gupT", [8, 11, 128, 4096], F32, kind="ExternalInput").ap()
    dnT_in = nc.dram_tensor("dnT", [8, 2816, 1024], F32, kind="ExternalInput").ap()
    wupb_h = nc.dram_tensor("wupbH", [4096, 2730], BF16, kind="Internal").ap()
    wdb_h = nc.dram_tensor("wdbH", [8 * 2730, 1024], BF16, kind="Internal").ap()
    gupb_h = nc.dram_tensor("gupbH", [11264, 4096], BF16, kind="Internal").ap()
    dnb_h = nc.dram_tensor("dnbH", [8 * 2816, 1024], BF16, kind="Internal").ap()

    with ExitStack() as ctx:
        sb = lambda n, s, dt: ctx.enter_context(nc.sbuf_tensor(n, s, dt))
        psm = lambda n, s, dt=F32: ctx.enter_context(nc.psum_tensor(n, s, dt))
        pb = [psm(f"pb{i}", [128, 512]) for i in range(6)]
        pt_bf = [psm(f"ptb{i}", [128, 512], BF16) for i in range(2)]
        S = Sched()
        sems = make_sems(nc, ctx)
        fuse = {
            "nc": nc, "S": S, "sems": sems, "pb": pb, "pt_bf": pt_bf,
            "x1_hbm": x1_hbm, "x2_hbm": x2_hbm,
            "pts_hbm": pts_h.ap(),
            "pTown_h": pTown_h.ap(),
            "wprobs_h": wprobs_h.ap(),
            "lat_own_hbm": lato_h.ap(),
            "lat_all_hbm": latall_h.ap(),
            "wup_in": wup_in, "wdn_in": wdn_in,
            "gupT_in": gupT_in, "dnT_in": dnT_in,
            "wupb_h": wupb_h, "wdb_h": wdb_h,
            "gupb_h": gupb_h, "dnb_h": dnb_h,
        }

        fuse["dbg"] = dbg
        build_A(fuse)
        S.op("POOL", lambda e: e.collective_compute(
            "AllGather", ALU.bypass, replica_groups=[list(range(8))],
            ins=[pTown_h.ap().opt()], outs=[pts_h.ap().opt()]),
            reads=["pTownH"], writes=["agp"], q="POOL", n=1)

        build_B(fuse)
        S.op("POOL", lambda e: e.collective_compute(
            "AllGather", ALU.bypass,
            replica_groups=[[0, 1], [2, 3], [4, 5], [6, 7]],
            ins=[lato_h.ap().opt()], outs=[latall_h.ap().opt()]),
            reads=[("lato", t_) for t_ in range(16)], writes=["agl"],
            q="POOL", n=1)

        build_CD(fuse)

        blk = ctx.enter_context(nc.Block())
        S.replay(blk, sems)
    return nc


def hostFull(inp, core):
    d = hostA(inp, core)
    b = hostB(inp, core, None)
    for k in ("x1", "probs_own", "probsT_r"):
        b.pop(k)
    d.update(b)
    cd = hostCD(inp, core, None)
    for k in ("x2", "latT"):
        cd.pop(k)
    d.update(cd)
    return d


from concourse.bass_utils import run_bass_kernel_spmd

_CACHE = {}


def _get(name, builder):
    if name not in _CACHE:
        _CACHE[name] = builder()
    return _CACHE[name]


def kernel(**inputs):
    nc = _get("FULL", build_full)
    maps = [hostFull(inputs, c) for c in range(8)]
    res = run_bass_kernel_spmd(nc, maps, core_ids=list(range(8))).results
    out = np.concatenate([np.asarray(r["out"]) for r in res], 0)
    return out.reshape(4, 4096, 1024)


# version 17 (final)
# speedup vs baseline: 1.0371x; 1.0160x over previous
import numpy as np
import ml_dtypes
from contextlib import ExitStack
import concourse.bass as bass
import concourse.mybir as mybir

F32 = mybir.dt.float32
BF16 = mybir.dt.bfloat16
AF = mybir.ActivationFunctionType
ALU = mybir.AluOpType
AX = mybir.AxisListType
BF = ml_dtypes.bfloat16

NT = 2048
HALO = 640
NTH = NT + HALO
DM = 1024

ENGS = ("TE", "ACT", "DVE", "POOL", "SYNC")
DROP_TE_SELF = False
DMAQ = {"SYNC": "QSP", "ACT": "QACT", "POOL": "QPOOL"}


class Sched:
    def __init__(self):
        self.items = {e: [] for e in ENGS}
        self.c = {"TE": 0, "ACT": 0, "DVE": 0, "POOL": 0,
                  "QSP": 0, "QACT": 0, "QPOOL": 0}
        self.lw = {}
        self.rd = {}

    def _tok_of(self, eng, q, n):
        q = q or eng
        self.c[q] += n
        return q, self.c[q]

    def op(self, eng, fn, reads=(), writes=(), waits=(), q=None, n=1):
        wl = {}

        def addw(tok):
            if tok is None:
                return
            qq, vv = tok
            if vv > 0 and wl.get(qq, 0) < vv:
                wl[qq] = vv

        for t in waits:
            addw(t)
        for k in reads:
            addw(self.lw.get(k))
        for k in writes:
            addw(self.lw.get(k))
            for qq, vv in self.rd.get(k, {}).items():
                addw((qq, vv))
        q = q or eng
        if q.startswith("Q"):
            addw((q, self.c[q]))
        self.c[q] += n
        tok = (q, self.c[q])
        self.items[eng].append((fn, sorted(wl.items()), q, n))
        for k in reads:
            d = self.rd.setdefault(k, {})
            if d.get(tok[0], 0) < tok[1]:
                d[tok[0]] = tok[1]
        for k in writes:
            self.lw[k] = tok
            self.rd[k] = {}
        return tok

    def dma(self, eng, out, in_, reads=(), writes=(), waits=()):
        return self.op(eng, lambda e, o=out, i=in_: e.dma_start(out=o, in_=i),
                       reads, writes, waits, q=DMAQ[eng], n=16)

    def mmg(self, bank, fns, reads=(), waits=()):
        tok = self.op("TE", fns[0], reads=reads, writes=[bank], waits=waits)
        for fn in fns[1:]:
            tok = self.op("TE", fn)
        self.lw[bank] = tok
        for k in reads:
            d = self.rd.setdefault(k, {})
            if d.get(tok[0], 0) < tok[1]:
                d[tok[0]] = tok[1]
        return tok

    def now(self, *qs):
        return [(q, self.c[q]) for q in qs]

    def barrier_tokens(self):
        return self.now("TE", "ACT", "DVE", "POOL", "QSP", "QACT", "QPOOL")

    def barrier(self):
        toks = [t for t in self.barrier_tokens() if t[1] > 0]
        for e in ENGS:
            self.items[e].append((None, toks, None, 0))

    def replay(self, blk, sems):
        def mk(eng_items, ename):
            def body(eng):
                wm = {}
                for fn, waits, q, n in eng_items:
                    for qq, vv in waits:
                        if ename == "TE" and qq == "TE" and DROP_TE_SELF:
                            continue
                        if vv > wm.get(qq, 0):
                            eng.wait_ge(sems[qq], vv)
                            wm[qq] = vv
                    if fn is not None:
                        fn(eng).then_inc(sems[q], n)
            return body
        blk.tensor(mk(self.items["TE"], "TE"))
        blk.scalar(mk(self.items["ACT"], "ACT"))
        blk.vector(mk(self.items["DVE"], "DVE"))
        blk.gpsimd(mk(self.items["POOL"], "POOL"))
        blk.sync(mk(self.items["SYNC"], "SYNC"))


def make_sems(nc, ctx):
    names = ("TE", "ACT", "DVE", "POOL", "QSP", "QACT", "QPOOL")
    return {q: ctx.enter_context(nc.semaphore("s_" + q.lower())) for q in names}


def _bf(x):
    return np.ascontiguousarray(np.asarray(x), dtype=BF)


def _f32(x):
    return np.ascontiguousarray(np.asarray(x), dtype=np.float32)


def _kc(x, p=128):
    K, N = x.shape
    return np.ascontiguousarray(x.reshape(K // p, p, N).transpose(1, 0, 2).reshape(p, -1))


def _ctile_cols(v, p=128):
    return np.ascontiguousarray(np.asarray(v, dtype=np.float32).reshape(-1, p).T)


def rotary_tables_half(t0, n):
    D = 64
    inv_freq = 1.0 / 10000.0 ** (np.arange(0, D, 2, dtype=np.float32) / D)
    pos = np.arange(t0, t0 + n, dtype=np.float32)
    fr = inv_freq[np.arange(D) % 32][:, None] * pos[None, :]
    return (np.concatenate([np.cos(fr)] * 2, 0).astype(np.float32),
            np.concatenate([np.sin(fr)] * 2, 0).astype(np.float32))


def rot_lhsT():
    R = np.zeros((128, 128), dtype=np.float32)
    for base in (0, 64):
        for d in range(32):
            R[base + d, base + d + 32] = -1.0
            R[base + d + 32, base + d] = 1.0
    return np.ascontiguousarray(R.T)


def l0_masks(half):
    out = np.zeros((128, 2 * 8 * 512), dtype=np.float32)
    NEG = -30000.0
    r = np.arange(128)[:, None]
    j = np.arange(512)[None, :]
    for m in range(8):
        d = 512 - 128 * m + j - r
        std = np.where((d >= 0) & (d <= 511), 0.0, NEG)
        out[:, m * 512:(m + 1) * 512] = std
        dead = np.full((128, 512), NEG, dtype=np.float32)
        first = dead if (half == 0 and m < 4) else std
        out[:, (8 + m) * 512:(9 + m) * 512] = first
    return _bf(out)


def hostA(inp, core):
    b, h = core // 2, core % 2
    x = _f32(inp["L0_attn_x"])
    own = x[b, h * 2048:(h + 1) * 2048]
    halo = (np.zeros((HALO, DM), np.float32) if h == 0
            else x[b, 2048 - HALO:2048])
    xh = np.concatenate([halo, own], 0)
    cos, sin = rotary_tables_half(h * 2048 - HALO, NTH)
    if h == 0:
        c2_, s2_ = rotary_tables_half(0, NTH)
        cos[:, :HALO] = c2_[:, :HALO]
        sin[:, :HALO] = s2_[:, :HALO]
    cdiag = {}
    for p in "qkv":
        w = _f32(inp[f"L0_attn_c{p}w"])
        d = np.zeros((128, 8 * 4 * 128), np.float32)
        for cch in range(8):
            for j in range(4):
                blk = np.zeros((128, 128), np.float32)
                np.fill_diagonal(blk, w[cch * 128:(cch + 1) * 128, j])
                d[:, (cch * 4 + j) * 128:(cch * 4 + j + 1) * 128] = blk
        cdiag[p] = d
    grp2 = np.zeros((128, 2), np.float32)
    grp2[:64, 0] = 1.0
    grp2[64:, 1] = 1.0
    return {
        "xh": _f32(xh),
        "wqT": _kc(_f32(inp["L0_attn_Wq"]).T),
        "wkT": _kc(_f32(inp["L0_attn_Wk"]).T),
        "wvT": _kc(_f32(inp["L0_attn_Wv"]).T),
        "woT": _kc(_f32(inp["L0_attn_Wo"]).T),
        "bq": _ctile_cols(inp["L0_attn_bq"]),
        "bk": _ctile_cols(inp["L0_attn_bk"]),
        "bv": _ctile_cols(inp["L0_attn_bv"]),
        "cbq": _ctile_cols(inp["L0_attn_cqb"]),
        "cbk": _ctile_cols(inp["L0_attn_ckb"]),
        "cbv": _ctile_cols(inp["L0_attn_cvb"]),
        "cdq": cdiag["q"], "cdk": cdiag["k"], "cdv": cdiag["v"],
        "cosT": cos, "sinT": sin, "rotT": rot_lhsT(),
        "grp2": grp2, "bc2": np.ascontiguousarray(grp2.T),
        "ones64": np.ones((128, 64), np.float32),
        "ident": np.eye(128, dtype=np.float32),
        "masks": l0_masks(h),
        "bo_bc": np.broadcast_to(_f32(inp["L0_attn_bo"]), (128, 1024)).copy(),
        "wr": _kc(_f32(inp["L0_ffn_Wr"])),
        "br_bc": np.broadcast_to(_f32(inp["L0_ffn_br"]), (128, 8)).copy(),
        "consts": np.tile(np.array(
            [0.125, 1e-12, 1e-5, 1.0 / 1024, 0.0, 1.0, -1.0, 0.5], np.float32),
            (128, 1)),
    }


def emit_ln(S, src_ap, dst_ap, mst, sq_scratch, c3, c2, rk_src, wk_dst,
            rk_extra=(), tag=0):
    mk, sk = ("mstk", tag), ("sqk", tag)
    rk_src = list(rk_src) if isinstance(rk_src, list) else [rk_src]
    S.op("DVE", lambda e: e.reduce_sum(mst[:, 0:1], src_ap, axis=AX.X),
         reads=rk_src, writes=[mk])
    S.op("ACT", lambda e: e.activation(sq_scratch, src_ap, AF.Square),
         reads=rk_src, writes=[sk])
    S.op("DVE", lambda e: e.reduce_sum(mst[:, 1:2], sq_scratch, axis=AX.X),
         reads=[sk], writes=[mk])
    S.op("DVE", lambda e: e.tensor_scalar(mst[:, 2:3], mst[:, 0:1], c3, None, ALU.mult),
         reads=[mk], writes=[mk])
    S.op("DVE", lambda e: e.tensor_scalar(mst[:, 3:4], mst[:, 1:2], c3, None, ALU.mult),
         reads=[mk], writes=[mk])
    S.op("DVE", lambda e: e.tensor_mul(mst[:, 4:5], mst[:, 2:3], mst[:, 2:3]),
         reads=[mk], writes=[mk])
    S.op("DVE", lambda e: e.tensor_sub(mst[:, 5:6], mst[:, 3:4], mst[:, 4:5]),
         reads=[mk], writes=[mk])
    S.op("ACT", lambda e: e.activation(mst[:, 6:7], mst[:, 5:6], AF.Sqrt, bias=c2),
         reads=[mk], writes=[mk])
    S.op("DVE", lambda e: e.reciprocal(mst[:, 6:7], mst[:, 6:7]),
         reads=[mk], writes=[mk])
    S.op("DVE", lambda e: e.tensor_mul(mst[:, 7:8], mst[:, 2:3], mst[:, 6:7]),
         reads=[mk], writes=[mk])
    S.op("DVE", lambda e: e.tensor_scalar(mst[:, 7:8], mst[:, 7:8], -1.0, None, ALU.mult),
         reads=[mk], writes=[mk])
    return S.op("ACT", lambda e: e.activation(dst_ap, src_ap, AF.Identity,
                                              bias=mst[:, 7:8], scale=mst[:, 6:7]),
                reads=[mk] + rk_src + list(rk_extra), writes=[wk_dst])


def make_pump_jobs(S, pstg, pbst, chunks, tag, outkey):
    jobs = []
    for i, (src0, dst0, rows, w) in enumerate(chunks):
        par = i % 2
        z = par * 1024

        def jl(src=src0, rows=rows, w=w, z=z, par=par):
            S.dma("SYNC", pstg[0:rows, z:z + w], src,
                  writes=[("pstg", tag, par)])

        def jc(rows=rows, w=w, z=z, par=par, eng=("DVE" if i % 2 == 0 else "ACT")):
            if eng == "DVE":
                S.op("DVE", lambda e, d=pbst[0:rows, z:z + w],
                     s=pstg[0:rows, z:z + w]: e.tensor_copy(d, s),
                     reads=[("pstg", tag, par)], writes=[("pbst", tag, par)])
            else:
                S.op("ACT", lambda e, d=pbst[0:rows, z:z + w],
                     s=pstg[0:rows, z:z + w]: e.activation(d, s, AF.Identity),
                     reads=[("pstg", tag, par)], writes=[("pbst", tag, par)])

        def js(dst=dst0, rows=rows, w=w, z=z, par=par):
            S.dma("ACT", dst, pbst[0:rows, z:z + w],
                  reads=[("pbst", tag, par)], writes=[outkey])
        jobs += [jl, jc, js]
    return jobs


def drain_jobs(jobs, k):
    for _ in range(k):
        if jobs:
            jobs.pop(0)()


CH = [512] * 5 + [128]
CH0 = [0, 512, 1024, 1536, 2048, 2560]


def build_A(fuse=None):
    nc = bass.Bass("TRN2", target_bir_lowering=False) if fuse is None else fuse["nc"]
    pref = "" if fuse is None else "a_"
    g = lambda n, s, dt=F32: nc.dram_tensor(n, s, dt, kind="ExternalInput").ap()

    xh = g("xh", [NTH, DM])
    wT = {p: g(f"w{p}T", [128, 8 * 1024]) for p in "qkv"}
    woT = g("woT", [128, 8 * 1024])
    bias_in = {p: g(f"b{p}", [128, 8]) for p in "qkv"}
    cbias_in = {p: g(f"cb{p}", [128, 8]) for p in "qkv"}
    cdiag_in = {p: g(f"cd{p}", [128, 8 * 4 * 128]) for p in "qkv"}
    cosT_in = g("cosT", [128, NTH])
    sinT_in = g("sinT", [128, NTH])
    rotT_in = g("rotT", [128, 128])
    grp2_in = g("grp2", [128, 2])
    bc2_in = g("bc2", [2, 128])
    ones64_in = g("ones64", [128, 64])
    ident_in = g("ident", [128, 128])
    masks_in = g("masks", [128, 8192], BF16)
    bo_in = g("bo_bc", [128, 1024])
    wr_in = g("wr", [128, 64])
    br_in = g("br_bc", [128, 8])
    consts_in = g("consts", [128, 8])

    if fuse is None:
        x1_out = nc.dram_tensor("x1", [NT, DM], F32, kind="ExternalOutput").ap()
        probs_out = nc.dram_tensor("probs", [NT, 8], F32, kind="ExternalOutput").ap()
    else:
        x1_out = fuse["x1_hbm"]
        probs_out = None

    qhbm = nc.dram_tensor("qhbm", [128, 8 * NTH], BF16, kind="Internal").ap()
    khbm = nc.dram_tensor("khbm", [128, 8 * NTH], BF16, kind="Internal").ap()
    vhbm = nc.dram_tensor("vhbm", [128, 21 * 1024], BF16, kind="Internal").ap()

    with ExitStack() as ctx:
        sb = lambda n, s, dt: ctx.enter_context(nc.sbuf_tensor(pref + n, s, dt))
        psm = lambda n, s, dt=F32: ctx.enter_context(nc.psum_tensor(n, s, dt))

        bias_sb = {p: sb(f"b{p}sb", [128, 8], F32) for p in "qkv"}
        cbias_sb = {p: sb(f"cb{p}sb", [128, 8], F32) for p in "qkv"}
        bo_sb = sb("bosb", [128, 1024], F32)
        br_sb = sb("brsb", [128, 8], F32)
        cst = sb("cst", [128, 8], F32)
        id_bf = sb("idbf", [128, 128], BF16)
        rot_bf = sb("rotbf", [128, 128], BF16)
        grp_bf = sb("grpbf", [128, 2], BF16)
        bc2_bf = sb("bc2bf", [2, 128], BF16)
        on64_bf = sb("on64bf", [128, 64], BF16)
        wr_bf = sb("wrbf", [128, 64], BF16)
        mst = sb("mst", [128, 16], F32)
        if fuse is not None:
            fz = {
                "w_probs": sb("fzwp", [128, 128], F32),
                "probsT_own_sb": sb("fzpt", [8, 2048], F32),
                "splf": sb("fzsplf", [128, 16], F32),
                "splb": sb("fzsplb", [128, 24], BF16),
                "ptf": sb("fzptf", [8, 128], F32),
            }

        if fuse is None:
            pb = [psm(f"pb{i}", [128, 512]) for i in range(6)]
            pt_bf = [psm(f"ptb{i}", [128, 512], BF16) for i in range(2)]
            S = Sched()
            sems = make_sems(nc, ctx)
        else:
            pb, pt_bf = fuse["pb"], fuse["pt_bf"]
            S, sems = fuse["S"], fuse["sems"]
        PB = [f"pb{i}" for i in range(6)]
        PT = ["pt0", "pt1"]

        c1 = cst[:, 1:2]
        c2 = cst[:, 2:3]
        c3 = cst[:, 3:4]

        with ExitStack() as p1:
            sb1 = lambda n, s, dt: p1.enter_context(nc.sbuf_tensor(pref + n, s, dt))
            lnxT = sb1("lnxT", [128, 8 * NTH], BF16)
            stage = sb1("stage", [128, 2048], F32)
            w_bf = sb1("w_bf", [128, 8 * 1024], BF16)
            cd_bf = sb1("cd_bf", [128, 4 * 128 * 8], BF16)
            cos_bf = sb1("cosbf", [128, NTH], BF16)
            sin_bf = sb1("sinbf", [128, NTH], BF16)
            xt = sb1("xt", [128, 2048], F32)
            sqs = sb1("sqs", [128, 2048], F32)
            lnx = sb1("lnx", [128, 2048], BF16)
            pump = []
            if fuse is not None:
                pstg = sb1("pstgA", [128, 2048], F32)
                pbst = sb1("pbstA", [128, 2048], BF16)
                wupi, wdni = fuse["wup_in"], fuse["wdn_in"]
                wupbh, wdbh = fuse["wupb_h"], fuse["wdb_h"]
                ch_up = []
                for gg4 in range(4):
                    for kch9 in range(8):
                        for c0w, c1w in ((0, 910), (910, 1820), (1820, 2730)):
                            ch_up.append((wupi[gg4, kch9 * 128:(kch9 + 1) * 128, c0w:c1w],
                                          wupbh[gg4 * 1024 + kch9 * 128:
                                                gg4 * 1024 + (kch9 + 1) * 128, c0w:c1w],
                                          128, c1w - c0w))
                ch_wd = []
                for e9 in range(8):
                    for mj9 in range(22):
                        mm9 = FCH[mj9]
                        ch_wd.append((wdni[e9, FOF[mj9]:FOF[mj9] + mm9, :],
                                      wdbh[e9 * 2730 + FOF[mj9]:
                                           e9 * 2730 + FOF[mj9] + mm9, :],
                                      mm9, 1024))
                pump = make_pump_jobs(S, pstg, pbst, ch_up, "pA", "wupbH")
                pump += make_pump_jobs(S, pstg, pbst, ch_wd, "pA", "wdbH")
            plin = {b: sb1(f"plin{b}", [128, 8 * 512], BF16) for b in range(2)}
            pc = sb1("pc", [128, 8 * 512], BF16)
            qch = sb1("qch", [128, 8 * 512], BF16)
            vch = sb1("vch", [128, 4 * 1024], BF16)
            tmp1 = sb1("tmp1", [128, 3 * 512], F32)
            tmp2 = sb1("tmp2", [128, 3 * 512], F32)
            tmp3 = sb1("tmp3", [128, 3 * 512], BF16)
            sqb = sb1("sqb", [128, 3 * 512], BF16)
            rs_f = sb1("rs_f", [2, 3 * 512], F32)
            rs_b = sb1("rs_b", [2, 3 * 512], BF16)
            rb_bf = sb1("rb_bf", [128, 3 * 512], BF16)

            for p in "qkv":
                S.dma("SYNC", bias_sb[p][:], bias_in[p], writes=[f"b{p}"])
                S.dma("SYNC", cbias_sb[p][:], cbias_in[p], writes=[f"cb{p}"])
            S.dma("SYNC", bo_sb[:], bo_in, writes=["bo"])
            S.dma("SYNC", br_sb[:], br_in, writes=["br"])
            S.dma("SYNC", cst[:], consts_in, writes=["cst"])

            def conv_in(src_ap, dst_ap, width, key):
                for off in range(0, width, 2048):
                    wdt = min(2048, width - off)
                    sl = stage[:, 0:wdt]
                    S.dma("SYNC", sl, src_ap[:, off:off + wdt], writes=["stage"])
                    S.op("DVE", lambda e, d=dst_ap[:, off:off + wdt], s=sl:
                         e.tensor_copy(d, s), reads=["stage"], writes=[key])

            conv_in(cosT_in, cos_bf[:], NTH, "cos")
            conv_in(sinT_in, sin_bf[:], NTH, "sin")
            conv_in(rotT_in, rot_bf[:], 128, "rot")
            conv_in(ident_in, id_bf[:], 128, "id")
            conv_in(ones64_in, on64_bf[:], 64, "on64")
            conv_in(wr_in, wr_bf[:], 64, "wrk")
            S.dma("SYNC", stage[0:128, 0:2], grp2_in, writes=["stage"])
            S.op("DVE", lambda e: e.tensor_copy(grp_bf[:], stage[0:128, 0:2]),
                 reads=["stage"], writes=["grp"])
            S.dma("SYNC", stage[0:2, 0:128], bc2_in, writes=["stage"])
            S.op("DVE", lambda e: e.tensor_copy(bc2_bf[:], stage[0:2, 0:128]),
                 reads=["stage"], writes=["bc2"])

            for tt in range(21):
                drain_jobs(pump, 2)
                par = tt % 2
                zz = par * 1024
                S.dma("SYNC", xt[:, zz:zz + 1024], xh[tt * 128:(tt + 1) * 128, :],
                      writes=[("xt", par)])
                emit_ln(S, xt[:, zz:zz + 1024], lnx[:, zz:zz + 1024],
                        mst[:, par * 8:(par + 1) * 8], sqs[:, zz:zz + 1024],
                        c3, c2, ("xt", par), ("lnx", par),
                        rk_extra=["cst"], tag=par)
                for half in range(2):
                    pti = half
                    fns = []
                    for q4 in range(4):
                        dch = half * 4 + q4
                        fns.append(lambda e, o=pt_bf[pti][:, q4 * 128:(q4 + 1) * 128],
                                   i=lnx[:, zz + dch * 128: zz + (dch + 1) * 128]:
                                   e.transpose(o, i, id_bf[:]))
                    tok = S.op("TE", fns[0], reads=[("lnx", par), "id"], writes=[PT[pti]])
                    for f in fns[1:]:
                        tok = S.op("TE", f)
                    S.lw[PT[pti]] = tok

                    def ev(e, pt=pt_bf[pti], half=half, tt=tt):
                        src = pt[:, 0:512].rearrange("p (a b) -> p a b", a=4)
                        d = lnxT[:].rearrange("p (a b) -> p a b", a=8, b=NTH)[
                            :, half * 4: half * 4 + 4, tt * 128: tt * 128 + 128]
                        return e.tensor_copy(d, src)
                    S.op("DVE", ev, reads=[PT[pti]], writes=[("lnxT", tt)])
            lnxT_keys = [("lnxT", tt) for tt in range(21)]

            pbi = [0]

            def next_pb():
                i = pbi[0]
                pbi[0] = (i + 1) % 4
                return i

            for p in "qkv":
                conv_in(wT[p], w_bf[:], 8192, "w_bf")
                conv_in(cdiag_in[p], cd_bf[:], 4096, "cd_bf")
                for ci, (n, t0) in enumerate(zip(CH, CH0)):
                    ntile = n // 128
                    buf = ci % 2
                    for cch in range(8):
                        drain_jobs(pump, 2)
                        bi = next_pb()
                        fns = [lambda e, o=pb[bi][:, 0:n],
                               l=w_bf[:, kch * 1024 + cch * 128: kch * 1024 + (cch + 1) * 128],
                               r=lnxT[:, kch * NTH + t0: kch * NTH + t0 + n],
                               st=(kch == 0), sp=(kch == 7):
                               e.matmul(o, l, r, start=st, stop=sp) for kch in range(8)]
                        S.mmg(PB[bi], fns, reads=lnxT_keys + ["w_bf"])
                        S.op("ACT", lambda e, o=plin[buf][:, cch * 512: cch * 512 + n],
                             i=pb[bi][:, 0:n], b=bias_sb[p][:, cch:cch + 1]:
                             e.activation(o, i, AF.Identity, bias=b),
                             reads=[PB[bi], f"b{p}"], writes=[("plin", buf, cch)])
                    prev_buf = 1 - buf
                    prevlen = CH[ci - 1] if ci > 0 else 0
                    for cch in range(8):
                        drain_jobs(pump, 2)
                        bi = next_pb()
                        specs = []
                        for j in [3, 2, 1, 0]:
                            sh = 3 - j
                            l_ap = cd_bf[:, (cch * 4 + j) * 128:(cch * 4 + j + 1) * 128]
                            if n - sh > 0:
                                specs.append((pb[bi][:, sh:n], l_ap,
                                              plin[buf][:, cch * 512: cch * 512 + n - sh]))
                            if sh > 0 and ci > 0:
                                specs.append((pb[bi][:, 0:sh], l_ap,
                                              plin[prev_buf][:, cch * 512 + prevlen - sh:
                                                             cch * 512 + prevlen]))
                        fns = [lambda e, o=o_, l=l_, r=r_, st=(i == 0),
                               sp=(i == len(specs) - 1):
                               e.matmul(o, l, r, start=st, stop=sp)
                               for i, (o_, l_, r_) in enumerate(specs)]
                        rd = [("plin", buf, cch), "cd_bf"]
                        if ci > 0:
                            rd.append(("plin", prev_buf, cch))
                        S.mmg(PB[bi], fns, reads=rd)
                        S.op("ACT", lambda e, o=pc[:, cch * 512: cch * 512 + n],
                             i=pb[bi][:, 0:n], b=cbias_sb[p][:, cch:cch + 1]:
                             e.activation(o, i, AF.Identity, bias=b),
                             reads=[PB[bi], f"cb{p}"], writes=[("pc", cch)])
                    if p in "qk":
                        for base in (0, 3, 6):
                            blk3 = [c for c in range(base, min(base + 3, 8))]
                            for cch in blk3:
                                drain_jobs(pump, 2)
                                par = cch % 3
                                z = par * 512
                                pcsl = pc[:, cch * 512: cch * 512 + n]
                                pck = ("pc", cch)
                                bi = next_pb()
                                S.mmg(PB[bi], [lambda e, o=pb[bi][:, 0:n], r=pcsl:
                                               e.matmul(o, rot_bf[:], r, start=True, stop=True)],
                                      reads=[pck, "rot"])
                                S.op("DVE", lambda e, o=tmp1[:, z:z + n], a=pcsl,
                                     b=cos_bf[:, t0:t0 + n]: e.tensor_mul(o, a, b),
                                     reads=[pck, "cos"], writes=[("tmp1", par)])
                                S.op("ACT", lambda e, o=tmp3[:, z:z + n], i=pb[bi][:, 0:n]:
                                     e.activation(o, i, AF.Copy), reads=[PB[bi]],
                                     writes=[("tmp3", par)])
                                S.op("DVE", lambda e, o=tmp2[:, z:z + n], a=tmp3[:, z:z + n],
                                     b=sin_bf[:, t0:t0 + n]: e.tensor_mul(o, a, b),
                                     reads=[("tmp3", par), "sin"], writes=[("tmp2", par)])
                                S.op("DVE", lambda e, o=pcsl, a=tmp1[:, z:z + n],
                                     b=tmp2[:, z:z + n]:
                                     e.tensor_add(o, a, b),
                                     reads=[("tmp1", par), ("tmp2", par)], writes=[pck])
                                S.op("DVE", lambda e, o=sqb[:, z:z + n], a=pcsl, b=pcsl:
                                     e.tensor_mul(o, a, b), reads=[pck], writes=[("sqb", par)])
                            for cch in blk3:
                                par = cch % 3
                                z = par * 512
                                bi2 = next_pb()
                                S.mmg(PB[bi2], [lambda e, o=pb[bi2][0:2, 0:n], r=sqb[:, z:z + n]:
                                                e.matmul(o, grp_bf[:], r, start=True, stop=True)],
                                      reads=[("sqb", par), "grp"])
                                S.op("ACT", lambda e, o=rs_f[:, z:z + n], i=pb[bi2][0:2, 0:n]:
                                     e.activation(o, i, AF.Sqrt, bias=c1[0:2, :]),
                                     reads=[PB[bi2], "cst"], writes=[("rs_f", par)])
                                S.op("DVE", lambda e, o=rs_f[:, z:z + n]: e.reciprocal(o, o),
                                     reads=[("rs_f", par)], writes=[("rs_f", par)])
                                S.op("DVE", lambda e, o=rs_b[:, z:z + n], i=rs_f[:, z:z + n]:
                                     e.tensor_copy(o, i),
                                     reads=[("rs_f", par)], writes=[("rs_b", par)])
                            for cch in blk3:
                                par = cch % 3
                                z = par * 512
                                pcsl = pc[:, cch * 512: cch * 512 + n]
                                pck = ("pc", cch)
                                bi3 = next_pb()
                                S.mmg(PB[bi3], [lambda e, o=pb[bi3][:, 0:n], r=rs_b[0:2, z:z + n]:
                                                e.matmul(o, bc2_bf[:], r, start=True, stop=True)],
                                      reads=[("rs_b", par), "bc2"])
                                S.op("ACT", lambda e, o=rb_bf[:, z:z + n], i=pb[bi3][:, 0:n]:
                                     e.activation(o, i, AF.Copy), reads=[PB[bi3]],
                                     writes=[("rb_bf", par)])
                                S.op("DVE", lambda e, o=qch[:, cch * 512: cch * 512 + n],
                                     a=pcsl, b=rb_bf[:, z:z + n]: e.tensor_mul(o, a, b),
                                     reads=[pck, ("rb_bf", par)], writes=[("qch", cch)])
                        dst_hbm = qhbm if p == "q" else khbm
                        for cch in range(8):
                            S.dma("SYNC", dst_hbm[:, cch * NTH + t0: cch * NTH + t0 + n],
                                  qch[:, cch * 512: cch * 512 + n],
                                  reads=[("qch", cch)],
                                  writes=[(p + "hbm", ci, cch)])
                    else:
                        for cch in range(8):
                            drain_jobs(pump, 2)
                            for k in range(ntile):
                                pti = k % 2
                                S.op("TE", lambda e, o=pt_bf[pti][:, 0:128],
                                     i=pc[:, cch * 512 + k * 128: cch * 512 + (k + 1) * 128]:
                                     e.transpose(o, i, id_bf[:]),
                                     reads=[("pc", cch), "id"], writes=[PT[pti]])
                                S.op("DVE", lambda e, o=vch[:, k * 1024 + cch * 128:
                                                           k * 1024 + (cch + 1) * 128],
                                     i=pt_bf[pti][:, 0:128]: e.tensor_copy(o, i),
                                     reads=[PT[pti]], writes=[("vch", k, cch)])
                        S.dma("SYNC", vhbm[:, (t0 // 128) * 1024: (t0 // 128 + ntile) * 1024],
                              vch[:, 0: ntile * 1024],
                              reads=[("vch", k_, c_) for k_ in range(ntile) for c_ in range(8)],
                              writes=[("vhbm", ci)])

            drain_jobs(pump, len(pump))
            S.barrier()

        with ExitStack() as p2:
            sb2 = lambda n, s, dt: p2.enter_context(nc.sbuf_tensor(pref + n, s, dt))
            mask_sb = sb2("masksb", [128, 8192], BF16)
            wo_bf = sb2("wobf", [128, 8 * 1024], BF16)
            stage2 = sb2("stage2", [128, 2048], F32)
            qT_sb = sb2("qTsb", [128, 8 * 512], BF16)
            kT_sb = sb2("kTsb", [128, 8 * 1024], BF16)
            vt_sb = sb2("vtsb", [128, 8 * 1024], BF16)
            outT_ch = sb2("outTch", [128, 8 * 512], BF16)
            pT = [sb2(f"pT{i}", [128, 512], BF16) for i in range(2)]
            dnr = sb2("dnr", [128, 512], F32)
            y_sb = sb2("y_sb", [128, 1024], F32)
            lt_sb = sb2("lt_sb", [128, 8 * 128], BF16)
            lg_sb = sb2("lg_sb", [128, 8], F32)
            m_sb = sb2("m_sb", [128, 2], F32)
            xt2 = sb2("xt2", [128, 1024], F32)
            lnx2 = sb2("lnx2", [128, 1024], BF16)
            sqs2 = sb2("sqs2", [128, 1024], F32)

            S.dma("SYNC", mask_sb[:], masks_in, writes=["mask"])
            for off in range(0, 8192, 2048):
                sl = stage2[:, 0:2048]
                S.dma("SYNC", sl, woT[:, off:off + 2048], writes=["stage2"])
                S.op("DVE", lambda e, d=wo_bf[:, off:off + 2048], s=sl:
                     e.tensor_copy(d, s), reads=["stage2"], writes=["wo"])

            spill_keys = ([("qhbm", ci, cc) for ci in range(6) for cc in range(8)] +
                          [("khbm", ci, cc) for ci in range(6) for cc in range(8)] +
                          [("vhbm", ci) for ci in range(6)])
            for qc in range(4):
                qoff = HALO + 512 * qc
                koff0 = qoff - 512
                mset = 1 if qc == 0 else 0
                for cch in range(8):
                    S.dma("SYNC", qT_sb[:, cch * 512:(cch + 1) * 512],
                          qhbm[:, cch * NTH + qoff: cch * NTH + qoff + 512],
                          reads=spill_keys, writes=["qT_sb"])
                    S.dma("SYNC", kT_sb[:, cch * 1024:(cch + 1) * 1024],
                          khbm[:, cch * NTH + koff0: cch * NTH + koff0 + 1024],
                          reads=spill_keys, writes=["kT_sb"])
                S.dma("SYNC", vt_sb[:], vhbm[:, (koff0 // 128) * 1024:
                                              (koff0 // 128 + 8) * 1024],
                      reads=spill_keys, writes=["vt_sb"])

                for pair in range(8):
                    last_mm = {}
                    for m in range(8):
                        ktt = m
                        for hh in range(2):
                            h = pair * 2 + hh
                            cch = h // 2
                            rb = 64 * (h % 2)
                            sbk = hh
                            ob = 2 + hh
                            db_ = 4 + hh
                            fns = [
                                lambda e, o=pb[sbk][:, 0:512],
                                l=kT_sb[rb:rb + 64, cch * 1024 + m * 128: cch * 1024 + (m + 1) * 128],
                                r=qT_sb[rb:rb + 64, cch * 512: (cch + 1) * 512]:
                                e.matmul(o, l, r, start=True, stop=False),
                                lambda e, o=pb[sbk][:, 0:512],
                                r=mask_sb[:, (mset * 8 + m) * 512:(mset * 8 + m + 1) * 512]:
                                e.matmul(o, id_bf[:], r, start=False, stop=True),
                            ]
                            S.mmg(PB[sbk], fns, reads=["kT_sb", "qT_sb", "mask", "id"])
                            pti = hh
                            S.op("ACT", lambda e, o=pT[pti][:], i=pb[sbk][:, 0:512]:
                                 e.activation(o, i, AF.Exp, scale=cst[:, 0:1]),
                                 reads=[PB[sbk], "cst"], writes=[("pT", pti)])
                            tok_o = S.op("TE", lambda e, o=pb[ob][rb:rb + 64, 0:512],
                                         l=vt_sb[:, ktt * 1024 + h * 64: ktt * 1024 + h * 64 + 64],
                                         r=pT[pti][:], st=(m == 0), sp=(m == 7), tp=(0, rb):
                                         e.matmul(o, l, r, start=st, stop=sp, tile_position=tp),
                                         reads=[("pT", pti), "vt_sb"] +
                                               ([PB[ob]] if m == 0 else []),
                                         writes=[PB[ob]] if m == 0 else [])
                            tok_d = S.op("TE", lambda e, o=pb[db_][rb:rb + 64, 0:512],
                                         l=on64_bf[:], r=pT[pti][:], st=(m == 0), sp=(m == 7),
                                         tp=(0, rb):
                                         e.matmul(o, l, r, start=st, stop=sp, tile_position=tp),
                                         reads=[("pT", pti), "on64"] +
                                               ([PB[db_]] if m == 0 else []),
                                         writes=[PB[db_]] if m == 0 else [])
                            last_mm[ob] = tok_o
                            last_mm[db_] = tok_d
                    for bk, tk in last_mm.items():
                        S.lw[PB[bk]] = tk
                    for hh in range(2):
                        ob, db_ = 2 + hh, 4 + hh
                        rb = 64 * hh
                        S.op("DVE", lambda e, o=dnr[rb:rb + 64, :], i=pb[db_][rb:rb + 64, 0:512]:
                             e.reciprocal(o, i), reads=[PB[db_]], writes=[("dnr", hh)])
                        S.op("DVE", lambda e, o=outT_ch[rb:rb + 64, pair * 512:(pair + 1) * 512],
                             a=pb[ob][rb:rb + 64, 0:512], b=dnr[rb:rb + 64, :]:
                             e.tensor_mul(o, a, b),
                             reads=[PB[ob], ("dnr", hh)], writes=[("outT", pair, hh)])
                for tt in range(4):
                    trow = qc * 512 + tt * 128
                    S.dma("SYNC", xt2[:], xh[HALO + trow: HALO + trow + 128, :],
                          writes=["xt2"])
                    for dmc in range(2):
                        bi = dmc
                        fns = [lambda e, o=pb[bi][:, 0:512],
                               l=outT_ch[:, kch * 512 + tt * 128: kch * 512 + tt * 128 + 128],
                               r=wo_bf[:, kch * 1024 + dmc * 512: kch * 1024 + dmc * 512 + 512],
                               st=(kch == 0), sp=(kch == 7):
                               e.matmul(o, l, r, start=st, stop=sp) for kch in range(8)]
                        S.mmg(PB[bi], fns,
                              reads=[("outT", pr, hh_) for pr in range(8)
                                     for hh_ in range(2)] + ["wo"])
                        S.op("DVE", lambda e, o=y_sb[:, dmc * 512:(dmc + 1) * 512],
                             a=pb[bi][:, 0:512], b=xt2[:, dmc * 512:(dmc + 1) * 512]:
                             e.tensor_add(o, a, b),
                             reads=[PB[bi], "xt2"], writes=[("y", dmc)])
                        S.op("DVE", lambda e, o=y_sb[:, dmc * 512:(dmc + 1) * 512],
                             a=y_sb[:, dmc * 512:(dmc + 1) * 512],
                             b=bo_sb[:, dmc * 512:(dmc + 1) * 512]: e.tensor_add(o, a, b),
                             reads=[("y", dmc), "bo"], writes=[("y", dmc)])
                    S.dma("SYNC", x1_out[trow: trow + 128, :], y_sb[:],
                          reads=[("y", 0), ("y", 1)])
                    emit_ln(S, y_sb[:], lnx2[:], mst, sqs2[:], c3, c2,
                            [("y", 0), ("y", 1)], "lnx2", rk_extra=["cst"])
                    for half in range(2):
                        pti = half
                        fns = []
                        for q4 in range(4):
                            dch = half * 4 + q4
                            fns.append(lambda e, o=pt_bf[pti][:, q4 * 128:(q4 + 1) * 128],
                                       i=lnx2[:, dch * 128:(dch + 1) * 128]:
                                       e.transpose(o, i, id_bf[:]))
                        tok = S.op("TE", fns[0], reads=["lnx2", "id"], writes=[PT[pti]])
                        for f in fns[1:]:
                            tok = S.op("TE", f)
                        S.lw[PT[pti]] = tok
                        S.op("DVE", lambda e, pt=pt_bf[pti], half=half:
                             e.tensor_copy(lt_sb[:, half * 512: half * 512 + 512], pt[:, 0:512]),
                             reads=[PT[pti]], writes=[("lt", half)])
                    bi = 2
                    fns = [lambda e, o=pb[bi][:, 0:8],
                           l=lt_sb[:, kch * 128:(kch + 1) * 128],
                           r=wr_bf[:, kch * 8:(kch + 1) * 8],
                           st=(kch == 0), sp=(kch == 7):
                           e.matmul(o, l, r, start=st, stop=sp) for kch in range(8)]
                    S.mmg(PB[bi], fns, reads=[("lt", 0), ("lt", 1), "wrk"])
                    S.op("DVE", lambda e: e.tensor_add(lg_sb[:], pb[2][:, 0:8], br_sb[:]),
                         reads=[PB[2], "br"], writes=["lg"])
                    S.op("DVE", lambda e: e.reduce_max(m_sb[:, 0:1], lg_sb[:], axis=AX.X),
                         reads=["lg"], writes=["msb"])
                    S.op("DVE", lambda e: e.tensor_scalar(m_sb[:, 1:2], m_sb[:, 0:1], -1.0,
                                                          None, ALU.mult),
                         reads=["msb"], writes=["msb"])
                    S.op("ACT", lambda e: e.activation(lg_sb[:], lg_sb[:], AF.Exp,
                                                       bias=m_sb[:, 1:2]),
                         reads=["lg", "msb"], writes=["lg"])
                    S.op("DVE", lambda e: e.reduce_sum(m_sb[:, 0:1], lg_sb[:], axis=AX.X),
                         reads=["lg"], writes=["msb"])
                    S.op("DVE", lambda e: e.reciprocal(m_sb[:, 0:1], m_sb[:, 0:1]),
                         reads=["msb"], writes=["msb"])
                    S.op("DVE", lambda e: e.tensor_scalar(lg_sb[:], lg_sb[:], m_sb[:, 0:1],
                                                          None, ALU.mult),
                         reads=["lg", "msb"], writes=["lg"])
                    tile_i = qc * 4 + tt
                    if fuse is None:
                        S.dma("SYNC", probs_out[trow: trow + 128, :], lg_sb[:],
                              reads=["lg"])
                    else:
                        emit_probs_fused(S, fz, lg_sb, pt_bf, PT, id_bf, tile_i)

            if fuse is not None:
                S.dma("SYNC", fuse["pTown_h"], fz["probsT_own_sb"][:],
                      reads=[("pTown", t_) for t_ in range(16)], writes=["pTownH"])
                S.dma("SYNC", fuse["wprobs_h"], fz["w_probs"][:],
                      reads=[("wp", t_) for t_ in range(16)], writes=["wpH"])
            S.op("SYNC", lambda e: e.nop(), waits=S.now("QSP"), q="QSP", n=16)
            S.barrier()

        if fuse is None:
            blk = ctx.enter_context(nc.Block())
            S.replay(blk, sems)

    return nc


def emit_probs_fused(S, fuse, lg_sb, pt_bf, PT, id_bf, tile_i):
    w_probs = fuse["w_probs"]
    pTown = fuse["probsT_own_sb"]
    splf = fuse["splf"]
    splb = fuse["splb"]
    ptf = fuse["ptf"]
    S.op("DVE", lambda e, o=w_probs[:, tile_i * 8:(tile_i + 1) * 8]:
         e.tensor_copy(o, lg_sb[:]), reads=["lg"], writes=[("wp", tile_i)])
    S.op("DVE", lambda e: e.tensor_copy(splb[:, 0:8], lg_sb[:]),
         reads=["lg"], writes=["splb"])
    S.op("DVE", lambda e: e.tensor_copy(splf[:, 0:8], splb[:, 0:8]),
         reads=["splb"], writes=["splf"])
    S.op("DVE", lambda e: e.tensor_sub(splf[:, 8:16], lg_sb[:], splf[:, 0:8]),
         reads=["lg", "splf"], writes=["splf"])
    S.op("DVE", lambda e: e.tensor_copy(splb[:, 8:16], splf[:, 8:16]),
         reads=["splf"], writes=["splb"])
    S.op("DVE", lambda e: e.tensor_copy(splf[:, 0:8], splb[:, 8:16]),
         reads=["splb"], writes=["splf"])
    S.op("DVE", lambda e: e.tensor_sub(splf[:, 8:16], splf[:, 8:16], splf[:, 0:8]),
         reads=["splf"], writes=["splf"])
    S.op("DVE", lambda e: e.tensor_copy(splb[:, 16:24], splf[:, 8:16]),
         reads=["splf"], writes=["splb"])
    for j in range(3):
        pti = j % 2
        S.op("TE", lambda e, o=pt_bf[pti][0:8, 0:128], i=splb[:, j * 8:(j + 1) * 8]:
             e.transpose(o, i, id_bf[:]),
             reads=["splb", "id"], writes=[PT[pti]])
        if j == 0:
            S.op("ACT", lambda e, o=ptf[0:8, 0:128], i=pt_bf[pti][0:8, 0:128]:
                 e.activation(o, i, AF.Identity),
                 reads=[PT[pti]], writes=["ptf"])
        else:
            S.op("DVE", lambda e, o=ptf[0:8, 0:128], i=pt_bf[pti][0:8, 0:128]:
                 e.tensor_add(o, o, i),
                 reads=[PT[pti], "ptf"], writes=["ptf"])
    S.op("DVE", lambda e, o=pTown[0:8, tile_i * 128:(tile_i + 1) * 128]:
         e.tensor_copy(o, ptf[0:8, 0:128]),
         reads=["ptf"], writes=[("pTown", tile_i)])


FCH = [128] * 21 + [42]
FOF = [sum(FCH[:i]) for i in range(22)]


def hostB(inp, core, probs_all):
    if probs_all is None:
        probs_all = np.zeros((16384, 8), np.float32)
    sel64T = np.zeros((64, 8), np.float32)
    sel8 = np.zeros((8, 64), np.float32)
    for r in range(64):
        sel64T[r, r % 8] = 1.0
        sel8[r % 8, r] = 1.0
    pT = probs_all.T.reshape(8, 8, 2048)
    segs = [(s + core) % 8 for s in range(8)]
    pr = np.zeros((64, 2048), np.float32)
    for sp in range(8):
        for e in range(8):
            pr[sp * 8 + e] = pT[e, segs[sp]]
    bu = np.zeros((128, 4 * 22), np.float32)
    b_up = _f32(inp["L0_ffn_b_up"])
    for gg in range(4):
        pad = np.zeros(2816, np.float32)
        pad[:2730] = b_up[gg]
        bu[:, gg * 22:(gg + 1) * 22] = pad.reshape(22, 128).T
    return {
        "x1": np.zeros((NT, DM), np.float32),
        "probs_own": np.zeros((NT, 8), np.float32),
        "probsT_r": pr,
        "sel64T": sel64T, "sel8": sel8,
        "wup": _f32(inp["L0_ffn_W_up"]),
        "bu": bu,
        "wdn": _f32(inp["L0_ffn_W_down"]),
        "bd_row": _f32(inp["L0_ffn_b_down"]).reshape(1, 8 * 1024),
        "ones_row": np.ones((1, 128), np.float32),
        "wkvT": _kc(_f32(inp["L1_attn_Wkv_down"]).T),
        "identB": np.eye(128, dtype=np.float32),
        "constsB": np.tile(np.array(
            [0.5, 1e-12, 1e-5, 1.0 / 1024, 2048.0, 1.0, -1.0, 1e-6], np.float32),
            (128, 1)),
    }


def build_B(fuse=None):
    nc = bass.Bass("TRN2", target_bir_lowering=False) if fuse is None else fuse["nc"]
    pref = "" if fuse is None else "b_"
    g = lambda n, s, dt=F32: nc.dram_tensor(n, s, dt, kind="ExternalInput").ap()
    if fuse is None:
        x1_in = g("x1", [NT, DM])
        probs_own = g("probs_own", [NT, 8])
        probsT_r = g("probsT_r", [64, 2048])
    else:
        x1_in = fuse["x1_hbm"]
        probs_own = None
        probsT_r = fuse["pts_hbm"]
    sel64T_in = g("sel64T", [64, 8])
    sel8_in = g("sel8", [8, 64])
    wup_in = fuse["wup_in"] if fuse else g("wup", [4, 1024, 2730])
    bu_in = g("bu", [128, 88])
    wdn_in = fuse["wdn_in"] if fuse else g("wdn", [8, 2730, 1024])
    bd_in = g("bd_row", [1, 8192])
    ones_in = g("ones_row", [1, 128])
    wkvT_in = g("wkvT", [128, 8 * 64])
    id_in = g("identB", [128, 128])
    cst_in = g("constsB", [128, 8])

    if fuse is None:
        x2_out = nc.dram_tensor("x2", [NT, DM], F32, kind="ExternalOutput").ap()
        lat_out = nc.dram_tensor("latent", [64, NT], F32, kind="ExternalOutput").ap()
        lo_dbg = nc.dram_tensor("lo_dbg", [64, 1], F32, kind="ExternalOutput").ap()
    else:
        x2_out = fuse["x2_hbm"]
        lat_out = fuse["lat_own_hbm"]
        lo_dbg = nc.dram_tensor("lo_dbg", [64, 1], F32, kind="Internal").ap()

    lnT_hbm = nc.dram_tensor("lnT_hbm", [128, 8 * NT], BF16, kind="Internal").ap()
    yg_hbm = nc.dram_tensor("yg_hbm", [4 * NT, DM], F32, kind="Internal").ap()

    with ExitStack() as ctx:
        sb = lambda n, s, dt: ctx.enter_context(nc.sbuf_tensor(pref + n, s, dt))
        psm = lambda n, s, dt=F32: ctx.enter_context(nc.psum_tensor(n, s, dt))

        cst = sb("cst", [128, 8], F32)
        id_bf = sb("idbf", [128, 128], BF16)
        ones_bf = sb("onesbf", [1, 128], BF16)
        bd_bf = sb("bdbf", [1, 8192], BF16)
        bu_sb = sb("busb", [128, 88], F32)
        mst = sb("mst", [128, 16], F32)
        w_tok = sb("w_tok", [128, 16 * 8], F32)
        wkv_bf = sb("wkvbf", [128, 8 * 64], BF16)

        if fuse is None:
            pb = [psm(f"pb{i}", [128, 512]) for i in range(6)]
            pt_bf = [psm(f"ptb{i}", [128, 512], BF16) for i in range(2)]
            S = Sched()
            sems = make_sems(nc, ctx)
        else:
            pb, pt_bf = fuse["pb"], fuse["pt_bf"]
            S, sems = fuse["S"], fuse["sems"]
        PB = [f"pb{i}" for i in range(6)]
        PT = ["pt0", "pt1"]

        c2 = cst[:, 2:3]
        c3 = cst[:, 3:4]

        S.dma("SYNC", cst[:], cst_in, writes=["cst"])
        S.dma("SYNC", bu_sb[:], bu_in, writes=["bu"])

        with ExitStack() as pA:
            sbA = lambda n, s, dt: pA.enter_context(nc.sbuf_tensor(pref + n, s, dt))
            stage = sbA("stage", [128, 2730], F32)
            xt = sbA("xt", [128, 2048], F32)
            sqs = sbA("sqs", [128, 2048], F32)
            lnx = sbA("lnx", [128, 2048], BF16)
            lnTfull = sbA("lnTfull", [128, 8 * NT], BF16)
            pts = sbA("pts", [64, 2048], F32)
            lo = sbA("lo", [64, 1], F32)
            hi = sbA("hi", [64, 1], F32)
            th = sbA("th", [64, 1], F32)
            cnt64 = sbA("cnt64", [64, 2048], F32)
            c64 = sbA("c64", [64, 2], F32)
            mask8 = sbA("mask8", [8, 2], BF16)
            maskR = sbA("maskR", [64, 2], F32)
            wsel = sbA("wsel", [8, 2048], BF16)
            mtmp = sbA("mtmp", [128, 8], F32)
            sel64T_sb = sbA("sel64Ts", [64, 8], F32)
            sel8_bf = sbA("sel8b", [8, 64], BF16)

            def conv_small(src_ap, dst_ap, rows, cols, key):
                S.dma("SYNC", stage[0:rows, 0:cols], src_ap, writes=["stage"])
                S.op("DVE", lambda e, d=dst_ap, s=stage[0:rows, 0:cols]:
                     e.tensor_copy(d, s), reads=["stage"], writes=[key])

            S.dma("SYNC", sel64T_sb[:], sel64T_in, writes=["sel64T"])
            S.dma("SYNC", pts[:], probsT_r, reads=["agp"], writes=["pts"])
            conv_small(id_in, id_bf[:], 128, 128, "id")
            conv_small(ones_in, ones_bf[:], 1, 128, "ones")
            conv_small(sel8_in, sel8_bf[:], 8, 64, "sel8")
            conv_small(wkvT_in, wkv_bf[:], 128, 512, "wkv")
            for off in range(0, 8192, 2048):
                conv_small(bd_in[:, off:off + 2048], bd_bf[:, off:off + 2048],
                           1, 2048, "bd")

            S.op("DVE", lambda e: e.memset(lo[:], 0.0), writes=["lo"])
            S.op("DVE", lambda e: e.memset(hi[:], 1.0), writes=["hi"])

            def emit_bisect_iter():
                S.op("DVE", lambda e: e.tensor_add(th[:], lo[:], hi[:]),
                     reads=["lo", "hi"], writes=["th"])
                S.op("DVE", lambda e: e.tensor_scalar(th[:], th[:], 0.5, None, ALU.mult),
                     reads=["th"], writes=["th"])
                S.op("DVE", lambda e: e.tensor_scalar(cnt64[:], pts[:], th[:], None, ALU.is_ge),
                     reads=["pts", "th"], writes=["cnt"])
                S.op("DVE", lambda e: e.reduce_sum(c64[:, 0:1], cnt64[:], axis=AX.X),
                     reads=["cnt"], writes=["c64"])
                S.mmg(PB[4], [lambda e: e.matmul(pb[4][0:8, 0:1], sel64T_sb[:], c64[:, 0:1],
                                                 start=True, stop=True)],
                      reads=["c64", "sel64T"])
                S.op("DVE", lambda e: e.tensor_scalar(mask8[:, 0:1], pb[4][0:8, 0:1],
                                                      2048.0, None, ALU.is_ge),
                     reads=[PB[4]], writes=["mask8"])
                S.op("DVE", lambda e: e.tensor_scalar(mask8[:, 1:2], pb[4][0:8, 0:1],
                                                      2048.0, None, ALU.is_lt),
                     reads=[PB[4]], writes=["mask8"])
                S.mmg(PB[5], [lambda e: e.matmul(pb[5][0:64, 0:2], sel8_bf[:], mask8[:],
                                                 start=True, stop=True)],
                      reads=["mask8", "sel8"])
                S.op("ACT", lambda e: e.activation(maskR[:], pb[5][0:64, 0:2], AF.Identity),
                     reads=[PB[5]], writes=["maskR"])
                S.op("DVE", lambda e: e.tensor_mul(c64[:, 0:1], th[:], maskR[:, 0:1]),
                     reads=["maskR", "th"], writes=["c64"])
                S.op("DVE", lambda e: e.scalar_tensor_tensor(
                    lo[:], lo[:], maskR[:, 1:2], c64[:, 0:1], ALU.mult, ALU.add),
                     reads=["maskR", "lo", "c64"], writes=["lo"])
                S.op("DVE", lambda e: e.tensor_mul(c64[:, 1:2], th[:], maskR[:, 1:2]),
                     reads=["maskR", "th"], writes=["c64b"])
                S.op("DVE", lambda e: e.scalar_tensor_tensor(
                    hi[:], hi[:], maskR[:, 0:1], c64[:, 1:2], ALU.mult, ALU.add),
                     reads=["maskR", "hi", "c64b"], writes=["hi"])

            bis_left = [40]

            def run_bisect(k):
                n = min(k, bis_left[0])
                for _ in range(n):
                    emit_bisect_iter()
                bis_left[0] -= n

            for tt in range(16):
                par = tt % 2
                S.dma("SYNC", xt[:, par * 1024:(par + 1) * 1024],
                      x1_in[tt * 128:(tt + 1) * 128, :], writes=[("xt", par)])
                emit_ln(S, xt[:, par * 1024:(par + 1) * 1024],
                        lnx[:, par * 1024:(par + 1) * 1024],
                        mst[:, par * 8:(par + 1) * 8], sqs[:, par * 1024:(par + 1) * 1024],
                        c3, c2, ("xt", par), ("lnx", par),
                        rk_extra=["cst"], tag=par)
                for half in range(2):
                    pti = half
                    fns = []
                    for q4 in range(4):
                        dch = half * 4 + q4
                        fns.append(lambda e, o=pt_bf[pti][:, q4 * 128:(q4 + 1) * 128],
                                   i=lnx[:, par * 1024 + dch * 128: par * 1024 + (dch + 1) * 128]:
                                   e.transpose(o, i, id_bf[:]))
                    tok = S.op("TE", fns[0], reads=[("lnx", par), "id"], writes=[PT[pti]])
                    for f in fns[1:]:
                        tok = S.op("TE", f)
                    S.lw[PT[pti]] = tok

                    def evb(e, pt=pt_bf[pti], half=half, tt=tt):
                        src_ = pt[:, 0:512].rearrange("p (a b) -> p a b", a=4)
                        d = lnTfull[:].rearrange("p (a b) -> p a b", a=8, b=NT)[
                            :, half * 4: half * 4 + 4, tt * 128: tt * 128 + 128]
                        return e.tensor_copy(d, src_)
                    S.op("DVE", evb, reads=[PT[pti]], writes=[("lnf", tt, half)])
                run_bisect(3)
            for kch in range(8):
                S.dma("SYNC", lnT_hbm[:, kch * NT: (kch + 1) * NT],
                      lnTfull[:, kch * NT: (kch + 1) * NT],
                      reads=[("lnf", t_, h_) for t_ in range(16) for h_ in range(2)],
                      writes=[("lnhbm", 0, kch)])
            run_bisect(40)
            S.dma("SYNC", lo_dbg, lo[:], reads=["lo"])
            if fuse is not None:
                b_wpr = pA.enter_context(nc.sbuf_tensor("b_wpr", [128, 128], F32))
                S.dma("SYNC", b_wpr[:], fuse["wprobs_h"],
                      reads=["wpH"], writes=["bwpr"])
            if fuse is None:
                own_src = pts[0:8, :]
                own_keys = ["pts"]
            else:
                b_ptown = pA.enter_context(nc.sbuf_tensor("b_ptown", [8, 2048], F32))
                S.dma("SYNC", b_ptown[:], fuse["pTown_h"],
                      reads=["pTownH"], writes=["bptown"])
                own_src = b_ptown[0:8, :]
                own_keys = ["bptown"]
            S.op("DVE", lambda e, i=own_src: e.tensor_scalar(wsel[:], i, lo[0:8, 0:1],
                                                             None, ALU.is_ge),
                 reads=own_keys + ["lo"], writes=["wsel"])
            for tt in range(16):
                S.op("TE", lambda e, o=pt_bf[tt % 2][0:128, 0:8],
                     i=wsel[:, tt * 128:(tt + 1) * 128]:
                     e.transpose(o, i, id_bf[0:8, 0:8]),
                     reads=["wsel", "id"], writes=[PT[tt % 2]])
                S.op("ACT", lambda e, o=mtmp[:], i=pt_bf[tt % 2][0:128, 0:8]:
                     e.activation(o, i, AF.Identity),
                     reads=[PT[tt % 2]], writes=["mtmp"])
                if fuse is None:
                    S.dma("SYNC", xt[:, 0:8], probs_own[tt * 128:(tt + 1) * 128, :],
                          writes=[("xt", 0)])
                    S.op("DVE", lambda e, o=w_tok[:, tt * 8:(tt + 1) * 8], a=mtmp[:],
                         tt=tt: e.tensor_mul(o, a, xt[:, 0:8]),
                         reads=["mtmp", ("xt", 0)], writes=[("w_tok", tt)])
                else:
                    S.op("DVE", lambda e, o=w_tok[:, tt * 8:(tt + 1) * 8], a=mtmp[:],
                         b=b_wpr[:, tt * 8:(tt + 1) * 8]:
                         e.tensor_mul(o, a, b),
                         reads=["mtmp", "bwpr"], writes=[("w_tok", tt)])

            S.barrier()

        lnkeys = [("lnhbm", 0, kch) for kch in range(8)]
        with ExitStack() as pB:
            sbB = lambda n, s, dt: pB.enter_context(nc.sbuf_tensor(pref + n, s, dt))
            wup_bf = sbB("wupbf", [128, 8 * 2730], BF16)
            wd_bf = {ee: sbB(f"wdbf{ee}", [128, 22 * 1024], BF16) for ee in range(2)}
            if fuse is None:
                stg = sbB("stg", [128, 2 * 910], F32)
                stg2 = sbB("stg2", [128, 2 * 512], F32)
            midT = sbB("midT", [128, 22 * 512], BF16)
            lnT_sb = sbB("lnTsb", [128, 2 * 8 * 512], BF16)
            y_sb = sbB("ysb", [128, 2048], F32)
            pump2 = []
            if fuse is not None:
                pstg2 = sbB("pstgB", [128, 2048], F32)
                pbst2 = sbB("pbstB", [128, 2048], BF16)
                gupi, dni = fuse["gupT_in"], fuse["dnT_in"]
                gupbh, dnbh = fuse["gupb_h"], fuse["dnb_h"]
                ch_g = []
                for e9 in range(8):
                    for mjp9 in range(11):
                        r0 = (e9 * 11 + mjp9) * 128
                        for c49 in range(4):
                            ch_g.append((gupi[e9, mjp9, :, c49 * 1024:(c49 + 1) * 1024],
                                         gupbh[r0:r0 + 128, c49 * 1024:(c49 + 1) * 1024],
                                         128, 1024))
                ch_d = []
                for e9 in range(8):
                    for mj9 in range(22):
                        ch_d.append((dni[e9, mj9 * 128:(mj9 + 1) * 128, :],
                                     dnbh[e9 * 2816 + mj9 * 128:
                                          e9 * 2816 + (mj9 + 1) * 128, :],
                                     128, 1024))
                pump2 = make_pump_jobs(S, pstg2, pbst2, ch_g, "pB", "gupbH")
                pump2 += make_pump_jobs(S, pstg2, pbst2, ch_d, "pB", "dnbH")

            def wup_jobs(gg, spread):
                jobs = []
                if fuse is not None:
                    wupbh_ = fuse["wupb_h"]
                    for kch in range(8):
                        for c0_, c1_ in ((0, 910), (910, 1820), (1820, 2730)):
                            def j1(kch=kch, c0_=c0_, c1_=c1_, gg=gg):
                                S.dma("SYNC",
                                      wup_bf[:, kch * 2730 + c0_: kch * 2730 + c1_],
                                      wupbh_[gg * 1024 + kch * 128:
                                             gg * 1024 + (kch + 1) * 128, c0_:c1_],
                                      reads=["wupbH"], writes=["wup"])
                            jobs.append(j1)
                    return jobs
                pi = [0]
                for kch in range(8):
                    for c0_, c1_ in ((0, 910), (910, 1820), (1820, 2730)):
                        par = pi[0] % 2
                        pi[0] += 1
                        o0 = par * 910

                        def j1(kch=kch, c0_=c0_, c1_=c1_, par=par, o0=o0, gg=gg):
                            S.dma("SYNC", stg[:, o0:o0 + c1_ - c0_],
                                  wup_in[gg, kch * 128:(kch + 1) * 128, c0_:c1_],
                                  writes=[("stg", par)])

                        def j2(kch=kch, c0_=c0_, c1_=c1_, par=par, o0=o0):
                            eng = "ACT" if (spread or par == 1) else "DVE"
                            if eng == "DVE":
                                S.op("DVE", lambda e, d=wup_bf[:, kch * 2730 + c0_: kch * 2730 + c1_],
                                     s=stg[:, o0:o0 + c1_ - c0_]: e.tensor_copy(d, s),
                                     reads=[("stg", par)], writes=["wup"])
                            else:
                                S.op("ACT", lambda e, d=wup_bf[:, kch * 2730 + c0_: kch * 2730 + c1_],
                                     s=stg[:, o0:o0 + c1_ - c0_]: e.activation(d, s, AF.Identity),
                                     reads=[("stg", par)], writes=["wup"])
                        jobs += [j1, j2]
                return jobs

            for gg in range(4):
                if gg == 0:
                    for j in wup_jobs(0, spread=False):
                        j()
                for ee in range(2):
                    e_ = gg * 2 + ee
                    for mj in range(22):
                        mm = FCH[mj]
                        if fuse is not None:
                            S.dma("ACT", wd_bf[ee][0:mm, mj * 1024:(mj + 1) * 1024],
                                  fuse["wdb_h"][e_ * 2730 + FOF[mj]:
                                                e_ * 2730 + FOF[mj] + mm, :],
                                  reads=["wdbH"], writes=[("wd", ee)])
                            continue
                        for pc in range(2):
                            par = (mj * 2 + pc) % 2
                            o0 = par * 512
                            S.dma("ACT", stg2[0:mm, o0:o0 + 512],
                                  wdn_in[e_, FOF[mj]:FOF[mj] + mm, pc * 512:(pc + 1) * 512],
                                  writes=[("stg2", par)])
                            if par == 0:
                                S.op("DVE", lambda e, d=wd_bf[ee][0:mm, mj * 1024 + pc * 512: mj * 1024 + (pc + 1) * 512],
                                     s=stg2[0:mm, o0:o0 + 512]: e.tensor_copy(d, s),
                                     reads=[("stg2", par)], writes=[("wd", ee)])
                            else:
                                S.op("ACT", lambda e, d=wd_bf[ee][0:mm, mj * 1024 + pc * 512: mj * 1024 + (pc + 1) * 512],
                                     s=stg2[0:mm, o0:o0 + 512]: e.activation(d, s, AF.Identity),
                                     reads=[("stg2", par)], writes=[("wd", ee)])
                pend = wup_jobs(gg + 1, spread=True) if gg < 3 else []
                for tc in range(4):
                    lpar = tc % 2
                    lz = lpar * 4096
                    if gg == 0 and tc == 0:
                        for kch in range(8):
                            S.dma("SYNC", lnT_sb[:, lz + kch * 512: lz + (kch + 1) * 512],
                                  lnT_hbm[:, kch * NT + tc * 512: kch * NT + tc * 512 + 512],
                                  reads=lnkeys, writes=[("lnT_sb", lpar)])
                    for mj in range(22):
                        drain_jobs(pump2, 2)
                        mm = FCH[mj]
                        bi = mj % 2
                        fns = [lambda e, o=pb[bi][0:mm, 0:512],
                               l=wup_bf[:, kch * 2730 + FOF[mj]: kch * 2730 + FOF[mj] + mm],
                               r=lnT_sb[:, lz + kch * 512: lz + (kch + 1) * 512],
                               st=(kch == 0), sp=(kch == 7):
                               e.matmul(o, l, r, start=st, stop=sp) for kch in range(8)]
                        S.mmg(PB[bi], fns, reads=["wup", ("lnT_sb", lpar)])
                        S.op("ACT", lambda e, o=midT[0:mm, mj * 512:(mj + 1) * 512],
                             i=pb[bi][0:mm, 0:512], b=bu_sb[0:mm, gg * 22 + mj: gg * 22 + mj + 1]:
                             e.activation(o, i, AF.Silu, bias=b),
                             reads=[PB[bi], "bu"], writes=[("midT", mj)])
                    if not (gg == 3 and tc == 3):
                        drain_jobs(pump2, 4)
                        ntc = (tc + 1) % 4
                        nlz = (ntc % 2) * 4096
                        for kch in range(8):
                            S.dma("SYNC", lnT_sb[:, nlz + kch * 512: nlz + (kch + 1) * 512],
                                  lnT_hbm[:, kch * NT + ntc * 512: kch * NT + ntc * 512 + 512],
                                  reads=lnkeys, writes=[("lnT_sb", ntc % 2)])
                    for tt in range(4):
                        gt = tc * 4 + tt
                        yp = tt % 2
                        for dmc in range(2):
                            for ee in range(2):
                                drain_jobs(pump2, 3)
                                if tc == 3:
                                    for _ in range(3):
                                        if pend:
                                            pend.pop(0)()
                                e_ = gg * 2 + ee
                                bi = 2 + ee
                                fns = []
                                for mj in range(22):
                                    mm = FCH[mj]
                                    fns.append(lambda e, o=pb[bi][:, 0:512],
                                               l=midT[0:mm, mj * 512 + tt * 128: mj * 512 + (tt + 1) * 128],
                                               r=wd_bf[ee][0:mm, mj * 1024 + dmc * 512: mj * 1024 + dmc * 512 + 512],
                                               st=(mj == 0): e.matmul(o, l, r, start=st, stop=False))
                                fns.append(lambda e, o=pb[bi][:, 0:512],
                                           l=ones_bf[:],
                                           r=bd_bf[:, e_ * 1024 + dmc * 512: e_ * 1024 + dmc * 512 + 512]:
                                           e.matmul(o, l, r, start=False, stop=True))
                                S.mmg(PB[bi], fns,
                                      reads=[("midT", mj_) for mj_ in range(22)] +
                                            [("wd", ee), "bd", "ones"])
                                wap = w_tok[:, gt * 8 + e_: gt * 8 + e_ + 1]
                                ysl = y_sb[:, (yp * 2 + dmc) * 512: (yp * 2 + dmc + 1) * 512]
                                if ee == 0:
                                    S.op("DVE", lambda e, o=ysl, i=pb[bi][:, 0:512], w=wap:
                                         e.tensor_scalar(o, i, w, None, ALU.mult),
                                         reads=[PB[bi], ("w_tok", gt)],
                                         writes=[("y", yp, dmc)])
                                else:
                                    S.op("DVE", lambda e, o=ysl, i=pb[bi][:, 0:512], w=wap:
                                         e.scalar_tensor_tensor(o, i, w, o, ALU.mult, ALU.add),
                                         reads=[PB[bi], ("w_tok", gt), ("y", yp, dmc)],
                                         writes=[("y", yp, dmc)])
                        S.dma("SYNC", yg_hbm[gg * NT + gt * 128: gg * NT + (gt + 1) * 128, :],
                              y_sb[:, yp * 1024:(yp + 1) * 1024],
                              reads=[("y", yp, 0), ("y", yp, 1)],
                              writes=[("yg", gg, gt)])
                    if tc == 3:
                        while pend:
                            pend.pop(0)()
            drain_jobs(pump2, len(pump2))
            S.barrier()

        with ExitStack() as pC:
            sbC = lambda n, s, dt: pC.enter_context(nc.sbuf_tensor(pref + n, s, dt))
            xt2 = sbC("xt2", [128, 2 * 1024], F32)
            yt = sbC("yt", [128, 2 * 4 * 1024], F32)
            sqs2 = sbC("sqs2", [128, 2 * 1024], F32)
            lnx2 = sbC("lnx2", [128, 2 * 1024], BF16)
            lnT2 = sbC("lnT2", [128, 2 * 8 * 128], BF16)
            mstC = sbC("mstC", [128, 16], F32)
            latT = (sbC("latT", [64, 2048], F32) if fuse is None
                    else sbC("latTtm", [128, 16 * 64], F32))

            ygkeys = [("yg", gg, gt) for gg in range(4) for gt in range(16)]
            for tt in range(16):
                p_ = tt % 2
                xz = p_ * 1024
                S.dma("SYNC", xt2[:, xz:xz + 1024], x1_in[tt * 128:(tt + 1) * 128, :],
                      writes=[("xt2", p_)])
                for gg in range(4):
                    S.dma("SYNC", yt[:, p_ * 4096 + gg * 1024: p_ * 4096 + (gg + 1) * 1024],
                          yg_hbm[gg * NT + tt * 128: gg * NT + (tt + 1) * 128, :],
                          reads=ygkeys, writes=[("ytl", p_, gg)])
                for gg in range(4):
                    S.op("DVE", lambda e, a=xt2[:, xz:xz + 1024],
                         b=yt[:, p_ * 4096 + gg * 1024: p_ * 4096 + (gg + 1) * 1024]:
                         e.tensor_add(a, a, b),
                         reads=[("ytl", p_, gg), ("xt2", p_)], writes=[("xt2", p_)])
                S.dma("SYNC", x2_out[tt * 128:(tt + 1) * 128, :], xt2[:, xz:xz + 1024],
                      reads=[("xt2", p_)])
                emit_ln(S, xt2[:, xz:xz + 1024], lnx2[:, xz:xz + 1024],
                        mstC[:, p_ * 8:(p_ + 1) * 8], sqs2[:, xz:xz + 1024],
                        c3, c2, ("xt2", p_), ("lnx2", p_),
                        rk_extra=["cst"], tag=p_)
                for half in range(2):
                    pti = half
                    fns = []
                    for q4 in range(4):
                        dch = half * 4 + q4
                        fns.append(lambda e, o=pt_bf[pti][:, q4 * 128:(q4 + 1) * 128],
                                   i=lnx2[:, xz + dch * 128: xz + (dch + 1) * 128]:
                                   e.transpose(o, i, id_bf[:]))
                    tok = S.op("TE", fns[0], reads=[("lnx2", p_), "id"], writes=[PT[pti]])
                    for f in fns[1:]:
                        tok = S.op("TE", f)
                    S.lw[PT[pti]] = tok
                    S.op("DVE", lambda e, pt=pt_bf[pti],
                         d=lnT2[:, p_ * 1024 + half * 512: p_ * 1024 + half * 512 + 512]:
                         e.tensor_copy(d, pt[:, 0:512]),
                         reads=[PT[pti]], writes=[("lnT2", p_, half)])
                bi = 4 + p_
                if fuse is None:
                    fns = [lambda e, o=pb[bi][0:64, 0:128],
                           l=wkv_bf[:, kch * 64:(kch + 1) * 64],
                           r=lnT2[:, p_ * 1024 + kch * 128: p_ * 1024 + (kch + 1) * 128],
                           st=(kch == 0), sp=(kch == 7):
                           e.matmul(o, l, r, start=st, stop=sp) for kch in range(8)]
                    S.mmg(PB[bi], fns, reads=[("lnT2", p_, 0), ("lnT2", p_, 1), "wkv"])
                    S.op("DVE", lambda e, o=latT[:, tt * 128:(tt + 1) * 128],
                         i=pb[bi][0:64, 0:128]: e.tensor_copy(o, i),
                         reads=[PB[bi]], writes=[("latT", tt)])
                else:
                    fns = [lambda e, o=pb[bi][:, 0:64],
                           l=lnT2[:, p_ * 1024 + kch * 128: p_ * 1024 + (kch + 1) * 128],
                           r=wkv_bf[:, kch * 64:(kch + 1) * 64],
                           st=(kch == 0), sp=(kch == 7):
                           e.matmul(o, l, r, start=st, stop=sp) for kch in range(8)]
                    S.mmg(PB[bi], fns, reads=[("lnT2", p_, 0), ("lnT2", p_, 1), "wkv"])
                    S.op("DVE", lambda e, o=latT[:, tt * 64:(tt + 1) * 64],
                         i=pb[bi][:, 0:64]: e.tensor_copy(o, i),
                         reads=[PB[bi]], writes=[("latT", tt)])
            if fuse is None:
                for i0 in range(0, 2048, 1024):
                    S.dma("SYNC", lat_out[:, i0:i0 + 1024], latT[:, i0:i0 + 1024],
                          reads=[("latT", t_) for t_ in range(16)])
            else:
                for tt in range(16):
                    S.dma("SYNC", lat_out[tt * 128:(tt + 1) * 128, :],
                          latT[:, tt * 64:(tt + 1) * 64],
                          reads=[("latT", tt)], writes=[("lato", tt)])
            S.op("SYNC", lambda e: e.nop(), waits=S.now("QSP", "QACT"), q="QSP", n=16)
            S.barrier()

        if fuse is None:
            blk = ctx.enter_context(nc.Block())
            S.replay(blk, sems)

    return nc




def _wq_headmajor(wqT):
    kc = _kc(wqT)
    v = kc.reshape(128, 8, 8, 256)
    return np.ascontiguousarray(v.transpose(0, 2, 1, 3).reshape(128, 16384))


def _gup_perm(gu):
    guT = gu.transpose(0, 2, 1)
    out = np.empty((8, 22, 128, 2048), np.float32)
    for mj in range(22):
        for gi in range(2):
            mjj = mj + 22 * gi
            blk = guT[:, :, mjj * 128:(mjj + 1) * 128]
            blk = blk.reshape(8, 8, 128, 128)
            out[:, mj, :, gi * 1024:(gi + 1) * 1024] = \
                blk.transpose(0, 2, 1, 3).reshape(8, 128, 1024)
    out = out.reshape(8, 11, 2, 128, 2048).transpose(0, 1, 3, 2, 4)
    return np.ascontiguousarray(out.reshape(8, 11, 128, 4096))

def hostCD(inp, core, latT_full):
    if latT_full is None:
        latT_full = np.zeros((64, 4096), np.float32)
    return {
        "x2": np.zeros((NT, DM), np.float32),
        "latT": np.ascontiguousarray(latT_full),
        "wqTl1": _wq_headmajor(_f32(inp["L1_attn_Wq"]).T),
        "wkupT": np.ascontiguousarray(_f32(inp["L1_attn_Wk_up"]).T),
        "wvupT": np.ascontiguousarray(_f32(inp["L1_attn_Wv_up"]).T),
        "woTl1": _kc(_f32(inp["L1_attn_Wo"]).T),
        "wnorm_bc": np.broadcast_to(_f32(inp["L1_ffn_norm_weight"]), (128, 1024)).copy(),
        "rwT": _kc(_f32(inp["L1_ffn_router_weight"]).T),
        "gupT": _gup_perm(_f32(inp["L1_ffn_gate_up_weight"])),
        "dnT": np.ascontiguousarray(_f32(inp["L1_ffn_down_weight"]).transpose(0, 2, 1)),
        "alpha_bc": np.broadcast_to(_f32(inp["L1_ffn_residual_alpha"]).reshape(1, 1),
                                    (128, 1)).copy(),
        "ones1": np.ones((1, 128), np.float32),
        "onescol": np.ones((128, 1), np.float32),
        "identC": np.eye(128, dtype=np.float32),
        "constsC": np.tile(np.array(
            [0.0625, 1e-12, 1e-5, 1.0 / 1024, 1e-6, 1.0, -1.0, -1e9], np.float32),
            (128, 1)),
    }


def build_CD(fuse=None):
    nc = bass.Bass("TRN2", target_bir_lowering=False) if fuse is None else fuse["nc"]
    pref = "" if fuse is None else "c_"
    g = lambda n, s, dt=F32: nc.dram_tensor(n, s, dt, kind="ExternalInput").ap()
    if fuse is None:
        x2_in = g("x2", [NT, DM])
        latT_in = g("latT", [64, 4096])
    else:
        x2_in = fuse["x2_hbm"]
        latT_in = None
    wqT_in = g("wqTl1", [128, 8 * 2048])
    wkupT_in = g("wkupT", [64, 2048])
    wvupT_in = g("wvupT", [64, 2048])
    woT_in = g("woTl1", [128, 16 * 1024])
    wnorm_in = g("wnorm_bc", [128, 1024])
    rwT_in = g("rwT", [128, 64])
    gupT_in = fuse["gupT_in"] if fuse else g("gupT", [8, 11, 128, 4096])
    dnT_in = fuse["dnT_in"] if fuse else g("dnT", [8, 2816, 1024])
    alpha_in = g("alpha_bc", [128, 1])
    ones1_in = g("ones1", [1, 128])
    onescol_in = g("onescol", [128, 1])
    id_in = g("identC", [128, 128])
    cst_in = g("constsC", [128, 8])

    out_f = nc.dram_tensor("out", [NT, DM], F32, kind="ExternalOutput").ap()
    ye_hbm = nc.dram_tensor("ye_hbm", [8 * NT, DM], BF16, kind="Internal").ap()
    x3_hbm = nc.dram_tensor("x3_hbm", [NT, DM], F32, kind="Internal" if fuse is not None else "ExternalOutput").ap()

    with ExitStack() as ctx:
        sb = lambda n, s, dt: ctx.enter_context(nc.sbuf_tensor(pref + n, s, dt))
        psm = lambda n, s, dt=F32: ctx.enter_context(nc.psum_tensor(n, s, dt))

        cst = sb("cst", [128, 8], F32)
        id_bf = sb("idbf", [128, 128], BF16)
        ones1_bf = sb("ones1bf", [1, 128], BF16)
        onescol_bf = sb("onescolbf", [128, 1], BF16)
        ones1_f = sb("ones1f", [1, 128], F32)
        mst = sb("mst", [128, 8], F32)
        alpha_sb = sb("alphasb", [128, 1], F32)
        w2_tok = sb("w2tok", [128, 16 * 8], F32)

        if fuse is None:
            pb = [psm(f"pb{i}", [128, 512]) for i in range(6)]
            pt_bf = [psm(f"ptb{i}", [128, 512], BF16) for i in range(2)]
            S = Sched()
            sems = make_sems(nc, ctx)
        else:
            pb, pt_bf = fuse["pb"], fuse["pt_bf"]
            S, sems = fuse["S"], fuse["sems"]
        PB = [f"pb{i}" for i in range(6)]
        PT = ["pt0", "pt1"]
        c1 = cst[:, 1:2]
        c2 = cst[:, 2:3]
        c3 = cst[:, 3:4]
        c4 = cst[:, 4:5]

        S.dma("SYNC", cst[:], cst_in, writes=["cst"])
        S.dma("SYNC", alpha_sb[:], alpha_in, writes=["alpha"])
        S.op("DVE", lambda e: e.tensor_scalar(alpha_sb[:], alpha_sb[:], 2.0, None, ALU.min),
             reads=["alpha"], writes=["alpha"])
        S.op("DVE", lambda e: e.tensor_scalar(alpha_sb[:], alpha_sb[:], -0.5, None, ALU.max),
             reads=["alpha"], writes=["alpha"])

        with ExitStack() as pC:
            sbC = lambda n, s, dt: pC.enter_context(nc.sbuf_tensor(pref + n, s, dt))
            stage = sbC("stage", [128, 1024], F32)
            stg3 = sbC("stg3", [128, 1024], F32)
            lnx2T = sbC("lnx2T", [128, 8 * 2048], BF16)
            lat_bf = sbC("latbf", [64, 4096], BF16)
            wkup_bf = sbC("wkupbf", [64, 2048], BF16)
            wvup_bf = sbC("wvupbf", [64, 2048], BF16)
            woh_bf = sbC("wohbf", [128, 2 * 2048], BF16)
            wqh_bf = sbC("wqhbf", [128, 2 * 2048], BF16)
            qT_h = sbC("qTh", [128, 2 * 2048], BF16)
            kT_h = sbC("kTh", [128, 2 * 4096], BF16)
            v_h = sbC("vh", [128, 32 * 256], BF16)
            outT_h = sbC("outTh", [128, 2 * 2048], BF16)
            y_acc = sbC("yaccC", [128, 16 * 1024], F32)
            pT = [sbC(f"pTc{i}", [128, 512], BF16) for i in range(2)]
            dnr = sbC("dnrc", [1, 512], F32)
            rb_f = sbC("rbf", [128, 512], F32)
            xt = sbC("xtc", [128, 1024], F32)
            sqs = sbC("sqsc", [128, 1024], F32)
            lnx = sbC("lnxc", [128, 1024], BF16)
            y_sb = sbC("ysbc", [128, 1024], F32)

            def conv_in(src_ap, dst_ap, rows, width, key, eng="SYNC"):
                for off in range(0, width, 1024):
                    wdt = min(1024, width - off)
                    S.dma(eng, stage[0:rows, 0:wdt], src_ap[0:rows, off:off + wdt],
                          writes=["stage"])
                    S.op("DVE", lambda e, d=dst_ap[0:rows, off:off + wdt],
                         s=stage[0:rows, 0:wdt]: e.tensor_copy(d, s),
                         reads=["stage"], writes=[key])

            conv_in(id_in, id_bf[:], 128, 128, "id")
            conv_in(ones1_in, ones1_bf[:], 1, 128, "ones1")
            conv_in(onescol_in, onescol_bf[:], 128, 1, "onescol")
            S.dma("SYNC", ones1_f[:], ones1_in, writes=["ones1f"])
            if fuse is None:
                conv_in(latT_in, lat_bf[:], 64, 4096, "lat")
            else:
                if fuse.get("dbg"):
                    latdbg = nc.dram_tensor("latdbg", [64, 4096], F32,
                                            kind="ExternalOutput").ap()
                lat_all = fuse["lat_all_hbm"]
                lt_f = pC.enter_context(nc.sbuf_tensor("c_ltf", [128, 64], F32))
                lt_b = pC.enter_context(nc.sbuf_tensor("c_ltb", [128, 64], BF16))
                lt_d = pC.enter_context(nc.sbuf_tensor("c_ltd", [64, 128], F32))
                for kt in range(32):
                    S.dma("SYNC", lt_f[:], lat_all[kt * 128:(kt + 1) * 128, :],
                          reads=["agl"], writes=["ltf"])
                    S.op("DVE", lambda e: e.tensor_copy(lt_b[:], lt_f[:]),
                         reads=["ltf"], writes=["ltb"])
                    S.op("TE", lambda e, o=pt_bf[kt % 2][0:64, 0:128], i=lt_b[:]:
                         e.transpose(o, i, id_bf[:]),
                         reads=["ltb", "id"], writes=[PT[kt % 2]])
                    S.op("DVE", lambda e, o=lat_bf[0:64, kt * 128:(kt + 1) * 128],
                         i=pt_bf[kt % 2][0:64, 0:128], kt=kt: e.tensor_copy(o, i),
                         reads=[PT[kt % 2]], writes=["lat"])
                    if fuse.get("dbg"):
                        S.op("DVE", lambda e, o=lt_d[:],
                             i=pt_bf[kt % 2][0:64, 0:128]: e.tensor_copy(o, i),
                             reads=[PT[kt % 2]], writes=["ltdbg"])
                        S.dma("SYNC", latdbg[:, kt * 128:(kt + 1) * 128],
                              lt_d[:], reads=["ltdbg"])
            conv_in(wkupT_in, wkup_bf[:], 64, 2048, "wkup")
            conv_in(wvupT_in, wvup_bf[:], 64, 2048, "wvup")

            for tt in range(16):
                S.dma("SYNC", xt[:], x2_in[tt * 128:(tt + 1) * 128, :], writes=["xt"])
                emit_ln(S, xt[:], lnx[:], mst, sqs[:], c3, c2, "xt", "lnx",
                        rk_extra=["cst"])
                for half in range(2):
                    pti = half
                    fns = []
                    for q4 in range(4):
                        dch = half * 4 + q4
                        fns.append(lambda e, o=pt_bf[pti][:, q4 * 128:(q4 + 1) * 128],
                                   i=lnx[:, dch * 128:(dch + 1) * 128]:
                                   e.transpose(o, i, id_bf[:]))
                    tok = S.op("TE", fns[0], reads=["lnx", "id"], writes=[PT[pti]])
                    for f in fns[1:]:
                        tok = S.op("TE", f)
                    S.lw[PT[pti]] = tok

                    def ev(e, pt=pt_bf[pti], half=half, tt=tt):
                        src = pt[:, 0:512].rearrange("p (a b) -> p a b", a=4)
                        d = lnx2T[:].rearrange("p (a b) -> p a b", a=8, b=2048)[
                            :, half * 4: half * 4 + 4, tt * 128: tt * 128 + 128]
                        return e.tensor_copy(d, src)
                    S.op("DVE", ev, reads=[PT[pti]], writes=[("lnx2T", tt)])
            ln2keys = [("lnx2T", t_) for t_ in range(16)]

            for h in range(8):
                hp = h % 2
                for off in range(0, 2048, 1024):
                    S.dma("SYNC", stage[:, 0:1024],
                          woT_in[:, 2 * h * 1024 + off: 2 * h * 1024 + off + 1024],
                          writes=["stage"])
                    S.op("DVE", lambda e, d=woh_bf[:, hp * 2048 + off: hp * 2048 + off + 1024],
                         s=stage[:, 0:1024]: e.tensor_copy(d, s),
                         reads=["stage"], writes=[("wo", hp)])
                for off in range(0, 2048, 1024):
                    S.dma("ACT", stg3[:, 0:1024],
                          wqT_in[:, h * 2048 + off: h * 2048 + off + 1024],
                          writes=["stg3"])
                    S.op("DVE", lambda e, d=wqh_bf[:, hp * 2048 + off: hp * 2048 + off + 1024],
                         s=stg3[:, 0:1024]: e.tensor_copy(d, s),
                         reads=["stg3"], writes=[("wqh", hp)])
                for mch in range(2):
                    for nc4 in range(4):
                        bi = (mch * 4 + nc4) % 6
                        fns = [lambda e, o=pb[bi][:, 0:512],
                               l=wqh_bf[:, hp * 2048 + kch * 256 + mch * 128:
                                         hp * 2048 + kch * 256 + mch * 128 + 128],
                               r=lnx2T[:, kch * 2048 + nc4 * 512: kch * 2048 + (nc4 + 1) * 512],
                               st=(kch == 0), sp=(kch == 7):
                               e.matmul(o, l, r, start=st, stop=sp) for kch in range(8)]
                        S.mmg(PB[bi], fns, reads=ln2keys + [("wqh", hp)])
                        S.op("ACT" if (mch * 4 + nc4) % 2 == 0 else "DVE",
                             lambda e, o=qT_h[:, mch * 2048 + nc4 * 512:
                                              mch * 2048 + (nc4 + 1) * 512],
                             i=pb[bi][:, 0:512]:
                             (e.activation(o, i, AF.Identity) if hasattr(e, "activation")
                              else e.tensor_copy(o, i)),
                             reads=[PB[bi]], writes=[("qT_h", mch * 4 + nc4)])
                for mch in range(2):
                    for nc8 in range(8):
                        ii = mch * 8 + nc8
                        bi = ii % 6
                        S.mmg(PB[bi], [lambda e, o=pb[bi][:, 0:512],
                                       l=wkup_bf[0:64, h * 256 + mch * 128: h * 256 + mch * 128 + 128],
                                       r=lat_bf[0:64, nc8 * 512:(nc8 + 1) * 512]:
                                       e.matmul(o, l, r, start=True, stop=True)],
                              reads=["wkup", "lat"])
                        S.op("ACT" if ii % 2 == 0 else "DVE",
                             lambda e, o=kT_h[:, mch * 4096 + nc8 * 512:
                                              mch * 4096 + (nc8 + 1) * 512],
                             i=pb[bi][:, 0:512]:
                             (e.activation(o, i, AF.Identity) if hasattr(e, "activation")
                              else e.tensor_copy(o, i)),
                             reads=[PB[bi]], writes=[("kT_h", ii)])
                for ktt in range(32):
                    bi = ktt % 6
                    S.mmg(PB[bi], [lambda e, o=pb[bi][:, 0:256],
                                   l=lat_bf[0:64, ktt * 128:(ktt + 1) * 128],
                                   r=wvup_bf[0:64, h * 256:(h + 1) * 256]:
                                   e.matmul(o, l, r, start=True, stop=True)],
                          reads=["lat", "wvup"])
                    S.op("ACT" if ktt % 2 == 0 else "DVE",
                         lambda e, o=v_h[:, ktt * 256:(ktt + 1) * 256],
                         i=pb[bi][:, 0:256]:
                         (e.activation(o, i, AF.Identity) if hasattr(e, "activation")
                          else e.tensor_copy(o, i)),
                         reads=[PB[bi]], writes=[("v_h", ktt)])
                for qc in range(4):
                    for m in range(32):
                        sbk = m % 2
                        fns = [lambda e, o=pb[sbk][:, 0:512],
                               l=kT_h[:, mch * 4096 + m * 128: mch * 4096 + (m + 1) * 128],
                               r=qT_h[:, mch * 2048 + qc * 512: mch * 2048 + (qc + 1) * 512],
                               st=(mch == 0), sp=(mch == 1):
                               e.matmul(o, l, r, start=st, stop=sp) for mch in range(2)]
                        S.mmg(PB[sbk], fns,
                              reads=[("kT_h", mch_ * 8 + m // 4) for mch_ in range(2)] +
                                    [("qT_h", mch_ * 4 + qc) for mch_ in range(2)])
                        pti = m % 2
                        S.op("ACT", lambda e, o=pT[pti][:], i=pb[sbk][:, 0:512]:
                             e.activation(o, i, AF.Exp, scale=cst[:, 0:1]),
                             reads=[PB[sbk], "cst"], writes=[("pT", pti)])
                        for dm in range(2):
                            S.op("TE", lambda e, o=pb[2 + dm][:, 0:512],
                                 l=v_h[:, m * 256 + dm * 128: m * 256 + (dm + 1) * 128],
                                 r=pT[pti][:], st=(m == 0), sp=(m == 31):
                                 e.matmul(o, l, r, start=st, stop=sp),
                                 reads=[("pT", pti), ("v_h", m)] +
                                       ([PB[2 + dm]] if m == 0 else []),
                                 writes=[PB[2 + dm]] if m == 0 else [])
                        tok_d = S.op("TE", lambda e, o=pb[4][0:1, 0:512],
                                     l=onescol_bf[:], r=pT[pti][:], st=(m == 0), sp=(m == 31):
                                     e.matmul(o, l, r, start=st, stop=sp),
                                     reads=[("pT", pti), "onescol"] +
                                           ([PB[4]] if m == 0 else []),
                                     writes=[PB[4]] if m == 0 else [])
                    S.lw[PB[2]] = S.now("TE")[0]
                    S.lw[PB[3]] = S.now("TE")[0]
                    S.lw[PB[4]] = tok_d
                    S.op("DVE", lambda e: e.reciprocal(dnr[:], pb[4][0:1, 0:512]),
                         reads=[PB[4]], writes=["dnr"])
                    S.mmg(PB[5], [lambda e: e.matmul(pb[5][:, 0:512], ones1_f[:], dnr[:],
                                                     start=True, stop=True)],
                          reads=["dnr", "ones1f"])
                    S.op("ACT", lambda e: e.activation(rb_f[:], pb[5][:, 0:512], AF.Identity),
                         reads=[PB[5]], writes=["rb_f"])
                    for dm in range(2):
                        S.op("DVE", lambda e, o=outT_h[:, dm * 2048 + qc * 512:
                                                       dm * 2048 + (qc + 1) * 512],
                             a=pb[2 + dm][:, 0:512], dm=dm: e.tensor_mul(o, a, rb_f[:]),
                             reads=[PB[2 + dm], "rb_f"], writes=[("outT", dm, qc)])
                okeys = [("outT", d_, q_) for d_ in range(2) for q_ in range(4)]
                for tt in range(16):
                    for dmc in range(2):
                        bi = dmc
                        fns = [lambda e, o=pb[bi][:, 0:512],
                               l=outT_h[:, dm_ * 2048 + tt * 128: dm_ * 2048 + tt * 128 + 128],
                               r=woh_bf[:, hp * 2048 + dm_ * 1024 + dmc * 512:
                                        hp * 2048 + dm_ * 1024 + dmc * 512 + 512],
                               st=(dm_ == 0), sp=(dm_ == 1):
                               e.matmul(o, l, r, start=st, stop=sp) for dm_ in range(2)]
                        S.mmg(PB[bi], fns, reads=okeys + [("wo", hp)])
                        ya = y_acc[:, tt * 1024 + dmc * 512: tt * 1024 + dmc * 512 + 512]
                        if h == 0:
                            S.op("ACT", lambda e, o=ya, i=pb[bi][:, 0:512]:
                                 e.activation(o, i, AF.Identity),
                                 reads=[PB[bi]], writes=[("yac", tt, dmc)])
                        else:
                            S.op("DVE", lambda e, o=ya, i=pb[bi][:, 0:512]:
                                 e.tensor_add(o, i, o),
                                 reads=[PB[bi], ("yac", tt, dmc)],
                                 writes=[("yac", tt, dmc)])
            for tt in range(16):
                S.dma("SYNC", xt[:], x2_in[tt * 128:(tt + 1) * 128, :], writes=["xt"])
                for dmc in range(2):
                    S.op("DVE", lambda e, o=y_sb[:, dmc * 512:(dmc + 1) * 512],
                         a=y_acc[:, tt * 1024 + dmc * 512: tt * 1024 + dmc * 512 + 512],
                         b=xt[:, dmc * 512:(dmc + 1) * 512]: e.tensor_add(o, a, b),
                         reads=[("yac", tt, dmc), "xt"], writes=[("y", dmc)])
                S.dma("SYNC", x3_hbm[tt * 128:(tt + 1) * 128, :], y_sb[:],
                      reads=[("y", 0), ("y", 1)], writes=[("x3", tt)])
            S.barrier()

        x3keys = [("x3", t_) for t_ in range(16)]
        with ExitStack() as pD:
            sbD = lambda n, s, dt: pD.enter_context(nc.sbuf_tensor(pref + n, s, dt))
            stage = sbD("stageD", [128, 64 if fuse is not None else 2 * 4096], F32)
            stg2 = sbD("stg2D", [128, 1024 if fuse is not None else 2 * 1024], F32)
            normT = sbD("normT", [128, 8 * 2048], BF16)
            hidT = sbD("hidT", [128, 22 * 1024], BF16)
            dn_bf = sbD("dnbf", [128, 22 * 1024], BF16)
            gup_bf = sbD("gupbf", [128, 2 * 4096], BF16)
            y_t = sbD("yt", [128, 2 * 1024], BF16)
            wn_sb = sbD("wnsb", [128, 1024], BF16)
            rw_bf = sbD("rwbf", [128, 64], BF16)
            xt3 = sbD("xt3", [128, 2 * 1024], F32)
            sqs3 = sbD("sqs3", [128, 2 * 1024], F32)
            lnx3 = sbD("lnx3", [128, 2 * 1024], BF16)
            nrm = sbD("nrm", [128, 2 * 1024], BF16)
            lg = sbD("lgD", [128, 2 * 8], F32)
            msD = sbD("msD", [128, 2 * 4], F32)
            prb = sbD("prb", [128, 2 * 8], F32)
            gtmp = sbD("gtmp", [128, 2 * 1024], BF16)
            msR = sbD("msR", [128, 16], F32)
            utmp = sbD("utmp", [128, 1024], BF16)
            lnf = sbD("lnf", [128, 1024], F32)
            moef = sbD("moef", [128, 1024], F32)

            S.dma("SYNC", stg2[:, 0:1024], wnorm_in, writes=[("stg2D", 0)])
            S.op("DVE", lambda e: e.tensor_copy(wn_sb[:], stg2[:, 0:1024]),
                 reads=[("stg2D", 0)], writes=["wn"])
            S.dma("SYNC", stage[:, 0:64], rwT_in, writes=["stageD"])
            S.op("DVE", lambda e: e.tensor_copy(rw_bf[:], stage[:, 0:64]),
                 reads=["stageD"], writes=["rw"])

            for tt in range(16):
                p_ = tt % 2
                rz = p_ * 1024
                xsl = xt3[:, rz:rz + 1024]
                lsl = lnx3[:, rz:rz + 1024]
                ssl = sqs3[:, rz:rz + 1024]
                gsl = gtmp[:, rz:rz + 1024]
                nsl = nrm[:, rz:rz + 1024]
                lgs = lg[:, p_ * 8:(p_ + 1) * 8]
                prs = prb[:, p_ * 8:(p_ + 1) * 8]
                m0, m1, m2, m3 = (msD[:, p_ * 4 + j: p_ * 4 + j + 1] for j in range(4))
                kx, kl, ks = ("xt3", p_), ("lnx3", p_), ("sqk", "r", p_)
                km, kg, kn = ("msD", p_), ("gt", p_), ("nrm", p_)
                klg, kpr = ("lg", p_), ("prb", p_)
                S.dma("SYNC", xsl, x3_hbm[tt * 128:(tt + 1) * 128, :],
                      reads=x3keys, writes=[kx])
                emit_ln(S, xsl, lsl, msR[:, p_ * 8:(p_ + 1) * 8], ssl, c3, c2, kx, kl,
                        rk_extra=["cst"], tag=(9, p_))
                S.op("ACT", lambda e, o=ssl, i=lsl: e.activation(o, i, AF.Square),
                     reads=[kl], writes=[ks])
                S.op("DVE", lambda e, o=m0, i=ssl: e.reduce_sum(o, i, axis=AX.X),
                     reads=[ks], writes=[km])
                S.op("DVE", lambda e, o=m0: e.tensor_scalar(o, o, c3, None, ALU.mult),
                     reads=[km, "cst"], writes=[km])
                S.op("ACT", lambda e, o=m1, i=m0: e.activation(o, i, AF.Sqrt, bias=c4),
                     reads=[km, "cst"], writes=[km])
                S.op("DVE", lambda e, o=m1: e.reciprocal(o, o),
                     reads=[km], writes=[km])
                S.op("DVE", lambda e, o=gsl, i=lsl, s=m1:
                     e.tensor_scalar(o, i, s, None, ALU.mult),
                     reads=[kl, km], writes=[kg])
                S.op("DVE", lambda e, o=nsl, a=gsl: e.tensor_mul(o, a, wn_sb[:]),
                     reads=[kg, "wn"], writes=[kn])
                for half in range(2):
                    pti = half
                    fns = []
                    for q4 in range(4):
                        dch = half * 4 + q4
                        fns.append(lambda e, o=pt_bf[pti][:, q4 * 128:(q4 + 1) * 128],
                                   i=nrm[:, rz + dch * 128: rz + (dch + 1) * 128]:
                                   e.transpose(o, i, id_bf[:]))
                    tok = S.op("TE", fns[0], reads=[kn, "id"], writes=[PT[pti]])
                    for f in fns[1:]:
                        tok = S.op("TE", f)
                    S.lw[PT[pti]] = tok

                    def ev2(e, pt=pt_bf[pti], half=half, tt=tt):
                        src = pt[:, 0:512].rearrange("p (a b) -> p a b", a=4)
                        d = normT[:].rearrange("p (a b) -> p a b", a=8, b=2048)[
                            :, half * 4: half * 4 + 4, tt * 128: tt * 128 + 128]
                        return e.tensor_copy(d, src)
                    S.op("DVE", ev2, reads=[PT[pti]], writes=[("normT", tt)])
                bi = 4 + p_
                fns = [lambda e, o=pb[bi][:, 0:8],
                       l=normT[:].rearrange("p (a b) -> p a b", a=8, b=2048)[
                           :, kch, tt * 128: tt * 128 + 128],
                       r=rw_bf[:, kch * 8:(kch + 1) * 8],
                       st=(kch == 0), sp=(kch == 7):
                       e.matmul(o, l, r, start=st, stop=sp) for kch in range(8)]
                S.mmg(PB[bi], fns, reads=[("normT", tt), "rw"])
                S.op("ACT", lambda e, o=lgs, i=pb[bi][:, 0:8]:
                     e.activation(o, i, AF.Identity),
                     reads=[PB[bi]], writes=[klg])
                S.op("DVE", lambda e, o=m2, i=lgs: e.reduce_max(o, i, axis=AX.X),
                     reads=[klg], writes=[km])
                S.op("DVE", lambda e, o=m3, i=m2:
                     e.tensor_scalar(o, i, -1.0, None, ALU.mult),
                     reads=[km], writes=[km])
                S.op("ACT", lambda e, o=prs, i=lgs, b=m3:
                     e.activation(o, i, AF.Exp, bias=b),
                     reads=[klg, km], writes=[kpr])
                S.op("DVE", lambda e, o=m2, i=prs: e.reduce_sum(o, i, axis=AX.X),
                     reads=[kpr], writes=[km])
                S.op("DVE", lambda e, o=m2: e.reciprocal(o, o),
                     reads=[km], writes=[km])
                S.op("DVE", lambda e, o=prs, s=m2:
                     e.tensor_scalar(o, o, s, None, ALU.mult),
                     reads=[kpr, km], writes=[kpr])
                S.op("DVE", lambda e, o=m2, i=prs: e.reduce_max(o, i, axis=AX.X),
                     reads=[kpr], writes=[km])
                S.op("DVE", lambda e, o=lgs, i=prs, s=m2:
                     e.tensor_scalar(o, i, s, None, ALU.is_ge),
                     reads=[kpr, km], writes=[klg])
                S.op("DVE", lambda e, o=ssl[:, 0:8], a=lgs, b=prs:
                     e.scalar_tensor_tensor(o, a, cst[:, 7:8], b, ALU.mult, ALU.add),
                     reads=[klg, kpr, "cst"], writes=[ks])
                S.op("DVE", lambda e, o=m3, i=ssl[:, 0:8]: e.reduce_max(o, i, axis=AX.X),
                     reads=[ks], writes=[km])
                S.op("DVE", lambda e, o=ssl[:, 8:16], i=ssl[:, 0:8], s=m3:
                     e.tensor_scalar(o, i, s, None, ALU.is_ge),
                     reads=[ks, km], writes=[ks])
                S.op("DVE", lambda e, o=lgs, b=ssl[:, 8:16]: e.tensor_add(o, o, b),
                     reads=[klg, ks], writes=[klg])
                S.op("DVE", lambda e, o=prs, b=lgs: e.tensor_mul(o, o, b),
                     reads=[kpr, klg], writes=[kpr])
                S.op("DVE", lambda e, o=m2, i=prs: e.reduce_sum(o, i, axis=AX.X),
                     reads=[kpr], writes=[km])
                S.op("DVE", lambda e, o=m2: e.reciprocal(o, o),
                     reads=[km], writes=[km])
                S.op("DVE", lambda e, o=w2_tok[:, tt * 8:(tt + 1) * 8], i=prs, s=m2:
                     e.tensor_scalar(o, i, s, None, ALU.mult),
                     reads=[kpr, km], writes=[("w2", tt)])

            nkeys = [("normT", t_) for t_ in range(16)]
            for e_ in range(8):
                for mj in range(22):
                    if fuse is not None:
                        S.dma("ACT", dn_bf[:, mj * 1024:(mj + 1) * 1024],
                              fuse["dnb_h"][e_ * 2816 + mj * 128:
                                            e_ * 2816 + (mj + 1) * 128, :],
                              reads=["dnbH"], writes=["dn"])
                        continue
                    par = mj % 2
                    o0 = par * 1024
                    S.dma("ACT", stg2[:, o0:o0 + 1024],
                          dnT_in[e_, mj * 128:(mj + 1) * 128, :], writes=[("stg2D", par)])
                    if par == 0:
                        S.op("DVE", lambda e, d=dn_bf[:, mj * 1024:(mj + 1) * 1024],
                             s=stg2[:, o0:o0 + 1024]: e.tensor_copy(d, s),
                             reads=[("stg2D", par)], writes=["dn"])
                    else:
                        S.op("ACT", lambda e, d=dn_bf[:, mj * 1024:(mj + 1) * 1024],
                             s=stg2[:, o0:o0 + 1024]: e.activation(d, s, AF.Identity),
                             reads=[("stg2D", par)], writes=["dn"])
                for tc2 in range(2):
                    for mjp in range(11):
                        par = mjp % 2
                        o0 = par * 4096
                        if fuse is not None:
                            r0 = (e_ * 11 + mjp) * 128
                            S.dma("SYNC", gup_bf[:, par * 4096:(par + 1) * 4096],
                                  fuse["gupb_h"][r0:r0 + 128, :],
                                  reads=["gupbH"], writes=[("gup", par)])
                        else:
                            S.dma("SYNC", stage[:, o0:o0 + 4096], gupT_in[e_, mjp, :, :],
                                  writes=[("stageD", par)])
                            S.op("DVE", lambda e,
                                 d=gup_bf[:, par * 4096: par * 4096 + 2048],
                                 s=stage[:, o0:o0 + 2048]: e.tensor_copy(d, s),
                                 reads=[("stageD", par)], writes=[("gup", par)])
                            S.op("ACT", lambda e,
                                 d=gup_bf[:, par * 4096 + 2048: par * 4096 + 4096],
                                 s=stage[:, o0 + 2048:o0 + 4096]: e.activation(d, s, AF.Identity),
                                 reads=[("stageD", par)], writes=[("gup", par)])
                        for sub in range(2):
                            mj = 2 * mjp + sub
                            gof = par * 4096 + sub * 2048
                            for nc2 in range(2):
                                tglob = tc2 * 1024 + nc2 * 512
                                big, biu = 0, 1
                                fns = [lambda e, o=pb[big][:, 0:512],
                                       l=gup_bf[:, gof + kch * 128: gof + (kch + 1) * 128],
                                       r=normT[:, kch * 2048 + tglob: kch * 2048 + tglob + 512],
                                       st=(kch == 0), sp=(kch == 7):
                                       e.matmul(o, l, r, start=st, stop=sp) for kch in range(8)]
                                S.mmg(PB[big], fns, reads=nkeys + [("gup", par)])
                                fns = [lambda e, o=pb[biu][:, 0:512],
                                       l=gup_bf[:, gof + 1024 + kch * 128:
                                                gof + 1024 + (kch + 1) * 128],
                                       r=normT[:, kch * 2048 + tglob: kch * 2048 + tglob + 512],
                                       st=(kch == 0), sp=(kch == 7):
                                       e.matmul(o, l, r, start=st, stop=sp) for kch in range(8)]
                                S.mmg(PB[biu], fns, reads=nkeys + [("gup", par)])
                                S.op("ACT", lambda e, o=gtmp[:, nc2 * 512:(nc2 + 1) * 512],
                                     i=pb[big][:, 0:512]: e.activation(o, i, AF.Silu),
                                     reads=[PB[big]], writes=["gt"])
                                S.op("ACT", lambda e, o=utmp[:, nc2 * 512:(nc2 + 1) * 512],
                                     i=pb[biu][:, 0:512]: e.activation(o, i, AF.Identity),
                                     reads=[PB[biu]], writes=["ut"])
                            S.op("DVE", lambda e, o=hidT[:, mj * 1024:(mj + 1) * 1024],
                                 a=gtmp[:, 0:1024], b=utmp[:, 0:1024]: e.tensor_mul(o, a, b),
                                 reads=["gt", "ut"], writes=[("hid", mj)])
                    for tt4 in range(8):
                        gt_ = tc2 * 8 + tt4
                        yp = tt4 % 2
                        for dmc in range(2):
                            bi = 4 + dmc
                            fns = [lambda e, o=pb[bi][:, 0:512],
                                   l=hidT[:, mj * 1024 + tt4 * 128: mj * 1024 + tt4 * 128 + 128],
                                   r=dn_bf[:, mj * 1024 + dmc * 512: mj * 1024 + dmc * 512 + 512],
                                   st=(mj == 0), sp=(mj == 21):
                                   e.matmul(o, l, r, start=st, stop=sp) for mj in range(22)]
                            S.mmg(PB[bi], fns,
                                  reads=[("hid", mj_) for mj_ in range(22)] + ["dn"])
                            wap = w2_tok[:, gt_ * 8 + e_: gt_ * 8 + e_ + 1]
                            S.op("DVE", lambda e, o=y_t[:, yp * 1024 + dmc * 512:
                                                        yp * 1024 + dmc * 512 + 512],
                                 i=pb[bi][:, 0:512], w=wap:
                                 e.tensor_scalar(o, i, w, None, ALU.mult),
                                 reads=[PB[bi], ("w2", gt_)], writes=[("ya", yp, dmc)])
                        S.dma("ACT", ye_hbm[e_ * NT + gt_ * 128: e_ * NT + (gt_ + 1) * 128, :],
                              y_t[:, yp * 1024:(yp + 1) * 1024],
                              reads=[("ya", yp, 0), ("ya", yp, 1)],
                              writes=[("yeh", e_, gt_)])

            for tt in range(16):
                S.dma("SYNC", xt3[:, 0:1024], x3_hbm[tt * 128:(tt + 1) * 128, :],
                      reads=x3keys, writes=["xt3"])
                mk = ("mstk", 0)
                S.op("DVE", lambda e: e.reduce_sum(mst[:, 0:1], xt3[:, 0:1024], axis=AX.X),
                     reads=["xt3"], writes=[mk])
                S.op("ACT", lambda e: e.activation(sqs3[:, 0:1024], xt3[:, 0:1024], AF.Square),
                     reads=["xt3"], writes=[("sqk", 0)])
                S.op("DVE", lambda e: e.reduce_sum(mst[:, 1:2], sqs3[:, 0:1024], axis=AX.X),
                     reads=[("sqk", 0)], writes=[mk])
                S.op("DVE", lambda e: e.tensor_scalar(mst[:, 2:3], mst[:, 0:1], c3, None, ALU.mult),
                     reads=[mk, "cst"], writes=[mk])
                S.op("DVE", lambda e: e.tensor_scalar(mst[:, 3:4], mst[:, 1:2], c3, None, ALU.mult),
                     reads=[mk], writes=[mk])
                S.op("DVE", lambda e: e.tensor_mul(mst[:, 4:5], mst[:, 2:3], mst[:, 2:3]),
                     reads=[mk], writes=[mk])
                S.op("DVE", lambda e: e.tensor_sub(mst[:, 5:6], mst[:, 3:4], mst[:, 4:5]),
                     reads=[mk], writes=[mk])
                S.op("ACT", lambda e: e.activation(mst[:, 6:7], mst[:, 5:6], AF.Sqrt, bias=c2),
                     reads=[mk, "cst"], writes=[mk])
                S.op("DVE", lambda e: e.reciprocal(mst[:, 6:7], mst[:, 6:7]),
                     reads=[mk], writes=[mk])
                S.op("DVE", lambda e: e.tensor_mul(mst[:, 7:8], mst[:, 2:3], mst[:, 6:7]),
                     reads=[mk], writes=[mk])
                S.op("DVE", lambda e: e.tensor_scalar(mst[:, 7:8], mst[:, 7:8], -1.0, None, ALU.mult),
                     reads=[mk], writes=[mk])
                S.op("ACT", lambda e: e.activation(lnf[:], xt3[:, 0:1024], AF.Identity,
                                                   bias=mst[:, 7:8], scale=mst[:, 6:7]),
                     reads=[mk, "xt3"], writes=["lnf"])
                for e_ in range(8):
                    S.dma("SYNC", y_t[:, (e_ % 2) * 1024:(e_ % 2 + 1) * 1024],
                          ye_hbm[e_ * NT + tt * 128: e_ * NT + (tt + 1) * 128, :],
                          reads=[("yeh", e_, tt)], writes=[("ytl", e_ % 2)])
                    if e_ == 0:
                        S.op("DVE", lambda e, i=y_t[:, 0:1024]: e.tensor_copy(moef[:], i),
                             reads=[("ytl", 0)], writes=["moef"])
                    else:
                        S.op("DVE", lambda e, i=y_t[:, (e_ % 2) * 1024:(e_ % 2 + 1) * 1024]:
                             e.tensor_add(moef[:], moef[:], i),
                             reads=[("ytl", e_ % 2), "moef"], writes=["moef"])
                S.op("DVE", lambda e: e.tensor_scalar(moef[:], moef[:],
                                                      alpha_sb[:, 0:1], None, ALU.mult),
                     reads=["moef", "alpha"], writes=["moef"])
                S.op("DVE", lambda e: e.tensor_add(lnf[:], lnf[:], xt3[:, 0:1024]),
                     reads=["lnf", "xt3"], writes=["lnf"])
                S.op("DVE", lambda e: e.tensor_add(lnf[:], lnf[:], moef[:]),
                     reads=["lnf", "moef"], writes=["lnf"])
                S.dma("SYNC", out_f[tt * 128:(tt + 1) * 128, :], lnf[:], reads=["lnf"])

            S.op("SYNC", lambda e: e.nop(), waits=S.now("QSP", "QACT"), q="QSP", n=16)
            S.barrier()

        if fuse is None:
            blk = ctx.enter_context(nc.Block())
            S.replay(blk, sems)

    return nc


def build_full(dbg=False):
    nc = bass.Bass("TRN2", target_bir_lowering=False)
    kind = "ExternalOutput" if dbg else "Internal"
    x1_hbm = nc.dram_tensor("x1f", [NT, DM], F32, kind=kind).ap()
    x2_hbm = nc.dram_tensor("x2f", [NT, DM], F32, kind=kind).ap()
    pTown_h = nc.dram_tensor("pTownh", [8, 2048], F32)
    wprobs_h = nc.dram_tensor("wprobsh", [128, 128], F32)
    pts_h = nc.dram_tensor("ptsh", [64, 2048], F32)
    lato_h = nc.dram_tensor("latoh", [2048, 64], F32)
    latall_h = nc.dram_tensor("latallh", [4096, 64], F32)
    wup_in = nc.dram_tensor("wup", [4, 1024, 2730], F32, kind="ExternalInput").ap()
    wdn_in = nc.dram_tensor("wdn", [8, 2730, 1024], F32, kind="ExternalInput").ap()
    gupT_in = nc.dram_tensor("gupT", [8, 11, 128, 4096], F32, kind="ExternalInput").ap()
    dnT_in = nc.dram_tensor("dnT", [8, 2816, 1024], F32, kind="ExternalInput").ap()
    wupb_h = nc.dram_tensor("wupbH", [4096, 2730], BF16, kind="Internal").ap()
    wdb_h = nc.dram_tensor("wdbH", [8 * 2730, 1024], BF16, kind="Internal").ap()
    gupb_h = nc.dram_tensor("gupbH", [11264, 4096], BF16, kind="Internal").ap()
    dnb_h = nc.dram_tensor("dnbH", [8 * 2816, 1024], BF16, kind="Internal").ap()

    with ExitStack() as ctx:
        sb = lambda n, s, dt: ctx.enter_context(nc.sbuf_tensor(n, s, dt))
        psm = lambda n, s, dt=F32: ctx.enter_context(nc.psum_tensor(n, s, dt))
        pb = [psm(f"pb{i}", [128, 512]) for i in range(6)]
        pt_bf = [psm(f"ptb{i}", [128, 512], BF16) for i in range(2)]
        S = Sched()
        sems = make_sems(nc, ctx)
        fuse = {
            "nc": nc, "S": S, "sems": sems, "pb": pb, "pt_bf": pt_bf,
            "x1_hbm": x1_hbm, "x2_hbm": x2_hbm,
            "pts_hbm": pts_h.ap(),
            "pTown_h": pTown_h.ap(),
            "wprobs_h": wprobs_h.ap(),
            "lat_own_hbm": lato_h.ap(),
            "lat_all_hbm": latall_h.ap(),
            "wup_in": wup_in, "wdn_in": wdn_in,
            "gupT_in": gupT_in, "dnT_in": dnT_in,
            "wupb_h": wupb_h, "wdb_h": wdb_h,
            "gupb_h": gupb_h, "dnb_h": dnb_h,
        }

        fuse["dbg"] = dbg
        build_A(fuse)
        S.op("POOL", lambda e: e.collective_compute(
            "AllGather", ALU.bypass, replica_groups=[list(range(8))],
            ins=[pTown_h.ap().opt()], outs=[pts_h.ap().opt()]),
            reads=["pTownH"], writes=["agp"], q="POOL", n=1)

        build_B(fuse)
        S.op("POOL", lambda e: e.collective_compute(
            "AllGather", ALU.bypass,
            replica_groups=[[0, 1], [2, 3], [4, 5], [6, 7]],
            ins=[lato_h.ap().opt()], outs=[latall_h.ap().opt()]),
            reads=[("lato", t_) for t_ in range(16)], writes=["agl"],
            q="POOL", n=1)

        build_CD(fuse)

        blk = ctx.enter_context(nc.Block())
        S.replay(blk, sems)
    return nc


def hostFull(inp, core):
    d = hostA(inp, core)
    b = hostB(inp, core, None)
    for k in ("x1", "probs_own", "probsT_r"):
        b.pop(k)
    d.update(b)
    cd = hostCD(inp, core, None)
    for k in ("x2", "latT"):
        cd.pop(k)
    d.update(cd)
    return d


from concourse.bass_utils import run_bass_kernel_spmd

_CACHE = {}


def _get(name, builder):
    if name not in _CACHE:
        _CACHE[name] = builder()
    return _CACHE[name]


def kernel(**inputs):
    nc = _get("FULL", build_full)
    maps = [hostFull(inputs, c) for c in range(8)]
    res = run_bass_kernel_spmd(nc, maps, core_ids=list(range(8))).results
    out = np.concatenate([np.asarray(r["out"]) for r in res], 0)
    return out.reshape(4, 4096, 1024)
